# version 09
# baseline (speedup 1.0000x reference)
import numpy as np
import ml_dtypes
from contextlib import ExitStack

import concourse.bass as bass
import concourse.mybir as mybir
from concourse.bass import IndirectOffsetOnAxis
from concourse.bass_utils import run_bass_kernel_spmd

F32 = mybir.dt.float32
BF16 = mybir.dt.bfloat16
I32 = mybir.dt.int32
AF = mybir.ActivationFunctionType
ALU = mybir.AluOpType
AX = mybir.AxisListType
BF = ml_dtypes.bfloat16

D = 2048
SEQ = 2048
B = 4
NCORE = 8
TPC = 1024
T = 8192
NH = 8
KVC = 128
NGC = 2
E = 32
EPC = 4
TOPK = 4
CCAP = 320
CP = 384
NGRP = 8
EPG = 4
TKG = 3
SCALE_W = 2.5
SCALE_ATT = float(KVC) ** -0.5
BIGI = 1.0e4
BIGM = 1.0e6
LN_EPS = 1e-5

DBG = ()


class Sem:
    def __init__(self, es, nc, name):
        self.h = es.enter_context(nc.semaphore(name))
        self.n = 0

    def inc(self, instr, k=1):
        self.n += k
        instr.then_inc(self.h, k)
        return instr


def build():
    nc = bass.Bass("TRN2", target_bir_lowering=False, debug=False)

    def din(name, shape, dt=F32):
        return nc.dram_tensor(name, shape, dt, kind="ExternalInput").ap()

    xT = din("xT", [D, SEQ])
    x_own = din("x_own", [TPC, D])
    wqh = din("wqh", [NH, D, KVC])
    wk = din("wk", [D, NGC * KVC])
    wv = din("wv", [D, NGC * KVC])
    wdense = din("wdense", [NH * KVC, D])
    routerT = din("routerT", [D, E])
    wg = din("wg", [EPC, D, 1024])
    wu = din("wu", [EPC, D, 1024])
    wd = din("wd", [EPC, 1024, D])
    sgT = din("sgT", [16, D, 128])
    suT = din("suT", [16, D, 128])
    sdT = din("sdT", [D, D])
    ident_bf_i = din("ident_bf_i", [128, 128], BF16)
    ident_f_i = din("ident_f_i", [128, 128])
    lstrict_i = din("lstrict_i", [128, 128], BF16)
    masks_i = din("masks_i", [4, 128, 512], BF16)
    iota384_i = din("iota384_i", [1, 384])
    grpidx_i = din("grpidx_i", [1, E])
    selrow_i = din("selrow_i", [EPC, E])
    ebias_i = din("ebias_i", [1, E])
    hilo_i = din("hilo_i", [128, 128], BF16)
    masktile_i = din("masktile_i", [64, 1])
    sbase_i = din("sbase_i", [1, E])
    selq_i = din("selq_i", [E, EPC])

    out = nc.dram_tensor("out", [TPC, D], F32, kind="ExternalOutput").ap()

    dbg_t = {}
    dbg_specs = {
        "attn": ([TPC, D], BF16), "h1": ([TPC, D], BF16),
        "scores": ([128, 8 * 32], F32), "aall": ([128, 64 * 32], F32),
        "posm": ([128, 8 * 32], F32), "sidx": ([128, 12], I32),
        "weff": ([128, 8 * 4], F32), "stok": ([128, 8 * 4], I32),
        "yag": ([NCORE * EPC * CCAP, D], BF16), "moe": ([TPC, D], F32),
        "hsh": ([128, 16 * 1024], BF16), "ctxt": ([128, NH * SEQ], BF16),
        "qt": ([128, NH * SEQ], BF16), "kt": ([128, NGC * SEQ], BF16),
        "qt0": ([128, SEQ], BF16), "qt1": ([128, SEQ], BF16),
        "vt": ([128, 16 * 256], BF16), "negb": ([128, NH], F32),
        "es0": ([128, 16 * 512], BF16),
    }
    for name in DBG:
        shape, dt = dbg_specs[name]
        dbg_t[name] = nc.dram_tensor("dbg_" + name, shape, dt,
                                     kind="ExternalOutput").ap()

    rs_in = nc.dram_tensor("rs_in", [SEQ, D], BF16)
    rs_out = nc.dram_tensor("rs_out", [TPC, D], BF16)
    hag_in = nc.dram_tensor("hag_in", [TPC, D], BF16)
    hag_out = nc.dram_tensor("hag_out", [T, D], BF16)
    aag_in = nc.dram_tensor("aag_in", [TPC, E], BF16)
    aag_out = nc.dram_tensor("aag_out", [T, E], BF16)
    yag_in = nc.dram_tensor("yag_in", [EPC * CCAP, D], BF16)
    bscr = nc.dram_tensor("bscr", [1, 64 * E], F32)
    lscr = nc.dram_tensor("lscr", [1, 8 * E], F32)
    yag_out = nc.dram_tensor("yag_out", [NCORE * EPC * CCAP, D], BF16)

    es = ExitStack()
    with es:
        def sb(name, shape, dt=F32, stk=es):
            return stk.enter_context(nc.sbuf_tensor(name, shape, dt))

        ident_bf = sb("ident_bf", [128, 128], BF16)
        ident_f = sb("ident_f", [128, 128])
        lstrict = sb("lstrict", [128, 128], BF16)
        masks = sb("masks", [128, 4, 512], BF16)
        iota_bc = sb("iota_bc", [128, 384])
        iplusb = sb("iplusb", [128, 384])
        iotaP1 = sb("iotaP1", [128, 384])
        grpidx_bc = sb("grpidx_bc", [128, E])
        selbc = sb("selbc", [128, EPC, E])
        ebias_bc = sb("ebias_bc", [128, E])
        hilo = sb("hilo", [128, 128], BF16)
        masktile = sb("masktile", [64, 1])
        sbase_bc = sb("sbase_bc", [128, E])
        selq = sb("selq", [E, EPC])
        ones_cb = sb("ones_cb", [128, 1], BF16)
        ones_cf = sb("ones_cf", [128, 1])
        ones_rf = sb("ones_rf", [1, 128])
        routerT_sb = sb("routerT_sb", [128, 16, E])
        scores_sb = sb("scores_sb", [128, 8, E])
        A_loc = sb("A_loc", [128, 8, E], BF16)
        idxk_sb = sb("idxk_sb", [128, 8, TOPK])
        wtop_sb = sb("wtop_sb", [128, 8, TOPK])
        weff_sb = sb("weff_sb", [128, 8, TOPK])
        sidx_tok = sb("sidx_tok", [128, 8, TOPK], I32)
        pos_mine = sb("pos_mine", [128, 8, E])
        sidx_exp = sb("sidx_exp", [128, 12], I32)
        arena1 = sb("arena1", [128, 32768], BF16)
        arena2 = sb("arena2", [128, 16384], BF16)
        a1 = arena1.ap()
        xT_sb = a1.rearrange("p (a b) -> p a b", b=2048)
        expS = [a1[:, i * 8192:(i + 1) * 8192].rearrange(
            "p (a b) -> p a b", b=512) for i in range(2)]
        ctxT_sb = a1[:, 16384:32768].rearrange("p (a b) -> p a b", b=2048)
        wdense_sb = a1[:, 0:16384].rearrange("p (a b) -> p a b", b=2048)
        hT_bf = a1[:, 0:16384].rearrange("p (a b) -> p a b", b=1024)
        XpT = a1[:, 16384:22528].rearrange("p (a b) -> p a b", b=CP)
        sdT_sb = a1.rearrange("p (a b) -> p a b", b=2048)
        a2 = arena2.ap()
        qT_sb = a2.rearrange("p (a b) -> p a b", b=2048)
        HshT = a2.rearrange("p (a b) -> p a b", b=1024)

        dl = Sem(es, nc, "dl")
        dsc = Sem(es, nc, "dsc")
        dg = Sem(es, nc, "dg")
        dgi = Sem(es, nc, "dgi")
        stm = Sem(es, nc, "stm")
        tS = Sem(es, nc, "tS")
        vS = Sem(es, nc, "vS")
        sS = Sem(es, nc, "sS")
        dB = Sem(es, nc, "dB")
        dsl = [Sem(es, nc, f"dsl{i}") for i in range(7)]
        ssl = [Sem(es, nc, f"ssl{i}") for i in range(2)]
        tsl = [Sem(es, nc, f"tsl{i}") for i in range(2)]
        gsl = [Sem(es, nc, f"gsl{i}") for i in range(2)]
        gS = Sem(es, nc, "gS")
        cSx = Sem(es, nc, "cSx")
        ccR = Sem(es, nc, "ccR")
        ccH = Sem(es, nc, "ccH")
        ccA = Sem(es, nc, "ccA")
        ccY = Sem(es, nc, "ccY")

        block = es.enter_context(nc.Block())

        @block.sync
        def _(sync):
            dl.inc(sync.dma_start(out=ident_bf[:, :], in_=ident_bf_i), 16)
            dl.inc(sync.dma_start(out=ident_f[:, :], in_=ident_f_i), 16)
            dl.inc(sync.dma_start(out=lstrict[:, :], in_=lstrict_i), 16)
            dl.inc(sync.dma_start(out=masks[:, :, :],
                                  in_=masks_i.rearrange("m k q -> k m q")), 16)
            dl.inc(sync.dma_start(out=iota_bc[:, :],
                                  in_=iota384_i.partition_broadcast(128)), 16)
            dl.inc(sync.dma_start(out=grpidx_bc[:, :],
                                  in_=grpidx_i.partition_broadcast(128)), 16)
            for i in range(EPC):
                dl.inc(sync.dma_start(
                    out=selbc[:, i, :],
                    in_=selrow_i[i:i + 1, :].partition_broadcast(128)), 16)
            dl.inc(sync.dma_start(out=ebias_bc[:, :],
                                  in_=ebias_i.partition_broadcast(128)), 16)
            dl.inc(sync.dma_start(out=hilo[:, :], in_=hilo_i), 16)
            dl.inc(sync.dma_start(out=masktile[:, :], in_=masktile_i), 16)
            dl.inc(sync.dma_start(out=sbase_bc[:, :],
                                  in_=sbase_i.partition_broadcast(128)), 16)
            dl.inc(sync.dma_start(out=selq[:, :], in_=selq_i), 16)
        p0_loads = dl.n

        @block.scalar
        def _(scalar):
            dsc.inc(scalar.dma_start(
                out=routerT_sb[:, :, :],
                in_=routerT.rearrange("(kc p) e -> p kc e", p=128)), 16)
        routerT_ld = dsc.n

        @block.vector
        def _(vector):
            vector.wait_ge(vS.h, vS.n)
            vector.memset(ones_cb[:, :], 1.0)
            vector.memset(ones_cf[:, :], 1.0)
            vector.memset(ones_rf[:, :], 1.0)
            vector.wait_ge(dl.h, p0_loads)
            vector.tensor_scalar(iplusb[:, :], iota_bc[:, :], BIGI,
                                 None, ALU.add)
            vS.inc(vector.tensor_scalar(iotaP1[:, :], iota_bc[:, :], 1.0,
                                        None, ALU.add))
        consts_done = vS.n

        esAq = ExitStack()
        kT_sb = sb("kT_sb", [128, NGC, SEQ], BF16, esAq)
        v_sb = sb("v_sb", [128, 16, NGC * KVC], BF16, esAq)
        sq_scr = sb("sq_scr", [128, SEQ], BF16, esAq)
        nrm_scr = sb("nrm_scr", [1, SEQ], F32, esAq)
        bnd_scr = sb("bnd_scr", [1, 20], F32, esAq)
        negb_col = sb("negb_col", [128, NH], F32, esAq)

        esA1 = ExitStack()
        wk_sb = sb("wk_sb", [128, 16, NGC * KVC], BF16, esA1)
        wv_sb = sb("wv_sb", [128, 16, NGC * KVC], BF16, esA1)
        wq_buf = sb("wq_buf", [128, 2, 16, KVC], BF16, esA1)

        @block.gpsimd
        def _(gpsimd):
            xTv = xT.rearrange("(kc p) t -> p kc t", p=128)
            for i in range(4):
                dB.inc(gpsimd.dma_start(out=xT_sb[:, 4 * i:4 * i + 4, :],
                                        in_=xTv[:, 4 * i:4 * i + 4, :]), 16)
            dB.inc(gpsimd.dma_start(
                out=wk_sb[:, :, :],
                in_=wk.rearrange("(kc p) e -> p kc e", p=128)), 16)
            dB.inc(gpsimd.dma_start(
                out=wv_sb[:, :, :],
                in_=wv.rearrange("(kc p) e -> p kc e", p=128)), 16)
        xT_loaded = dB.n

        esA1p = ExitStack()
        ps_q = [esA1p.enter_context(nc.psum_tensor(f"ps_q{i}", [128, 4, 512], F32))
                for i in range(2)]
        qmm_snap = [0] * (NH + NGC)
        qev_snap = [0] * (NH + NGC)
        wqld_snap = [0] * NH

        for h in range(NH):
            @block.gpsimd
            def _(gpsimd, h=h):
                if h >= 2:
                    gpsimd.wait_ge(tS.h, qmm_snap[h - 2])
                dsl[h % 2].inc(gpsimd.dma_start(
                    out=wq_buf[:, h % 2, :, :],
                    in_=wqh[h].rearrange("(kc p) c -> p kc c", p=128)), 16)
            wqld_snap[h] = dsl[h % 2].n

            @block.tensor
            def _(tensor, h=h):
                tensor.wait_ge(dB.h, xT_loaded)
                tensor.wait_ge(dsl[h % 2].h, wqld_snap[h])
                if h >= 2:
                    tensor.wait_ge(sS.h, qev_snap[h - 2])
                for kc in range(16):
                    for qc in range(4):
                        tS.inc(tensor.matmul(
                            ps_q[h % 2][:, qc, :], wq_buf[:, h % 2, kc, :],
                            xT_sb[:, kc, qc * 512:(qc + 1) * 512],
                            start=(kc == 0), stop=(kc == 15)))
            qmm_snap[h] = tS.n

            @block.scalar
            def _(scalar, h=h):
                scalar.wait_ge(tS.h, qmm_snap[h])
                sS.inc(scalar.copy(qT_sb[:, h, :],
                                   ps_q[h % 2].rearrange("p a b -> p (a b)")))
            qev_snap[h] = sS.n

        if "qt0" in DBG:
            @block.sync
            def _(sync):
                sync.wait_ge(sS.h, qev_snap[0])
                stm.inc(sync.dma_start(out=dbg_t["qt0"], in_=qT_sb[:, 0, :]), 16)

        for g in range(NGC):
            u = NH + g

            @block.tensor
            def _(tensor, g=g, u=u):
                tensor.wait_ge(sS.h, qev_snap[u - 2])
                for kc in range(16):
                    for qc in range(4):
                        tS.inc(tensor.matmul(
                            ps_q[u % 2][:, qc, :],
                            wk_sb[:, kc, g * KVC:(g + 1) * KVC],
                            xT_sb[:, kc, qc * 512:(qc + 1) * 512],
                            start=(kc == 0), stop=(kc == 15)))
            qmm_snap[u] = tS.n

            @block.scalar
            def _(scalar, g=g, u=u):
                scalar.wait_ge(tS.h, qmm_snap[u])
                sS.inc(scalar.copy(kT_sb[:, g, :],
                                   ps_q[u % 2].rearrange("p a b -> p (a b)")))
            qev_snap[u] = sS.n

        @block.tensor
        def _(tensor):
            tensor.wait_ge(sS.h, qev_snap[NH])
            tensor.wait_ge(sS.h, qev_snap[NH + 1])
        esA1p.close()

        esA1v = ExitStack()
        ps_v = [esA1v.enter_context(nc.psum_tensor(f"ps_v{i}", [128, NGC * KVC], F32))
                for i in range(2)]
        vmm_snap = [0] * 16
        vev_snap = [0] * 16
        for tt in range(16):
            @block.tensor
            def _(tensor, tt=tt):
                if tt >= 2:
                    tensor.wait_ge(sS.h, vev_snap[tt - 2])
                for kc in range(16):
                    tS.inc(tensor.matmul(
                        ps_v[tt % 2][:, :], xT_sb[:, kc, tt * 128:(tt + 1) * 128],
                        wv_sb[:, kc, :], start=(kc == 0), stop=(kc == 15)))
            vmm_snap[tt] = tS.n

            @block.scalar
            def _(scalar, tt=tt):
                scalar.wait_ge(tS.h, vmm_snap[tt])
                sS.inc(scalar.copy(v_sb[:, tt, :], ps_v[tt % 2][:, :]))
            vev_snap[tt] = sS.n
        a1_done = sS.n

        @block.tensor
        def _(tensor):
            tensor.wait_ge(sS.h, a1_done)
        esA1v.close()
        esA1.close()

        esB = ExitStack()
        ps_nb = esB.enter_context(nc.psum_tensor("ps_nb", [1, 4, 512], F32))
        ps_nc = esB.enter_context(nc.psum_tensor("ps_nc", [128, NH], F32))
        bnd_units = [("q", h) for h in range(NH)] + [("k", g) for g in range(NGC)]
        bsq_snap = [0] * 10
        bmm_snap = [0] * 10
        bred_snap = [0] * 10
        for ui, (kind, i) in enumerate(bnd_units):
            @block.scalar
            def _(scalar, ui=ui, kind=kind, i=i):
                scalar.wait_ge(sS.h, a1_done)
                if ui > 0:
                    scalar.wait_ge(tS.h, bmm_snap[ui - 1])
                src = qT_sb[:, i, :] if kind == "q" else kT_sb[:, i, :]
                sS.inc(scalar.activation(sq_scr[:, :], src, AF.Square))
            bsq_snap[ui] = sS.n

            @block.tensor
            def _(tensor, ui=ui):
                tensor.wait_ge(sS.h, bsq_snap[ui])
                if ui == 0:
                    tensor.wait_ge(vS.h, consts_done)
                if ui > 0:
                    tensor.wait_ge(vS.h, bred_snap[ui - 1])
                for qc in range(4):
                    tS.inc(tensor.matmul(
                        ps_nb[0:1, qc, :], ones_cb[:, :],
                        sq_scr[:, qc * 512:(qc + 1) * 512],
                        start=True, stop=True))
            bmm_snap[ui] = tS.n

            @block.vector
            def _(vector, ui=ui):
                vector.wait_ge(vS.h, vS.n)
                vector.wait_ge(tS.h, bmm_snap[ui])
                vector.tensor_copy(nrm_scr[:, :],
                                   ps_nb.rearrange("p a b -> p (a b)"))
                vector.drain()
                vS.inc(vector.reduce_max(bnd_scr[:, ui:ui + 1], nrm_scr[:, :],
                                         axis=AX.X))
            bred_snap[ui] = vS.n

        @block.scalar
        def _(scalar):
            scalar.wait_ge(vS.h, bred_snap[9])
            sS.inc(scalar.activation(bnd_scr[:, 0:10], bnd_scr[:, 0:10],
                                     AF.Sqrt))
        bnd_sqrt = sS.n

        @block.vector
        def _(vector):
            vector.wait_ge(vS.h, vS.n)
            vector.wait_ge(sS.h, bnd_sqrt)
            for h in range(NH):
                vector.tensor_tensor(bnd_scr[:, 11 + h:12 + h],
                                     bnd_scr[:, h:h + 1],
                                     bnd_scr[:, 8 + h // 4:9 + h // 4], ALU.mult)
            vector.drain()
            vS.inc(vector.tensor_scalar(bnd_scr[:, 11:11 + NH],
                                        bnd_scr[:, 11:11 + NH],
                                        -SCALE_ATT * 1.001, -1e-3,
                                        ALU.mult, ALU.add))
        negb_row = vS.n

        @block.tensor
        def _(tensor):
            tensor.wait_ge(vS.h, negb_row)
            for h in range(NH):
                tS.inc(tensor.matmul(ps_nc[:, h:h + 1], ones_rf[:, :],
                                     bnd_scr[0:1, 11 + h:12 + h],
                                     start=True, stop=True))
        negb_mm = tS.n

        @block.vector
        def _(vector):
            vector.wait_ge(vS.h, vS.n)
            vector.wait_ge(tS.h, negb_mm)
            vS.inc(vector.tensor_copy(negb_col[:, :], ps_nc[:, :]))
        negb_done = vS.n

        @block.tensor
        def _(tensor):
            tensor.wait_ge(vS.h, negb_done)
        esB.close()

        if "qt1" in DBG:
            @block.sync
            def _(sync):
                sync.wait_ge(sS.h, a1_done)
                stm.inc(sync.dma_start(out=dbg_t["qt1"], in_=qT_sb[:, 0, :]), 16)
        if "qt" in DBG:
            @block.sync
            def _(sync):
                sync.wait_ge(sS.h, a1_done)
                stm.inc(sync.dma_start(
                    out=dbg_t["qt"], in_=qT_sb.rearrange("p a b -> p (a b)")), 16)
        if "kt" in DBG:
            @block.sync
            def _(sync):
                sync.wait_ge(sS.h, a1_done)
                stm.inc(sync.dma_start(
                    out=dbg_t["kt"], in_=kT_sb.rearrange("p a b -> p (a b)")), 16)
        if "vt" in DBG:
            @block.sync
            def _(sync):
                sync.wait_ge(sS.h, a1_done)
                stm.inc(sync.dma_start(
                    out=dbg_t["vt"], in_=v_sb.rearrange("p a b -> p (a b)")), 16)
        if "negb" in DBG:
            @block.sync
            def _(sync):
                sync.wait_ge(vS.h, negb_done)
                stm.inc(sync.dma_start(out=dbg_t["negb"], in_=negb_col[:, :]), 16)

        esA2 = ExitStack()
        sum_row = [sb(f"sum_row{i}", [1, 512], F32, esA2) for i in range(2)]
        rec_row = [sb(f"rec_row{i}", [1, 512], F32, esA2) for i in range(2)]
        rb_sb = [sb(f"rb_sb{i}", [128, 512], F32, esA2) for i in range(2)]
        ps_S = [esA2.enter_context(nc.psum_tensor(f"ps_S{i}", [128, 512], F32))
                for i in range(2)]
        ps_sum = [esA2.enter_context(nc.psum_tensor(f"ps_sum{i}", [1, 512], F32))
                  for i in range(2)]
        ps_rb = [esA2.enter_context(nc.psum_tensor(f"ps_rb{i}", [128, 512], F32))
                 for i in range(2)]
        ps_ctx = [esA2.enter_context(nc.psum_tensor(f"ps_ctx{i}", [128, 512], F32))
                  for i in range(2)]

        chunks = [(h, qc) for h in range(NH) for qc in range(4)]
        NCH = len(chunks)
        tiles_of = []
        g0 = 0
        for (h, qc) in chunks:
            tiles_of.append(list(range(g0, g0 + 4 * qc + 4)))
            g0 += 4 * qc + 4
        exp_snap = {}
        smm_snap = {}
        maskd = [0] * NCH
        sumsd = [0] * NCH
        recipd = [0] * NCH
        rbmmd = [0] * NCH
        rbevd = [0] * NCH
        ctxmmd = [0] * NCH
        ctxevd = [0] * NCH

        def a2_S(ci):
            h, qc = chunks[ci]
            sbase = sS.n
            t0c = tiles_of[ci][0]

            @block.tensor
            def _(tensor):
                if ci == 0:
                    tensor.wait_ge(vS.h, consts_done)
                g = h // 4
                for j, tg in enumerate(tiles_of[ci]):
                    if tg >= 2:
                        tg2 = tg - 2
                        snap = (exp_snap[tg2] if tg2 < t0c
                                else sbase + (tg2 - t0c + 1))
                        tensor.wait_ge(sS.h, snap)
                    tS.inc(tensor.matmul(
                        ps_S[tg % 2][:, :],
                        kT_sb[:, g, j * 128:(j + 1) * 128],
                        qT_sb[:, h, qc * 512:(qc + 1) * 512],
                        start=True, stop=True))
                    smm_snap[tg] = tS.n

        def a2_exp(ci):
            h, qc = chunks[ci]

            @block.scalar
            def _(scalar):
                if ci >= 2:
                    scalar.wait_ge(tS.h, ctxmmd[ci - 2])
                for j, tg in enumerate(tiles_of[ci]):
                    scalar.wait_ge(tS.h, smm_snap[tg])
                    sS.inc(scalar.activation(
                        expS[ci % 2][:, j, :], ps_S[tg % 2][:, :], AF.Exp,
                        bias=negb_col[:, h:h + 1], scale=SCALE_ATT))
                    exp_snap[tg] = sS.n

        def a2_mask(ci):
            h, qc = chunks[ci]

            @block.vector
            def _(vector):
                vector.wait_ge(vS.h, vS.n)
                vector.wait_ge(sS.h, exp_snap[tiles_of[ci][-1]])
                for m in range(4):
                    vS.inc(vector.tensor_mul(expS[ci % 2][:, 4 * qc + m, :],
                                             expS[ci % 2][:, 4 * qc + m, :],
                                             masks[:, m, :]))
            maskd[ci] = vS.n

        def a2_sums(ci):
            h, qc = chunks[ci]
            nkt = 4 * qc + 4

            @block.tensor
            def _(tensor):
                tensor.wait_ge(vS.h, maskd[ci])
                if ci >= 2:
                    tensor.wait_ge(vS.h, recipd[ci - 2])
                for j in range(nkt):
                    tS.inc(tensor.matmul(
                        ps_sum[ci % 2][0:1, :], ones_cb[:, :],
                        expS[ci % 2][:, j, :],
                        start=(j == 0), stop=(j == nkt - 1)))
            sumsd[ci] = tS.n

        def a2_recip(ci):
            @block.vector
            def _(vector):
                vector.wait_ge(vS.h, vS.n)
                vector.wait_ge(tS.h, sumsd[ci])
                vector.drain()
                vS.inc(vector.tensor_copy(sum_row[ci % 2][:, :],
                                          ps_sum[ci % 2][0:1, :]))
            recipd[ci] = vS.n

        def a2_rbctx(ci):
            h, qc = chunks[ci]
            nkt = 4 * qc + 4
            g = h // 4

            @block.tensor
            def _(tensor):
                tensor.wait_ge(vS.h, recipd[ci])
                if ci >= 2:
                    tensor.wait_ge(vS.h, rbevd[ci - 2])
                    tensor.wait_ge(vS.h, ctxevd[ci - 2])
                tS.inc(tensor.matmul(ps_rb[ci % 2][:, :], ones_rf[:, :],
                                     sum_row[ci % 2][0:1, :],
                                     start=True, stop=True))
                rbmmd[ci] = tS.n
                for j in range(nkt):
                    tS.inc(tensor.matmul(
                        ps_ctx[ci % 2][:, :],
                        v_sb[:, j, g * KVC:(g + 1) * KVC],
                        expS[ci % 2][:, j, :],
                        start=(j == 0), stop=(j == nkt - 1)))
                ctxmmd[ci] = tS.n

        def a2_evac(ci):
            h, qc = chunks[ci]

            @block.vector
            def _(vector):
                vector.wait_ge(vS.h, vS.n)
                vector.wait_ge(tS.h, rbmmd[ci])
                vector.drain()
                vS.inc(vector.reciprocal(rb_sb[ci % 2][:, :],
                                         ps_rb[ci % 2][:, :]))
                rbevd[ci] = vS.n
                vector.wait_ge(tS.h, ctxmmd[ci])
                vector.wait_ge(vS.h, vS.n)
                vector.drain()
                vS.inc(vector.tensor_mul(
                    ctxT_sb[:, h, qc * 512:(qc + 1) * 512],
                    ps_ctx[ci % 2][:, :], rb_sb[ci % 2][:, :]))
                ctxevd[ci] = vS.n

        for ci in range(NCH):
            a2_S(ci)
            a2_exp(ci)
            a2_mask(ci)
            if ci == 0 and "es0" in DBG:
                @block.sync
                def _(sync):
                    sync.wait_ge(vS.h, maskd[0])
                    stm.inc(sync.dma_start(
                        out=dbg_t["es0"],
                        in_=expS[0].rearrange("p a b -> p (a b)")), 16)
            if ci >= 1:
                a2_sums(ci - 1)
                a2_recip(ci - 1)
                a2_rbctx(ci - 1)
                a2_evac(ci - 1)
        a2_sums(NCH - 1)
        a2_recip(NCH - 1)
        a2_rbctx(NCH - 1)
        a2_evac(NCH - 1)
        a2_done = ctxevd[NCH - 1]
        a2_last_mm = ctxmmd[NCH - 1]

        if "ctxt" in DBG:
            @block.sync
            def _(sync):
                sync.wait_ge(vS.h, a2_done)
                stm.inc(sync.dma_start(
                    out=dbg_t["ctxt"],
                    in_=ctxT_sb.rearrange("p h s -> p (h s)")), 16)

        esA2.close()
        esD = ExitStack()
        dstage = [sb(f"dstage{i}", [128, D], BF16, esD) for i in range(2)]
        ps_d = [esD.enter_context(nc.psum_tensor(f"ps_d{i}", [128, 4, 512], F32))
                for i in range(2)]

        @block.gpsimd
        def _(gpsimd):
            gpsimd.wait_ge(tS.h, a2_last_mm)
            dB.inc(gpsimd.dma_start(
                out=wdense_sb[:, :, :],
                in_=wdense.rearrange("(h p) d -> p h d", p=128)), 16)
        wdense_ld = dB.n

        dmm_snap = [0] * 16
        dev_snap = [0] * 16
        dst_snap = [0] * 16
        for qt in range(16):
            @block.tensor
            def _(tensor, qt=qt):
                if qt == 0:
                    tensor.wait_ge(dB.h, wdense_ld)
                    tensor.wait_ge(vS.h, a2_done)
                if qt >= 2:
                    tensor.wait_ge(vS.h, dev_snap[qt - 2])
                for ec in range(NH):
                    for nn in range(4):
                        tS.inc(tensor.matmul(
                            ps_d[qt % 2][:, nn, :],
                            ctxT_sb[:, ec, qt * 128:(qt + 1) * 128],
                            wdense_sb[:, ec, nn * 512:(nn + 1) * 512],
                            start=(ec == 0), stop=(ec == NH - 1)))
            dmm_snap[qt] = tS.n

            @block.vector
            def _(vector, qt=qt):
                vector.wait_ge(vS.h, vS.n)
                vector.wait_ge(tS.h, dmm_snap[qt])
                if qt >= 2:
                    vector.wait_ge(tsl[qt % 2].h, dst_snap[qt - 2])
                vS.inc(vector.tensor_copy(
                    dstage[qt % 2][:, :],
                    ps_d[qt % 2].rearrange("p a b -> p (a b)")))
            dev_snap[qt] = vS.n

            @block.sync
            def _(sync, qt=qt):
                sync.wait_ge(vS.h, dev_snap[qt])
                tsl[qt % 2].inc(sync.dma_start(
                    out=rs_in[qt * 128:(qt + 1) * 128, :],
                    in_=dstage[qt % 2][:, :]), 16)
            dst_snap[qt] = tsl[qt % 2].n
        dense_last = dmm_snap[15]
        dense_stores = (tsl[0].n, tsl[1].n)

        @block.gpsimd
        def _(gpsimd):
            gpsimd.wait_ge(tsl[0].h, dense_stores[0])
            gpsimd.wait_ge(tsl[1].h, dense_stores[1])
            ccR.inc(gpsimd.collective_compute(
                "ReduceScatter", ALU.add,
                replica_groups=[[0, 1], [2, 3], [4, 5], [6, 7]],
                ins=[rs_in.ap().opt()], outs=[rs_out.ap().opt()]))
        rs_done = ccR.n

        @block.tensor
        def _(tensor):
            tensor.wait_ge(vS.h, dev_snap[15])
        esD.close()
        esAq.close()

        if "attn" in DBG:
            @block.sync
            def _(sync):
                sync.wait_ge(ccR.h, rs_done)
                stm.inc(sync.dma_start(out=dbg_t["attn"], in_=rs_out[:, :]), 16)

        esL = ExitStack()
        xo_sb = [sb(f"xo{i}", [128, D], F32, esL) for i in range(2)]
        at_sb = [sb(f"at{i}", [128, D], BF16, esL) for i in range(2)]
        h_st = [sb(f"h_st{i}", [128, D], F32, esL) for i in range(2)]
        ln_scr = sb("ln_scr", [128, D], F32, esL)
        g_scr = sb("g_scr", [128, D], F32, esL)
        ln_cols = sb("ln_cols", [128, 8], F32, esL)
        htmp = sb("htmp", [128, 16, 128], F32, esL)
        ps_t1 = [esL.enter_context(nc.psum_tensor(f"ps_t1{i}", [128, 128], F32))
                 for i in range(2)]
        ps_r = esL.enter_context(nc.psum_tensor("ps_r", [128, E], F32))

        ld_snap = [0] * 8
        sig_snap = [0] * 8
        hdone = [0] * 8
        hdma = [0] * 8
        trN = [0] * 8
        htc = [0] * 8
        rmm = [0] * 8
        for tt in range(8):
            @block.scalar
            def _(scalar, tt=tt):
                if tt == 0:
                    scalar.wait_ge(ccR.h, rs_done)
                if tt >= 2:
                    scalar.wait_ge(vS.h, hdone[tt - 2])
                ssl[tt % 2].inc(scalar.dma_start(
                    out=xo_sb[tt % 2][:, :],
                    in_=x_own[tt * 128:(tt + 1) * 128, :]), 16)
                ssl[tt % 2].inc(scalar.dma_start(
                    out=at_sb[tt % 2][:, :],
                    in_=rs_out[tt * 128:(tt + 1) * 128, :]), 16)
            ld_snap[tt] = ssl[tt % 2].n

            @block.vector
            def _(vector, tt=tt):
                vector.wait_ge(vS.h, vS.n)
                V = vector
                V.wait_ge(ssl[tt % 2].h, ld_snap[tt])
                if tt >= 2:
                    V.wait_ge(dsl[2 + tt % 2].h, hdma[tt - 2])
                    V.wait_ge(tS.h, smm_snap[("t1", tt - 2, 15)])
                hs = h_st[tt % 2]
                V.drain()
                V.tensor_copy(g_scr[:, :], at_sb[tt % 2][:, :])
                V.drain()
                vS.inc(V.tensor_add(hs[:, :], g_scr[:, :],
                                    xo_sb[tt % 2][:, :]))
            resid_done = vS.n

            @block.scalar
            def _(scalar, tt=tt, resid_done=resid_done):
                scalar.wait_ge(vS.h, resid_done)
                scalar.wait_ge(sS.h, sS.n)
                hs = h_st[tt % 2]
                sS.inc(scalar.activation(ln_scr[:, :], hs[:, :], AF.Copy,
                                         accum_out=ln_cols[:, 0:1]))
                scalar.wait_ge(sS.h, sS.n)
                sS.inc(scalar.activation(ln_scr[:, :], hs[:, :], AF.Square,
                                         accum_out=ln_cols[:, 2:3]))
            stats_done = sS.n

            @block.vector
            def _(vector, tt=tt, stats_done=stats_done):
                vector.wait_ge(vS.h, vS.n)
                V = vector
                V.wait_ge(sS.h, stats_done)
                V.drain()
                V.tensor_scalar(ln_cols[:, 1:2], ln_cols[:, 0:1], 1.0 / D,
                                None, ALU.mult)
                V.drain()
                V.tensor_tensor(ln_cols[:, 3:4], ln_cols[:, 1:2],
                                ln_cols[:, 1:2], ALU.mult)
                V.drain()
                V.tensor_scalar(ln_cols[:, 4:5], ln_cols[:, 2:3], 1.0 / D,
                                None, ALU.mult)
                V.drain()
                V.tensor_sub(ln_cols[:, 5:6], ln_cols[:, 4:5], ln_cols[:, 3:4])
                V.drain()
                vS.inc(V.tensor_scalar(ln_cols[:, 5:6], ln_cols[:, 5:6], 1.0,
                                       LN_EPS, ALU.mult, ALU.add))
            var_done = vS.n

            @block.scalar
            def _(scalar, var_done=var_done):
                scalar.wait_ge(vS.h, var_done)
                sS.inc(scalar.activation(ln_cols[:, 6:7], ln_cols[:, 5:6],
                                         AF.Sqrt))
            std_done = sS.n

            @block.vector
            def _(vector, tt=tt, std_done=std_done):
                vector.wait_ge(vS.h, vS.n)
                V = vector
                V.wait_ge(sS.h, std_done)
                V.reciprocal(ln_cols[:, 7:8], ln_cols[:, 6:7])
                V.drain()
                vS.inc(V.tensor_scalar(h_st[tt % 2][:, :], h_st[tt % 2][:, :],
                                       ln_cols[:, 1:2], ln_cols[:, 7:8],
                                       ALU.subtract, ALU.mult))
            hdone[tt] = vS.n

            @block.gpsimd
            def _(gpsimd, tt=tt):
                gpsimd.wait_ge(vS.h, hdone[tt])
                dsl[2 + tt % 2].inc(gpsimd.dma_start(
                    out=hag_in[tt * 128:(tt + 1) * 128, :],
                    in_=h_st[tt % 2][:, :]), 16)
            hdma[tt] = dsl[2 + tt % 2].n

            sbase2 = sS.n

            @block.tensor
            def _(tensor, tt=tt, sbase2=sbase2):
                tensor.wait_ge(vS.h, hdone[tt])
                if tt == 0:
                    tensor.wait_ge(tS.h, dense_last)
                if tt >= 1:
                    tensor.wait_ge(sS.h, trN[tt - 1])
                for dc in range(16):
                    if dc >= 2:
                        tensor.wait_ge(sS.h, sbase2 + dc - 1)
                    tS.inc(tensor.transpose(
                        ps_t1[dc % 2][:, :],
                        h_st[tt % 2][:, dc * 128:(dc + 1) * 128], ident_f[:, :]))
                    smm_snap[("t1", tt, dc)] = tS.n

            @block.scalar
            def _(scalar, tt=tt):
                for dc in range(16):
                    scalar.wait_ge(tS.h, smm_snap[("t1", tt, dc)])
                    sS.inc(scalar.copy(htmp[:, dc, :], ps_t1[dc % 2][:, :]))
            trN[tt] = sS.n
            assert trN[tt] == sbase2 + 16

            @block.tensor
            def _(tensor, tt=tt):
                tensor.wait_ge(sS.h, trN[tt])
                if tt == 0:
                    tensor.wait_ge(dsc.h, routerT_ld)
                if tt >= 1:
                    tensor.wait_ge(sS.h, sig_snap[tt - 1])
                for dc in range(16):
                    tS.inc(tensor.matmul(
                        ps_r[:, :], htmp[:, dc, :], routerT_sb[:, dc, :],
                        start=(dc == 0), stop=(dc == 15)))
            rmm[tt] = tS.n

            @block.vector
            def _(vector, tt=tt):
                vector.wait_ge(vS.h, vS.n)
                vector.wait_ge(sS.h, trN[tt])
                vS.inc(vector.tensor_copy(hT_bf[:, :, tt * 128:(tt + 1) * 128],
                                          htmp[:, :, :]))
            htc[tt] = vS.n

            @block.scalar
            def _(scalar, tt=tt):
                scalar.wait_ge(tS.h, rmm[tt])
                sS.inc(scalar.activation(scores_sb[:, tt, :], ps_r[:, :],
                                         AF.Sigmoid))
            sig_snap[tt] = sS.n
        scores_done = sS.n
        h_all_dma = (dsl[2].n, dsl[3].n)
        ht_all = htc[7]
        p4_ts_all = tS.n

        @block.gpsimd
        def _(gpsimd):
            gpsimd.wait_ge(dsl[2].h, h_all_dma[0])
            gpsimd.wait_ge(dsl[3].h, h_all_dma[1])
            ccH.inc(gpsimd.collective_compute(
                "AllGather", ALU.bypass, replica_groups=[list(range(NCORE))],
                ins=[hag_in.ap().opt()], outs=[hag_out.ap().opt()]))
        hag_done = ccH.n

        if "h1" in DBG:
            @block.sync
            def _(sync):
                sync.wait_ge(ccH.h, hag_done)
                stm.inc(sync.dma_start(out=dbg_t["h1"], in_=hag_in[:, :]), 16)

        @block.tensor
        def _(tensor):
            tensor.wait_ge(sS.h, scores_done)
        @block.vector
        def _(vector):
            vector.wait_ge(vS.h, vS.n)
            vector.wait_ge(sS.h, scores_done)
        esL.close()

        guld = {}
        esXw = ExitStack()
        wgbuf = [sb(f"wgbuf{i}", [128, 4, 1024], BF16, esXw) for i in range(3)]

        @block.gpsimd
        def _(gpsimd):
            for k in range(3):
                e, r0 = divmod(k, 8)
                pw, b4 = divmod(r0, 4)
                wsrc = [wg, wu][pw]
                dsl[4 + k % 3].inc(gpsimd.dma_start(
                    out=wgbuf[k % 3][:, :, :],
                    in_=wsrc[e].rearrange("(c p) i -> p c i", p=128)[
                        :, b4 * 4:b4 * 4 + 4, :]), 16)
                guld[(e, pw, b4)] = dsl[4 + k % 3].n

        esR = ExitStack()
        r_sc = sb("r_sc", [128, E], F32, esR)
        r_m1 = sb("r_m1", [128, NGRP], F32, esR)
        r_b32 = sb("r_b32", [128, E], F32, esR)
        r_eq = sb("r_eq", [128, E], F32, esR)
        r_cand = sb("r_cand", [128, E], F32, esR)
        r_m2 = sb("r_m2", [128, NGRP], F32, esR)
        r_gsc = sb("r_gsc", [128, NGRP], F32, esR)
        r_col = sb("r_col", [128, 8], F32, esR)
        r_eq8 = sb("r_eq8", [128, NGRP], F32, esR)
        r_smask = sb("r_smask", [128, E], F32, esR)
        r_msk = sb("r_msk", [128, E], F32, esR)
        r_acc = sb("r_acc", [128, E], F32, esR)
        r_tmp = sb("r_tmp", [128, E], F32, esR)

        @block.vector
        def _(vector):
            vector.wait_ge(vS.h, vS.n)
            vector.wait_ge(sS.h, scores_done)
            vector.wait_ge(dsl[2].h, h_all_dma[0])
            vector.wait_ge(dsl[3].h, h_all_dma[1])
            for tt in range(8):
                V = vector
                V.drain()
                V.tensor_add(r_sc[:, :], scores_sb[:, tt, :], ebias_bc[:, :])
                V.drain()
                V.reduce_max(r_m1[:, :],
                             r_sc.rearrange("p (g e) -> p g e", e=EPG),
                             axis=AX.X)
                V.drain()
                for j in range(EPG):
                    V.tensor_copy(
                        r_b32.rearrange("p (g e) -> p g e", e=EPG)[:, :, j],
                        r_m1[:, :])
                V.drain()
                V.tensor_tensor(r_eq[:, :], r_sc[:, :], r_b32[:, :],
                                ALU.is_equal)
                V.drain()
                V.scalar_tensor_tensor(r_cand[:, :], r_eq[:, :], -BIGI,
                                       iplusb[:, 0:E], ALU.mult, ALU.add)
                V.drain()
                V.tensor_reduce(r_m2[:, :],
                                r_cand.rearrange("p (g e) -> p g e", e=EPG),
                                axis=AX.X, op=ALU.min)
                V.drain()
                for j in range(EPG):
                    V.tensor_copy(
                        r_b32.rearrange("p (g e) -> p g e", e=EPG)[:, :, j],
                        r_m2[:, :])
                V.drain()
                V.tensor_tensor(r_eq[:, :], iota_bc[:, 0:E], r_b32[:, :],
                                ALU.is_equal)
                V.drain()
                V.scalar_tensor_tensor(r_cand[:, :], r_eq[:, :], -BIGM,
                                       r_sc[:, :], ALU.mult, ALU.add)
                V.drain()
                V.reduce_max(r_m2[:, :],
                             r_cand.rearrange("p (g e) -> p g e", e=EPG),
                             axis=AX.X)
                V.drain()
                V.tensor_add(r_gsc[:, :], r_m1[:, :], r_m2[:, :])
                V.drain()
                V.memset(r_smask[:, :], 0.0)
                for it in range(TKG):
                    V.reduce_max(r_col[:, 0:1], r_gsc[:, :], axis=AX.X)
                    V.drain()
                    V.tensor_scalar(r_eq8[:, :], r_gsc[:, :], r_col[:, 0:1],
                                    None, ALU.is_equal)
                    V.drain()
                    V.scalar_tensor_tensor(r_eq8[:, :], r_eq8[:, :], -BIGI,
                                           iplusb[:, 0:NGRP], ALU.mult,
                                           ALU.add)
                    V.drain()
                    V.tensor_reduce(r_col[:, 1:2], r_eq8[:, :], axis=AX.X,
                                    op=ALU.min)
                    V.drain()
                    V.tensor_scalar(r_eq8[:, :], iplusb[:, 0:NGRP],
                                    r_col[:, 1:2], BIGI, ALU.subtract,
                                    ALU.is_equal)
                    V.drain()
                    V.scalar_tensor_tensor(r_gsc[:, :], r_eq8[:, :], -BIGM,
                                           r_gsc[:, :], ALU.mult, ALU.add)
                    V.tensor_scalar(r_tmp[:, :], grpidx_bc[:, :],
                                    r_col[:, 1:2], None, ALU.is_equal)
                    V.drain()
                    V.tensor_add(r_smask[:, :], r_smask[:, :], r_tmp[:, :])
                    V.drain()
                V.tensor_mul(r_msk[:, :], r_sc[:, :], r_smask[:, :])
                V.drain()
                V.memset(r_acc[:, :], 0.0)
                for kk in range(TOPK):
                    V.reduce_max(r_col[:, 2:3], r_msk[:, :], axis=AX.X)
                    V.drain()
                    V.tensor_scalar(r_eq[:, :], r_msk[:, :], r_col[:, 2:3],
                                    None, ALU.is_equal)
                    V.drain()
                    V.scalar_tensor_tensor(r_cand[:, :], r_eq[:, :], -BIGI,
                                           iplusb[:, 0:E], ALU.mult, ALU.add)
                    V.drain()
                    V.tensor_reduce(idxk_sb[:, tt, kk:kk + 1], r_cand[:, :],
                                    axis=AX.X, op=ALU.min)
                    V.drain()
                    V.tensor_scalar(r_eq[:, :], iplusb[:, 0:E],
                                    idxk_sb[:, tt, kk:kk + 1], BIGI,
                                    ALU.subtract, ALU.is_equal)
                    V.drain()
                    V.tensor_mul(r_tmp[:, :], r_eq[:, :], scores_sb[:, tt, :])
                    V.drain()
                    V.reduce_sum(wtop_sb[:, tt, kk:kk + 1], r_tmp[:, :],
                                 axis=AX.X)
                    V.tensor_add(r_acc[:, :], r_acc[:, :], r_eq[:, :])
                    V.drain()
                    V.scalar_tensor_tensor(r_msk[:, :], r_eq[:, :], -BIGM,
                                           r_msk[:, :], ALU.mult, ALU.add)
                    V.drain()
                V.reduce_sum(r_col[:, 3:4], wtop_sb[:, tt, :], axis=AX.X)
                V.drain()
                V.tensor_scalar(r_col[:, 3:4], r_col[:, 3:4], 1.0, 1e-20,
                                ALU.mult, ALU.add)
                V.drain()
                V.reciprocal(r_col[:, 4:5], r_col[:, 3:4])
                V.drain()
                V.tensor_scalar(wtop_sb[:, tt, :], wtop_sb[:, tt, :],
                                r_col[:, 4:5], SCALE_W, ALU.mult, ALU.mult)
                vS.inc(V.tensor_copy(A_loc[:, tt, :], r_acc[:, :]))
        topk_done = vS.n

        if "scores" in DBG:
            @block.sync
            def _(sync):
                sync.wait_ge(vS.h, topk_done)
                stm.inc(sync.dma_start(
                    out=dbg_t["scores"],
                    in_=scores_sb.rearrange("p a b -> p (a b)")), 16)

        @block.sync
        def _(sync):
            sync.wait_ge(vS.h, topk_done)
            stm.inc(sync.dma_start(
                out=aag_in.rearrange("(n p) e -> p n e", p=128),
                in_=A_loc[:, :, :]), 16)
        a_store = stm.n

        esS7 = ExitStack()
        sg_buf = [sb(f"sg_buf{i}", [128, 16, 128], BF16, esS7) for i in range(2)]
        su_buf = [sb(f"su_buf{i}", [128, 16, 128], BF16, esS7) for i in range(2)]
        gs_scr = sb("gs_scr", [128, TPC], F32, esS7)
        us_scr = sb("us_scr", [128, TPC], F32, esS7)
        ps_G = [esS7.enter_context(nc.psum_tensor(f"ps_G{i}", [128, 2, 512], F32))
                for i in range(2)]
        ps_U = [esS7.enter_context(nc.psum_tensor(f"ps_U{i}", [128, 2, 512], F32))
                for i in range(2)]
        sgld = [0] * 16
        shmm = [0] * 16
        shev = [0] * 16

        def emit_sgld(jc):
            @block.gpsimd
            def _(gpsimd, jc=jc):
                if jc >= 2:
                    gpsimd.wait_ge(tS.h, shmm[jc - 2])
                else:
                    gpsimd.wait_ge(tS.h, p4_ts_all)
                dsl[jc % 2].inc(gpsimd.dma_start(
                    out=sg_buf[jc % 2][:, :, :],
                    in_=sgT[jc].rearrange("(kc p) j -> p kc j", p=128)), 16)
                dsl[jc % 2].inc(gpsimd.dma_start(
                    out=su_buf[jc % 2][:, :, :],
                    in_=suT[jc].rearrange("(kc p) j -> p kc j", p=128)), 16)
            sgld[jc] = dsl[jc % 2].n

        emit_sgld(0)
        emit_sgld(1)

        @block.gpsimd
        def _(gpsimd):
            gpsimd.wait_ge(stm.h, a_store)
            ccA.inc(gpsimd.collective_compute(
                "AllGather", ALU.bypass, replica_groups=[list(range(NCORE))],
                ins=[aag_in.ap().opt()], outs=[aag_out.ap().opt()]))
        aag_done = ccA.n


        for jc in range(16):
            if jc >= 2:
                emit_sgld(jc)

            @block.tensor
            def _(tensor, jc=jc):
                tensor.wait_ge(dsl[jc % 2].h, sgld[jc])
                if jc == 0:
                    tensor.wait_ge(vS.h, ht_all)
                    tensor.wait_ge(tS.h, a2_last_mm)
                if jc >= 2:
                    tensor.wait_ge(vS.h, shev[jc - 2])
                for dc in range(16):
                    for nn in range(2):
                        tS.inc(tensor.matmul(
                            ps_G[jc % 2][:, nn, :], sg_buf[jc % 2][:, dc, :],
                            hT_bf[:, dc, nn * 512:(nn + 1) * 512],
                            start=(dc == 0), stop=(dc == 15)))
                        tS.inc(tensor.matmul(
                            ps_U[jc % 2][:, nn, :], su_buf[jc % 2][:, dc, :],
                            hT_bf[:, dc, nn * 512:(nn + 1) * 512],
                            start=(dc == 0), stop=(dc == 15)))
            shmm[jc] = tS.n

            @block.scalar
            def _(scalar, jc=jc):
                scalar.wait_ge(tS.h, shmm[jc])
                if jc >= 1:
                    scalar.wait_ge(vS.h, shev[jc - 1])
                sS.inc(scalar.activation(
                    gs_scr[:, :], ps_G[jc % 2].rearrange("p a b -> p (a b)"),
                    AF.Silu))
            silu_d = sS.n

            @block.vector
            def _(vector, jc=jc, silu_d=silu_d):
                vector.wait_ge(vS.h, vS.n)
                vector.wait_ge(sS.h, silu_d)
                vector.drain()
                vector.tensor_copy(us_scr[:, :],
                                   ps_U[jc % 2].rearrange("p a b -> p (a b)"))
                vector.drain()
                vS.inc(vector.tensor_mul(HshT[:, jc, :], gs_scr[:, :],
                                         us_scr[:, :]))
            shev[jc] = vS.n
        shared_gu_done = shev[15]
        shared_gu_mm = shmm[15]

        @block.tensor
        def _(tensor):
            tensor.wait_ge(vS.h, shared_gu_done)
        esS7.close()

        if "hsh" in DBG:
            @block.sync
            def _(sync):
                sync.wait_ge(vS.h, shared_gu_done)
                stm.inc(sync.dma_start(
                    out=dbg_t["hsh"],
                    in_=HshT.rearrange("p a b -> p (a b)")), 16)

        esG = ExitStack()
        A_all = sb("A_all", [128, 64, E], BF16, esG)
        A_f32 = sb("A_f32", [128, 64, E], F32, esG)
        cs_sb = sb("cs_sb", [E, 64], F32, esG)
        cs2_sb = sb("cs2_sb", [E, 64], F32, esG)
        csraw = sb("csraw", [E, 64], F32, esG)
        excl_sb = sb("excl_sb", [E, 64], F32, esG)
        bases_bc = sb("bases_bc", [128, 64, E], F32, esG)
        lb_bc = sb("lb_bc", [128, 8, E], F32, esG)
        basesT = sb("basesT", [64, E], F32, esG)
        bases_flat = sb("bases_flat", [1, 64 * E], F32, esG)
        lb_flat = sb("lb_flat", [1, 8 * E], F32, esG)
        colsT = sb("colsT", [64, E], F32, esG)
        bm_row = sb("bm_row", [1, E], F32, esG)
        lcs = sb("lcs", [E, 8], F32, esG)
        lcs2 = sb("lcs2", [E, 8], F32, esG)
        lexcl = sb("lexcl", [E, 8], F32, esG)
        lbT = sb("lbT", [8, E], F32, esG)
        g_t0 = sb("g_t0", [128, E], F32, esG)
        g_t1 = sb("g_t1", [128, 2, E], F32, esG)

        @block.scalar
        def _(scalar):
            scalar.wait_ge(ccA.h, aag_done)
            dsc.inc(scalar.dma_start(
                out=A_all[:, :, :],
                in_=aag_out.rearrange("(n p) e -> p n e", p=128)), 16)
        aall_ld = dsc.n

        @block.vector
        def _(vector):
            vector.wait_ge(vS.h, vS.n)
            vector.wait_ge(dsc.h, aall_ld)
            vS.inc(vector.tensor_copy(
                A_f32.rearrange("p n e -> p (n e)"),
                A_all.rearrange("p n e -> p (n e)")))
        af32_done = vS.n

        if "aall" in DBG:
            @block.sync
            def _(sync):
                sync.wait_ge(vS.h, af32_done)
                stm.inc(sync.dma_start(
                    out=dbg_t["aall"],
                    in_=A_f32.rearrange("p n e -> p (n e)")), 16)

        esGp1 = ExitStack()
        ps_cs = esGp1.enter_context(nc.psum_tensor("ps_cs", [E, 64], F32))
        ps_tr = esGp1.enter_context(nc.psum_tensor("ps_tr", [64, E], F32))
        ps_bb = [esGp1.enter_context(nc.psum_tensor(f"ps_bb{i}", [128, E], F32))
                 for i in range(2)]
        ps_bm = esGp1.enter_context(nc.psum_tensor("ps_bm", [1, E], F32))

        @block.tensor
        def _(tensor):
            tensor.wait_ge(dsc.h, aall_ld)
            for j in range(64):
                tS.inc(tensor.matmul(ps_cs[:, j:j + 1], A_all[:, j, :],
                                     ones_cb[:, :], start=True, stop=True))
        cs_mm = tS.n

        @block.vector
        def _(vector):
            vector.wait_ge(vS.h, vS.n)
            V = vector
            V.wait_ge(tS.h, cs_mm)
            V.drain()
            V.tensor_copy(csraw[:, :], ps_cs[:, :])
            V.tensor_copy(cs_sb[:, :], ps_cs[:, :])
            V.drain()
            bufs = [cs_sb, cs2_sb]
            cur = 0
            sh = 1
            while sh < 64:
                a, b = bufs[cur], bufs[1 - cur]
                V.tensor_copy(b[:, 0:sh], a[:, 0:sh])
                V.tensor_add(b[:, sh:64], a[:, sh:64], a[:, 0:64 - sh])
                V.drain()
                cur = 1 - cur
                sh *= 2
            incl = bufs[cur]
            V.memset(excl_sb[:, 0:1], 0.0)
            vS.inc(V.tensor_copy(excl_sb[:, 1:64], incl[:, 0:63]))
        prefix_done = vS.n

        @block.tensor
        def _(tensor):
            tensor.wait_ge(vS.h, prefix_done)
            tS.inc(tensor.transpose(ps_tr[:, :], excl_sb[:, :], ident_f[0:E, 0:E]))
        tr1 = tS.n

        @block.vector
        def _(vector):
            vector.wait_ge(vS.h, vS.n)
            vector.wait_ge(tS.h, tr1)
            vS.inc(vector.tensor_copy(basesT[:, :], ps_tr[:, :]))
        basesT_d = vS.n

        @block.sync
        def _(sync):
            sync.wait_ge(vS.h, basesT_d)
            w0 = dl.n
            dl.inc(sync.dma_start(
                out=bscr.rearrange("o (n e) -> (o n) e", e=E),
                in_=basesT[:, :]), 16)
            sync.wait_ge(dl.h, w0 + 16)
            dl.inc(sync.dma_start(out=bases_flat[0:1, :], in_=bscr[:, :]), 16)
        bases_flat_d = dl.n

        @block.tensor
        def _(tensor):
            tensor.wait_ge(vS.h, basesT_d)
            tS.inc(tensor.transpose(ps_tr[:, :], csraw[:, :], ident_f[0:E, 0:E]))
        tr2 = tS.n

        @block.vector
        def _(vector):
            vector.wait_ge(vS.h, vS.n)
            vector.wait_ge(tS.h, tr2)
            vS.inc(vector.tensor_copy(colsT[:, :], ps_tr[:, :]))
        colsT_d = vS.n

        @block.tensor
        def _(tensor):
            tensor.wait_ge(vS.h, colsT_d)
            tS.inc(tensor.matmul(ps_bm[0:1, :], masktile[:, :], colsT[:, :],
                                 start=True, stop=True))
        bm_mm = tS.n

        @block.vector
        def _(vector):
            vector.wait_ge(vS.h, vS.n)
            vector.wait_ge(tS.h, bm_mm)
            vS.inc(vector.tensor_copy(bm_row[:, :], ps_bm[0:1, :]))
        bm_done = vS.n

        bb_ev = [0] * 64
        for j in range(64):
            @block.tensor
            def _(tensor, j=j):
                if j == 0:
                    tensor.wait_ge(dl.h, bases_flat_d)
                if j >= 2:
                    tensor.wait_ge(vS.h, bb_ev[j - 2])
                tS.inc(tensor.matmul(ps_bb[j % 2][:, :], ones_rf[:, :],
                                     bases_flat[0:1, j * E:(j + 1) * E],
                                     start=True, stop=True))
                smm_snap[("bb", j)] = tS.n

            @block.vector
            def _(vector, j=j):
                vector.wait_ge(vS.h, vS.n)
                vector.wait_ge(tS.h, smm_snap[("bb", j)])
                vS.inc(vector.tensor_copy(bases_bc[:, j, :],
                                          ps_bb[j % 2][:, :]))
                bb_ev[j] = vS.n

        @block.tensor
        def _(tensor):
            tensor.wait_ge(vS.h, topk_done)
            tensor.wait_ge(vS.h, bb_ev[63])
            for m in range(8):
                tS.inc(tensor.matmul(ps_cs[:, m:m + 1], A_loc[:, m, :],
                                     ones_cb[:, :], start=True, stop=True))
        lcs_mm = tS.n

        @block.vector
        def _(vector):
            vector.wait_ge(vS.h, vS.n)
            V = vector
            V.wait_ge(tS.h, lcs_mm)
            V.drain()
            V.tensor_copy(lcs[:, :], ps_cs[:, 0:8])
            V.drain()
            bufs = [lcs, lcs2]
            cur = 0
            sh = 1
            while sh < 8:
                a, b = bufs[cur], bufs[1 - cur]
                V.tensor_copy(b[:, 0:sh], a[:, 0:sh])
                V.tensor_add(b[:, sh:8], a[:, sh:8], a[:, 0:8 - sh])
                V.drain()
                cur = 1 - cur
                sh *= 2
            incl = bufs[cur]
            V.memset(lexcl[:, 0:1], 0.0)
            vS.inc(V.tensor_copy(lexcl[:, 1:8], incl[:, 0:7]))
        lpre_done = vS.n

        @block.tensor
        def _(tensor):
            tensor.wait_ge(vS.h, lpre_done)
            tS.inc(tensor.transpose(ps_tr[0:8, :], lexcl[:, 0:8],
                                    ident_f[0:E, 0:E]))
        ltr = tS.n

        @block.vector
        def _(vector):
            vector.wait_ge(vS.h, vS.n)
            vector.wait_ge(tS.h, ltr)
            vS.inc(vector.tensor_copy(lbT[:, :], ps_tr[0:8, :]))
        lbT_d = vS.n

        @block.sync
        def _(sync):
            sync.wait_ge(vS.h, lbT_d)
            w0 = dl.n
            dl.inc(sync.dma_start(
                out=lscr.rearrange("o (n e) -> (o n) e", e=E),
                in_=lbT[:, :]), 16)
            sync.wait_ge(dl.h, w0 + 16)
            dl.inc(sync.dma_start(out=lb_flat[0:1, :], in_=lscr[:, :]), 16)
        lb_flat_d = dl.n

        lb_ev = [0] * 8
        for m in range(8):
            @block.tensor
            def _(tensor, m=m):
                if m == 0:
                    tensor.wait_ge(dl.h, lb_flat_d)
                else:
                    tensor.wait_ge(vS.h, lb_ev[max(0, m - 2)])
                tensor.wait_ge(vS.h, bm_done)
                tS.inc(tensor.matmul(ps_bb[m % 2][:, :], ones_rf[:, :],
                                     lb_flat[0:1, m * E:(m + 1) * E],
                                     start=True, stop=False))
                tS.inc(tensor.matmul(ps_bb[m % 2][:, :], ones_rf[:, :],
                                     bm_row[0:1, :], start=False, stop=True))
                smm_snap[("lb", m)] = tS.n

            @block.vector
            def _(vector, m=m):
                vector.wait_ge(vS.h, vS.n)
                vector.wait_ge(tS.h, smm_snap[("lb", m)])
                vS.inc(vector.tensor_copy(lb_bc[:, m, :], ps_bb[m % 2][:, :]))
            lb_ev[m] = vS.n

        @block.tensor
        def _(tensor):
            tensor.wait_ge(vS.h, lb_ev[7])
        esGp1.close()

        esGp2 = ExitStack()
        ps_p = [esGp2.enter_context(nc.psum_tensor(f"ps_p{i}", [128, E], F32))
                for i in range(2)]
        ps_sl = [esGp2.enter_context(nc.psum_tensor(f"ps_sl{i}", [2, CP], F32))
                 for i in range(EPC)]
        ps_my = [esGp2.enter_context(
            nc.psum_tensor(f"ps_my{i}", [128, 128], F32)) for i in range(2)]
        p1aT = [sb(f"p1aT{i}", [E, 128], F32, esG) for i in range(2)]
        myrow = [sb(f"myrow{i}", [EPC, 128], F32, esG) for i in range(2)]
        mypos4 = sb("mypos4", [128, EPC], F32, esG)
        Ecmp = sb("Ecmp", [128, EPC, CP], BF16, esG)
        sl_sb = sb("sl_sb", [2, EPC * CP], F32, esG)
        trs_sb = sb("trs_sb", [128, 2], F32, esG)

        pm_ev = [0] * 8
        for m in range(8):
            @block.tensor
            def _(tensor, m=m):
                if m >= 2:
                    tensor.wait_ge(vS.h, pm_ev[m - 2])
                tS.inc(tensor.matmul(ps_p[m % 2][:, :], lstrict[:, :],
                                     A_loc[:, m, :], start=True, stop=True))
            smm_snap[("pm", m)] = tS.n

            @block.vector
            def _(vector, m=m):
                vector.wait_ge(vS.h, vS.n)
                vector.wait_ge(tS.h, smm_snap[("pm", m)])
                vector.tensor_copy(g_t0[:, :], ps_p[m % 2][:, :])
                vector.drain()
                vS.inc(vector.tensor_add(pos_mine[:, m, :], g_t0[:, :],
                                         lb_bc[:, m, :]))
            pm_ev[m] = vS.n
        pos_mine_d = pm_ev[7]

        if "posm" in DBG:
            @block.sync
            def _(sync):
                sync.wait_ge(vS.h, pos_mine_d)
                stm.inc(sync.dma_start(
                    out=dbg_t["posm"],
                    in_=pos_mine.rearrange("p a b -> p (a b)")), 16)

        @block.vector
        def _(vector):
            vector.wait_ge(vS.h, vS.n)
            vector.wait_ge(vS.h, pos_mine_d)
            for tt in range(8):
                V = vector
                V.drain()
                V.tensor_add(r_msk[:, :], pos_mine[:, tt, :], sbase_bc[:, :])
                V.drain()
                for kk in range(TOPK):
                    V.wait_ge(vS.h, vS.n)
                    V.tensor_scalar(r_eq[:, :], iplusb[:, 0:E],
                                    idxk_sb[:, tt, kk:kk + 1], BIGI,
                                    ALU.subtract, ALU.is_equal)
                    V.drain()
                    V.tensor_mul(r_tmp[:, :], r_eq[:, :], pos_mine[:, tt, :])
                    V.drain()
                    V.reduce_sum(r_col[:, 0:1], r_tmp[:, :], axis=AX.X)
                    V.drain()
                    V.tensor_mul(r_tmp[:, :], r_eq[:, :], r_msk[:, :])
                    V.drain()
                    V.reduce_sum(r_col[:, 3:4], r_tmp[:, :], axis=AX.X)
                    V.drain()
                    V.tensor_scalar(r_col[:, 1:2], r_col[:, 0:1], float(CCAP),
                                    None, ALU.is_lt)
                    V.drain()
                    V.tensor_tensor(weff_sb[:, tt, kk:kk + 1],
                                    wtop_sb[:, tt, kk:kk + 1], r_col[:, 1:2],
                                    ALU.mult)
                    V.tensor_mul(r_col[:, 3:4], r_col[:, 3:4], r_col[:, 1:2])
                    V.drain()
                    V.tensor_scalar(r_col[:, 3:4], r_col[:, 3:4],
                                    float(NCORE * EPC * CCAP - 1), 0.0,
                                    ALU.min, ALU.max)
                    V.drain()
                    vS.inc(V.tensor_copy(sidx_tok[:, tt, kk:kk + 1],
                                         r_col[:, 3:4]))
        tokp_done = vS.n

        if "weff" in DBG:
            @block.sync
            def _(sync):
                sync.wait_ge(vS.h, tokp_done)
                stm.inc(sync.dma_start(
                    out=dbg_t["weff"],
                    in_=weff_sb.rearrange("p a b -> p (a b)")), 16)
        if "stok" in DBG:
            @block.sync
            def _(sync):
                sync.wait_ge(vS.h, tokp_done)
                stm.inc(sync.dma_start(
                    out=dbg_t["stok"],
                    in_=sidx_tok.rearrange("p a b -> p (a b)")), 16)

        pg_ev = {}
        ec_snap = {}
        slmm = {}

        def p6_pos(j):
            @block.tensor
            def _(tensor):
                if j < 2:
                    tensor.wait_ge(vS.h, pos_mine_d)
                else:
                    tensor.wait_ge(vS.h, pg_ev[j - 2])
                tS.inc(tensor.matmul(ps_p[j % 2][:, :], lstrict[:, :],
                                     A_all[:, j, :], start=True, stop=True))
                smm_snap[("pg", j)] = tS.n

        def p6_vec(j):
            @block.vector
            def _(vector):
                V = vector
                V.wait_ge(vS.h, vS.n)
                V.wait_ge(tS.h, smm_snap[("pg", j)])
                V.drain()
                V.tensor_add(g_t0[:, :], bases_bc[:, j, :], ps_p[j % 2][:, :])
                V.drain()
                vS.inc(V.scalar_tensor_tensor(g_t1[:, j % 2, :], g_t0[:, :],
                                              1.0, A_f32[:, j, :], ALU.add,
                                              ALU.mult))
            smm_snap[("p1a", j)] = vS.n

            @block.tensor
            def _(tensor):
                tensor.wait_ge(vS.h, smm_snap[("p1a", j)])
                if j >= 2:
                    tensor.wait_ge(vS.h, smm_snap[("myev", j - 2)])
                tS.inc(tensor.transpose(ps_my[j % 2][0:E, :], g_t1[:, j % 2, :],
                                        ident_f[:, :]))
            smm_snap[("myt1", j)] = tS.n

            @block.vector
            def _(vector):
                vector.wait_ge(vS.h, vS.n)
                vector.wait_ge(tS.h, smm_snap[("myt1", j)])
                vector.drain()
                vS.inc(vector.tensor_copy(p1aT[j % 2][:, :],
                                      ps_my[j % 2][0:E, :]))
            smm_snap[("myt1e", j)] = vS.n

            @block.tensor
            def _(tensor):
                tensor.wait_ge(vS.h, smm_snap[("myt1e", j)])
                tS.inc(tensor.matmul(ps_my[j % 2][0:EPC, :], selq[:, :],
                                     p1aT[j % 2][:, :],
                                     start=True, stop=True))
            smm_snap[("mym", j)] = tS.n

            @block.vector
            def _(vector):
                vector.wait_ge(vS.h, vS.n)
                vector.wait_ge(tS.h, smm_snap[("mym", j)])
                vector.drain()
                vS.inc(vector.tensor_copy(myrow[j % 2][:, :],
                                      ps_my[j % 2][0:EPC, :]))
            smm_snap[("myme", j)] = vS.n

            @block.tensor
            def _(tensor):
                tensor.wait_ge(vS.h, smm_snap[("myme", j)])
                tS.inc(tensor.transpose(ps_my[j % 2][:, 0:EPC],
                                        myrow[j % 2][:, :],
                                        ident_f[0:EPC, 0:EPC]))
            smm_snap[("myt2", j)] = tS.n

            @block.vector
            def _(vector):
                V = vector
                V.wait_ge(vS.h, vS.n)
                V.wait_ge(tS.h, smm_snap[("myt2", j)])
                V.drain()
                V.tensor_copy(mypos4[:, :], ps_my[j % 2][:, 0:EPC])
                V.drain()
                if j >= 1:
                    V.wait_ge(tS.h, slmm[(j - 1, EPC - 1)])
                for i in range(EPC):
                    vS.inc(V.tensor_scalar(Ecmp[:, i, :], iotaP1[:, :],
                                           mypos4[:, i:i + 1], None,
                                           ALU.is_equal))
                    ec_snap[(j, i)] = vS.n
                pg_ev[j] = vS.n
            smm_snap[("myev", j)] = vS.n

        def p6_cmp(j):
            @block.tensor
            def _(tensor):
                for i in range(EPC):
                    tensor.wait_ge(vS.h, ec_snap[(j, i)])
                    tS.inc(tensor.matmul(ps_sl[i][:, :], hilo[:, 2 * j:2 * j + 2],
                                         Ecmp[:, i, :],
                                         start=(j == 0), stop=(j == 63)))
                    slmm[(j, i)] = tS.n

        for j in range(64):
            p6_pos(j)
            if j >= 1:
                p6_cmp(j - 1)
            p6_vec(j)
        p6_cmp(63)
        comp_done = slmm[(63, EPC - 1)]

        @block.vector
        def _(vector):
            vector.wait_ge(vS.h, vS.n)
            vector.wait_ge(tS.h, comp_done)
            for i in range(EPC):
                vS.inc(vector.tensor_copy(sl_sb[:, i * CP:(i + 1) * CP],
                                          ps_sl[i][:, :]))
        sl_ev = vS.n

        trs_snap = {}
        prev_trs = 0
        for i in range(EPC):
            for ch in range(3):
                @block.tensor
                def _(tensor, i=i, ch=ch, prev_trs=prev_trs):
                    if i == 0 and ch == 0:
                        tensor.wait_ge(vS.h, sl_ev)
                    else:
                        tensor.wait_ge(vS.h, prev_trs)
                    tS.inc(tensor.transpose(
                        ps_my[0][:, 0:2],
                        sl_sb[:, i * CP + ch * 128:i * CP + (ch + 1) * 128],
                        ident_f[0:2, 0:2]))
                    smm_snap[("trs", i, ch)] = tS.n

                @block.vector
                def _(vector, i=i, ch=ch):
                    vector.wait_ge(vS.h, vS.n)
                    V = vector
                    V.wait_ge(tS.h, smm_snap[("trs", i, ch)])
                    V.drain()
                    V.tensor_copy(trs_sb[:, :], ps_my[0][:, 0:2])
                    V.drain()
                    V.scalar_tensor_tensor(
                        trs_sb[:, 0:1], trs_sb[:, 0:1], 64.0, trs_sb[:, 1:2],
                        ALU.mult, ALU.add)
                    V.drain()
                    V.tensor_scalar(trs_sb[:, 0:1], trs_sb[:, 0:1],
                                    float(T - 1), 0.0, ALU.min, ALU.max)
                    V.drain()
                    vS.inc(V.tensor_copy(
                        sidx_exp[:, i * 3 + ch:i * 3 + ch + 1],
                        trs_sb[:, 0:1]))
                trs_snap[(i, ch)] = vS.n
                prev_trs = vS.n
        sidx_done = prev_trs

        if "sidx" in DBG:
            @block.sync
            def _(sync):
                sync.wait_ge(vS.h, sidx_done)
                stm.inc(sync.dma_start(out=dbg_t["sidx"], in_=sidx_exp[:, :]),
                        16)

        @block.tensor
        def _(tensor):
            tensor.wait_ge(vS.h, sidx_done)
            tensor.wait_ge(vS.h, tokp_done)
        esGp2.close()
        esG.close()
        esR.close()

        esX = ExitStack()
        xp_tok = sb("xp_tok", [128, 2, D], BF16, esX)
        wd_sb = a1[:, 0:16384].rearrange("p (a b) -> p a b", b=2048)
        gsxa = sb("gsxa", [128, 3, 1024], F32, esX)
        Hx = sb("Hx", [128, 3, 1024], BF16, esX)
        HTx = sb("HTx", [128, 8, 128], BF16, esX)
        ystage = [sb(f"ystage{i}", [128, D], BF16, esX) for i in range(2)]

        xg_snap = {}
        xpt_ev = {}
        kbatch_mm = {}
        wdld = {}
        gu_done = {}
        hx_done = {}
        silu_all = {}
        ymm = {}
        yev = {}
        yst = {}
        for e in range(EPC):
            esXt = ExitStack()
            ps_sm = [esXt.enter_context(
                nc.psum_tensor(f"ps_sm{e}_{i}", [128, 128], BF16))
                for i in range(2)]
            for ch in range(3):
                @block.gpsimd
                def _(gpsimd, e=e, ch=ch):
                    if e == 0 and ch == 0:
                        gpsimd.wait_ge(vS.h, sidx_done)
                        gpsimd.wait_ge(ccH.h, hag_done)
                    idx = 3 * e + ch
                    if idx >= 2:
                        pe, pch = divmod(idx - 2, 3)
                        gpsimd.wait_ge(vS.h, xpt_ev[(pe, pch, 15)])
                    gsl[(3 * e + ch) % 2].inc(gpsimd.indirect_dma_start(
                        out=xp_tok[:, (3 * e + ch) % 2, :], out_offset=None,
                        in_=hag_out[0:T, :],
                        in_offset=IndirectOffsetOnAxis(
                            sidx_exp[:, e * 3 + ch:e * 3 + ch + 1], 0)), 16)
                xg_snap[(e, ch)] = gsl[(3 * e + ch) % 2].n

                vbase = vS.n

                @block.tensor
                def _(tensor, e=e, ch=ch, vbase=vbase):
                    tensor.wait_ge(gsl[(3 * e + ch) % 2].h, xg_snap[(e, ch)])
                    if e == 0 and ch == 0:
                        tensor.wait_ge(tS.h, dense_last)
                    if e > 0 and ch == 0:
                        tensor.wait_ge(vS.h, yev[(e - 1, 2)])
                    for dc in range(16):
                        if dc < 2 and (3 * e + ch) >= 1:
                            pe2, pch2 = divmod(3 * e + ch - 1, 3)
                            tensor.wait_ge(vS.h, xpt_ev[(pe2, pch2, 14 + dc)])
                        if dc >= 2:
                            tensor.wait_ge(vS.h, vbase + dc - 1)
                        tS.inc(tensor.transpose(
                            ps_sm[dc % 2][:, :],
                            xp_tok[:, (3 * e + ch) % 2, dc * 128:(dc + 1) * 128],
                            ident_bf[:, :]))
                        smm_snap[("xt", e, ch, dc)] = tS.n

                @block.vector
                def _(vector, e=e, ch=ch):
                    vector.wait_ge(vS.h, vS.n)
                    for dc in range(16):
                        vector.wait_ge(tS.h, smm_snap[("xt", e, ch, dc)])
                        vS.inc(vector.tensor_copy(
                            XpT[:, dc, ch * 128:(ch + 1) * 128],
                            ps_sm[dc % 2][:, :]))
                        xpt_ev[(e, ch, dc)] = vS.n
                assert xpt_ev[(e, ch, 15)] == vbase + 16
            xpt_done = xpt_ev[(e, 2, 15)]
            esXt.close()

            @block.gpsimd
            def _(gpsimd, e=e):
                if e == 0:
                    gpsimd.wait_ge(tS.h, shared_gu_mm)
                else:
                    gpsimd.wait_ge(tS.h, ymm[(e - 1, 2)])
                wdv = wd[e].rearrange("(c p) d -> p c d", p=128)
                dB.inc(gpsimd.dma_start(out=wd_sb[:, 0:4, :],
                                        in_=wdv[:, 0:4, :]), 16)
                dB.inc(gpsimd.dma_start(out=wd_sb[:, 4:8, :],
                                        in_=wdv[:, 4:8, :]), 16)
            wdld[e] = dB.n

            esXp = ExitStack()
            ps_gu = [esXp.enter_context(
                nc.psum_tensor(f"ps_gu{e}_{st}", [128, 2, 512], F32))
                for st in range(3)]
            for pw, wsrc in ((0, wg), (1, wu)):
                for b4 in range(4):
                    k = (e * 2 + pw) * 4 + b4
                    if k >= 3:
                        @block.gpsimd
                        def _(gpsimd, e=e, pw=pw, b4=b4, k=k, wsrc=wsrc):
                            gpsimd.wait_ge(tS.h, kbatch_mm[k - 3])
                            dsl[4 + k % 3].inc(gpsimd.dma_start(
                                out=wgbuf[k % 3][:, :, :],
                                in_=wsrc[e].rearrange(
                                    "(c p) i -> p c i", p=128)[
                                    :, b4 * 4:b4 * 4 + 4, :]), 16)
                        guld[(e, pw, b4)] = dsl[4 + k % 3].n

                    @block.tensor
                    def _(tensor, e=e, pw=pw, b4=b4, k=k, xpt_done=xpt_done):
                        tensor.wait_ge(dsl[4 + k % 3].h, guld[(e, pw, b4)])
                        if b4 == 0:
                            tensor.wait_ge(vS.h, xpt_done)
                            if pw == 0 and e > 0:
                                tensor.wait_ge(vS.h, yev[(e - 1, 2)])
                            if pw == 1:
                                tensor.wait_ge(sS.h, silu_all[e])
                        for c4 in range(4):
                            dc = b4 * 4 + c4
                            for st in range(3):
                                sl = slice(st * 128, (st + 1) * 128)
                                for nn in range(2):
                                    tS.inc(tensor.matmul(
                                        ps_gu[st][:, nn, :], XpT[:, dc, sl],
                                        wgbuf[k % 3][:, c4, nn * 512:(nn + 1) * 512],
                                        start=(dc == 0), stop=(dc == 15)))
                    kbatch_mm[k] = tS.n
                    gu_done[(e, pw, b4)] = tS.n

                if pw == 0:
                    for st in range(3):
                        @block.scalar
                        def _(scalar, e=e, st=st):
                            scalar.wait_ge(tS.h, gu_done[(e, 0, 3)])
                            if e > 0 and st == 0:
                                scalar.wait_ge(vS.h, hx_done[(e - 1, 2)])
                            sS.inc(scalar.activation(
                                gsxa[:, st, :],
                                ps_gu[st].rearrange("p a b -> p (a b)"),
                                AF.Silu))
                    silu_all[e] = sS.n
                else:
                    for st in range(3):
                        @block.vector
                        def _(vector, e=e, st=st):
                            vector.wait_ge(vS.h, vS.n)
                            vector.wait_ge(tS.h, gu_done[(e, 1, 3)])
                            vector.wait_ge(sS.h, silu_all[e])
                            vector.drain()
                            vS.inc(vector.tensor_mul(
                                Hx[:, st, :], gsxa[:, st, :],
                                ps_gu[st].rearrange("p a b -> p (a b)")))
                        hx_done[(e, st)] = vS.n
            esXp.close()

            esXy = ExitStack()
            ps_eY = esXy.enter_context(
                nc.psum_tensor(f"ps_eY{e}", [128, 4, 512], F32))
            ps_sm = [esXy.enter_context(
                nc.psum_tensor(f"ps_smy{e}_{i}", [128, 128], BF16))
                for i in range(2)]
            for st in range(3):
                vbase = vS.n

                @block.tensor
                def _(tensor, e=e, st=st, vbase=vbase):
                    tensor.wait_ge(vS.h, hx_done[(e, st)])
                    if st == 0:
                        tensor.wait_ge(vS.h, hx_done[(e, 2)])
                        k = (e - 1, 2)
                        if k in yev:
                            tensor.wait_ge(vS.h, yev[k])
                    for ic in range(8):
                        if ic >= 2:
                            tensor.wait_ge(vS.h, vbase + ic - 1)
                        tS.inc(tensor.transpose(
                            ps_sm[ic % 2][:, :], Hx[:, st, ic * 128:(ic + 1) * 128],
                            ident_bf[:, :]))
                        smm_snap[("ht", e, st, ic)] = tS.n

                @block.vector
                def _(vector, e=e, st=st):
                    vector.wait_ge(vS.h, vS.n)
                    for ic in range(8):
                        vector.wait_ge(tS.h, smm_snap[("ht", e, st, ic)])
                        vS.inc(vector.tensor_copy(HTx[:, ic, :],
                                                  ps_sm[ic % 2][:, :]))
                ht_d = vS.n
                assert ht_d == vbase + 8

                @block.tensor
                def _(tensor, e=e, st=st, ht_d=ht_d):
                    tensor.wait_ge(vS.h, ht_d)
                    tensor.wait_ge(dB.h, wdld[e])
                    for ic in range(8):
                        for nn in range(4):
                            tS.inc(tensor.matmul(
                                ps_eY[:, nn, :], HTx[:, ic, :],
                                wd_sb[:, ic, nn * 512:(nn + 1) * 512],
                                start=(ic == 0), stop=(ic == 7)))
                ymm[(e, st)] = tS.n

                @block.vector
                def _(vector, e=e, st=st):
                    vector.wait_ge(vS.h, vS.n)
                    vector.wait_ge(tS.h, ymm[(e, st)])
                    k = ("st", e, st - 2) if st >= 2 else ("st", e - 1, st + 1)
                    if k in yst:
                        vector.wait_ge(tsl[(3 * (k[1]) + k[2]) % 2].h, yst[k])
                    vS.inc(vector.tensor_copy(
                        ystage[(3 * e + st) % 2][:, :],
                        ps_eY.rearrange("p a b -> p (a b)")))
                yev[(e, st)] = vS.n

                @block.sync
                def _(sync, e=e, st=st):
                    sync.wait_ge(vS.h, yev[(e, st)])
                    nrows = 128 if st < 2 else 64
                    tsl[(3 * e + st) % 2].inc(sync.dma_start(
                        out=yag_in[e * CCAP + st * 128:
                                   e * CCAP + st * 128 + nrows, :],
                        in_=ystage[(3 * e + st) % 2][0:nrows, :]), 16)
                yst[("st", e, st)] = tsl[(3 * e + st) % 2].n

            @block.tensor
            def _(tensor, e=e):
                tensor.wait_ge(vS.h, yev[(e, 2)])
            esXy.close()
        y_stores = (tsl[0].n, tsl[1].n)
        exp_gu_last = gu_done[(EPC - 1, 1, 3)]

        for i in range(EPC):
            @block.gpsimd
            def _(gpsimd, i=i):
                for st in range(3):
                    par = (3 * i + st) % 2
                    gpsimd.wait_ge(tsl[par].h, yst[("st", i, st)])
                ccY.inc(gpsimd.collective_compute(
                    "AllGather", ALU.bypass,
                    replica_groups=[list(range(NCORE))],
                    ins=[yag_in[i * CCAP:(i + 1) * CCAP, :].opt()],
                    outs=[yag_out[i * NCORE * CCAP:(i + 1) * NCORE * CCAP, :].opt()]))
        yag_done = ccY.n

        @block.tensor
        def _(tensor):
            tensor.wait_ge(vS.h, yev[(EPC - 1, 2)])
        esX.close()
        esXw.close()

        if "yag" in DBG:
            @block.sync
            def _(sync):
                sync.wait_ge(ccY.h, yag_done)
                stm.inc(sync.dma_start(out=dbg_t["yag"], in_=yag_out[:, :]), 16)

        esC = ExitStack()
        ygb = [sb(f"ygb{i}", [128, TOPK, D], BF16, esC) for i in range(2)]
        h_rel = [sb(f"h_rel{i}", [128, D], BF16, esC) for i in range(2)]
        dw_sb = [sb(f"dw{i}", [128, TOPK, 128], BF16, esC) for i in range(2)]
        o_scr = sb("o_scr", [128, D], F32, esC)
        o_scr2 = sb("o_scr2", [128, D], F32, esC)
        o_cols = sb("o_cols", [128, 8], F32, esC)
        out_st = [sb(f"out_st{i}", [128, D], F32, esC) for i in range(2)]
        ps_m = [esC.enter_context(nc.psum_tensor(f"ps_m{i}", [128, 4, 512], F32))
                for i in range(2)]

        @block.gpsimd
        def _(gpsimd):
            gpsimd.wait_ge(tS.h, exp_gu_last)
            gpsimd.wait_ge(tS.h, ymm[(EPC - 1, 2)])
            sdv = sdT.rearrange("(kc p) d -> p kc d", p=128)
            for i in range(4):
                dB.inc(gpsimd.dma_start(out=sdT_sb[:, 4 * i:4 * i + 4, :],
                                        in_=sdv[:, 4 * i:4 * i + 4, :]), 16)
        sdT_ld = dB.n

        ygg = [0] * 8
        dwb = [0] * 8
        hrl = [0] * 8
        cmm = [0] * 8
        oev = [0] * 8
        ost = [0] * 8
        for tt in range(8):
            @block.gpsimd
            def _(gpsimd, tt=tt):
                if tt == 0:
                    gpsimd.wait_ge(ccY.h, yag_done)
                    gpsimd.wait_ge(vS.h, tokp_done)
                if tt >= 2:
                    gpsimd.wait_ge(tS.h, cmm[tt - 2])
                for kk in range(TOPK):
                    gsl[tt % 2].inc(gpsimd.indirect_dma_start(
                        out=ygb[tt % 2][:, kk, :], out_offset=None,
                        in_=yag_out[0:NCORE * EPC * CCAP, :],
                        in_offset=IndirectOffsetOnAxis(
                            sidx_tok[:, tt, kk:kk + 1], 0)), 16)
            ygg[tt] = gsl[tt % 2].n

            @block.vector
            def _(vector, tt=tt):
                vector.wait_ge(vS.h, vS.n)
                if tt == 0:
                    vector.wait_ge(vS.h, tokp_done)
                    vector.wait_ge(ccY.h, yag_done)
                if tt >= 2:
                    vector.wait_ge(tS.h, cmm[tt - 2])
                for kk in range(TOPK):
                    vS.inc(vector.tensor_scalar(
                        dw_sb[tt % 2][:, kk, :], ident_bf[:, :],
                        weff_sb[:, tt, kk:kk + 1], None, ALU.mult))
            dwb[tt] = vS.n

            @block.scalar
            def _(scalar, tt=tt):
                if tt == 0:
                    scalar.wait_ge(ccY.h, yag_done)
                if tt >= 2:
                    scalar.wait_ge(vS.h, oev[tt - 2])
                ssl[tt % 2].inc(scalar.dma_start(
                    out=h_rel[tt % 2][:, :],
                    in_=hag_in[tt * 128:(tt + 1) * 128, :]), 16)
            hrl[tt] = ssl[tt % 2].n

            @block.tensor
            def _(tensor, tt=tt):
                if tt == 0:
                    tensor.wait_ge(dB.h, sdT_ld)
                    tensor.wait_ge(vS.h, shared_gu_done)
                    tensor.wait_ge(vS.h, yev[(EPC - 1, 2)])
                if tt >= 2:
                    tensor.wait_ge(vS.h, oev[tt - 2])
                for jc in range(16):
                    for nn in range(4):
                        tS.inc(tensor.matmul(
                            ps_m[tt % 2][:, nn, :],
                            HshT[:, jc, tt * 128:(tt + 1) * 128],
                            sdT_sb[:, jc, nn * 512:(nn + 1) * 512],
                            start=(jc == 0), stop=False))
                tensor.wait_ge(gsl[tt % 2].h, ygg[tt])
                tensor.wait_ge(vS.h, dwb[tt])
                for kk in range(TOPK):
                    for nn in range(4):
                        tS.inc(tensor.matmul(
                            ps_m[tt % 2][:, nn, :], dw_sb[tt % 2][:, kk, :],
                            ygb[tt % 2][:, kk, nn * 512:(nn + 1) * 512],
                            start=False, stop=(kk == TOPK - 1)))
            cmm[tt] = tS.n

            @block.vector
            def _(vector, tt=tt):
                vector.wait_ge(vS.h, vS.n)
                V = vector
                V.wait_ge(ssl[tt % 2].h, hrl[tt])
                if tt >= 2:
                    V.wait_ge(vS.h, oev[tt - 2])
                V.drain()
                V.tensor_copy(o_scr[:, :], h_rel[tt % 2][:, :])
                V.wait_ge(tS.h, cmm[tt])
                V.drain()
                vS.inc(V.tensor_add(o_scr2[:, :],
                                    ps_m[tt % 2].rearrange("p a b -> p (a b)"),
                                    o_scr[:, :]))
            resid2_done = vS.n

            @block.scalar
            def _(scalar, tt=tt, resid2_done=resid2_done):
                scalar.wait_ge(vS.h, resid2_done)
                scalar.wait_ge(sS.h, sS.n)
                sS.inc(scalar.activation(o_scr[:, :], o_scr2[:, :], AF.Copy,
                                         accum_out=o_cols[:, 0:1]))
                scalar.wait_ge(sS.h, sS.n)
                sS.inc(scalar.activation(o_scr[:, :], o_scr2[:, :], AF.Square,
                                         accum_out=o_cols[:, 2:3]))
            stats2 = sS.n

            @block.vector
            def _(vector, tt=tt, stats2=stats2):
                vector.wait_ge(vS.h, vS.n)
                V = vector
                V.wait_ge(sS.h, stats2)
                V.drain()
                V.tensor_scalar(o_cols[:, 1:2], o_cols[:, 0:1], 1.0 / D, None,
                                ALU.mult)
                V.drain()
                V.tensor_tensor(o_cols[:, 3:4], o_cols[:, 1:2], o_cols[:, 1:2],
                                ALU.mult)
                V.drain()
                V.tensor_scalar(o_cols[:, 4:5], o_cols[:, 2:3], 1.0 / D, None,
                                ALU.mult)
                V.drain()
                V.tensor_sub(o_cols[:, 5:6], o_cols[:, 4:5], o_cols[:, 3:4])
                V.drain()
                vS.inc(V.tensor_scalar(o_cols[:, 5:6], o_cols[:, 5:6], 1.0,
                                       LN_EPS, ALU.mult, ALU.add))
            var2 = vS.n

            @block.scalar
            def _(scalar, var2=var2):
                scalar.wait_ge(vS.h, var2)
                sS.inc(scalar.activation(o_cols[:, 6:7], o_cols[:, 5:6],
                                         AF.Sqrt))
            std2 = sS.n

            @block.vector
            def _(vector, tt=tt, std2=std2):
                vector.wait_ge(vS.h, vS.n)
                V = vector
                V.wait_ge(sS.h, std2)
                V.reciprocal(o_cols[:, 7:8], o_cols[:, 6:7])
                V.drain()
                if tt >= 2:
                    V.wait_ge(tsl[tt % 2].h, ost[tt - 2])
                vS.inc(V.tensor_scalar(out_st[tt % 2][:, :], o_scr2[:, :],
                                       o_cols[:, 1:2], o_cols[:, 7:8],
                                       ALU.subtract, ALU.mult))
            oev[tt] = vS.n

            if "moe" in DBG:
                @block.sync
                def _(sync, tt=tt):
                    sync.wait_ge(vS.h, oev[tt])
                    stm.inc(sync.dma_start(
                        out=dbg_t["moe"][tt * 128:(tt + 1) * 128, :],
                        in_=o_scr2[:, :]), 16)

            @block.sync
            def _(sync, tt=tt):
                sync.wait_ge(vS.h, oev[tt])
                tsl[tt % 2].inc(sync.dma_start(
                    out=out[tt * 128:(tt + 1) * 128, :],
                    in_=out_st[tt % 2][:, :]), 16)
            ost[tt] = tsl[tt % 2].n
        final_st = (tsl[0].n, tsl[1].n, stm.n)

        @block.sync
        def _(sync):
            sync.wait_ge(tsl[0].h, final_st[0])
            sync.wait_ge(tsl[1].h, final_st[1])
            sync.wait_ge(stm.h, final_st[2])

        @block.gpsimd
        def _(gpsimd):
            gpsimd.wait_ge(tsl[0].h, final_st[0])
            gpsimd.wait_ge(tsl[1].h, final_st[1])
            gpsimd.wait_ge(stm.h, final_st[2])
        esC.close()

    return nc


def _host_inputs(inputs):
    x = np.ascontiguousarray(np.asarray(inputs["L0_attn_x"], np.float32))
    w_q = np.asarray(inputs["L0_attn_w_q"], np.float32)
    w_kv = np.asarray(inputs["L0_attn_w_kv"], np.float32)
    w_dense = np.asarray(inputs["L0_attn_w_dense"], np.float32)
    router_w = np.asarray(inputs["L0_ffn_router_w"], np.float32)
    e_bias = np.asarray(inputs["L0_ffn_e_bias"], np.float32)
    W_gate = np.asarray(inputs["L0_ffn_W_gate"], np.float32)
    W_up = np.asarray(inputs["L0_ffn_W_up"], np.float32)
    W_down = np.asarray(inputs["L0_ffn_W_down"], np.float32)
    sg = np.asarray(inputs["L0_ffn_shared_gate"], np.float32)
    su = np.asarray(inputs["L0_ffn_shared_up"], np.float32)
    sd = np.asarray(inputs["L0_ffn_shared_down"], np.float32)

    ident = np.eye(128, dtype=np.float32)
    lstrict = (np.arange(128)[:, None] < np.arange(128)[None, :]).astype(BF)
    masks = np.zeros((4, 128, 512), dtype=BF)
    for m in range(4):
        masks[m] = (np.arange(128)[:, None] <=
                    (np.arange(512)[None, :] - 128 * m)).astype(BF)
    iota384 = np.arange(384, dtype=np.float32)[None, :]
    grpidx = (np.arange(E) // EPG).astype(np.float32)[None, :]
    hilo = np.zeros((128, 128), dtype=BF)
    for j in range(64):
        t = 128 * j + np.arange(128)
        hilo[:, 2 * j] = (t // 64).astype(BF)
        hilo[:, 2 * j + 1] = (t % 64).astype(BF)
    sgT = np.ascontiguousarray(sg.T.reshape(D, 16, 128).transpose(1, 0, 2))
    suT = np.ascontiguousarray(su.T.reshape(D, 16, 128).transpose(1, 0, 2))
    sdT = np.ascontiguousarray(sd.T)

    in_maps = []
    for c in range(NCORE):
        b, half = c // 2, c % 2
        wq_c = w_q[:, half * 1024:(half + 1) * 1024]
        wqh = np.ascontiguousarray(wq_c.reshape(D, NH, KVC).transpose(1, 0, 2))
        selrow = np.zeros((EPC, E), np.float32)
        for i in range(EPC):
            selrow[i, EPC * c + i] = 1.0
        masktile = (np.arange(64) < 8 * c).astype(np.float32)[:, None]
        ee = np.arange(E)
        sbase = ((ee % EPC) * (NCORE * CCAP) + (ee // EPC) * CCAP).astype(
            np.float32)[None, :]
        m = {
            "xT": np.ascontiguousarray(x[b].T),
            "x_own": np.ascontiguousarray(x[b, half * 1024:(half + 1) * 1024]),
            "wqh": wqh,
            "wk": np.ascontiguousarray(w_kv[:, 256 * half:256 * half + 256]),
            "wv": np.ascontiguousarray(
                w_kv[:, 512 + 256 * half:512 + 256 * half + 256]),
            "wdense": np.ascontiguousarray(
                w_dense[half * 1024:(half + 1) * 1024, :]),
            "routerT": np.ascontiguousarray(router_w.T),
            "wg": np.ascontiguousarray(W_gate[EPC * c:EPC * c + EPC]),
            "wu": np.ascontiguousarray(W_up[EPC * c:EPC * c + EPC]),
            "wd": np.ascontiguousarray(W_down[EPC * c:EPC * c + EPC]),
            "sgT": sgT, "suT": suT, "sdT": sdT,
            "ident_bf_i": ident.astype(BF), "ident_f_i": ident,
            "lstrict_i": lstrict, "masks_i": masks,
            "iota384_i": iota384, "grpidx_i": grpidx,
            "selrow_i": selrow,
            "ebias_i": e_bias[None, :].astype(np.float32),
            "hilo_i": hilo, "masktile_i": masktile, "sbase_i": sbase,
            "selq_i": np.ascontiguousarray(selrow.T),
        }
        in_maps.append(m)
    return in_maps


def kernel(**inputs):
    in_maps = _host_inputs(inputs)
    nc = build()
    res = run_bass_kernel_spmd(nc, in_maps, core_ids=list(range(NCORE)))
    y = np.zeros((B, SEQ, D), np.float32)
    for c in range(NCORE):
        b, half = c // 2, c % 2
        y[b, half * 1024:(half + 1) * 1024] = res.results[c]["out"]
    if DBG:
        kernel.dbg = [{k: np.asarray(v) for k, v in r.items()
                       if k.startswith("dbg_")} for r in res.results]
    return y


# version 11
# speedup vs baseline: 1.0638x; 1.0638x over previous
import numpy as np
import ml_dtypes
from contextlib import ExitStack

import concourse.bass as bass
import concourse.mybir as mybir
from concourse.bass import IndirectOffsetOnAxis
from concourse.bass_utils import run_bass_kernel_spmd

F32 = mybir.dt.float32
BF16 = mybir.dt.bfloat16
I32 = mybir.dt.int32
AF = mybir.ActivationFunctionType
ALU = mybir.AluOpType
AX = mybir.AxisListType
BF = ml_dtypes.bfloat16

D = 2048
SEQ = 2048
B = 4
NCORE = 8
TPC = 1024
T = 8192
NH = 8
KVC = 128
NGC = 2
E = 32
EPC = 4
TOPK = 4
CCAP = 320
CP = 384
NGRP = 8
EPG = 4
TKG = 3
SCALE_W = 2.5
SCALE_ATT = float(KVC) ** -0.5
BIGI = 1.0e4
BIGM = 1.0e6
LN_EPS = 1e-5

DBG = ()


class Sem:
    def __init__(self, es, nc, name):
        self.h = es.enter_context(nc.semaphore(name))
        self.n = 0

    def inc(self, instr, k=1):
        self.n += k
        instr.then_inc(self.h, k)
        return instr


def build():
    nc = bass.Bass("TRN2", target_bir_lowering=False, debug=False)

    def din(name, shape, dt=F32):
        return nc.dram_tensor(name, shape, dt, kind="ExternalInput").ap()

    xT = din("xT", [D, SEQ])
    x_own = din("x_own", [TPC, D])
    wqh = din("wqh", [NH, D, KVC])
    wk = din("wk", [D, NGC * KVC])
    wv = din("wv", [D, NGC * KVC])
    wdense = din("wdense", [NH * KVC, D])
    routerT = din("routerT", [D, E])
    wg = din("wg", [EPC, D, 1024])
    wu = din("wu", [EPC, D, 1024])
    wd = din("wd", [EPC, 1024, D])
    sgT = din("sgT", [16, D, 128])
    suT = din("suT", [16, D, 128])
    sdT = din("sdT", [D, D])
    ident_bf_i = din("ident_bf_i", [128, 128], BF16)
    ident_f_i = din("ident_f_i", [128, 128])
    lstrict_i = din("lstrict_i", [128, 128], BF16)
    masks_i = din("masks_i", [4, 128, 512], BF16)
    iota384_i = din("iota384_i", [1, 384])
    grpidx_i = din("grpidx_i", [1, E])
    ebias_i = din("ebias_i", [1, E])
    hiloown_i = din("hiloown_i", [128, 16], BF16)
    cmask8_i = din("cmask8_i", [8, 1])
    selhl_i = din("selhl_i", [128, 8])
    sbase_i = din("sbase_i", [1, E])

    out = nc.dram_tensor("out", [TPC, D], F32, kind="ExternalOutput").ap()

    dbg_t = {}
    dbg_specs = {
        "attn": ([TPC, D], BF16), "h1": ([TPC, D], BF16),
        "scores": ([128, 8 * 32], F32), "aall": ([128, 64 * 32], F32),
        "posm": ([128, 8 * 32], F32), "sidx": ([128, 12], I32),
        "weff": ([128, 8 * 4], F32), "stok": ([128, 8 * 4], I32),
        "yag": ([NCORE * EPC * CCAP, D], BF16), "moe": ([TPC, D], F32),
        "hsh": ([128, 16 * 1024], BF16), "ctxt": ([128, NH * SEQ], BF16),
        "qt": ([128, NH * SEQ], BF16), "kt": ([128, NGC * SEQ], BF16),
        "qt0": ([128, SEQ], BF16), "qt1": ([128, SEQ], BF16),
        "vt": ([128, 16 * 256], BF16), "negb": ([128, NH], F32),
        "es0": ([128, 16 * 512], BF16),
    }
    for name in DBG:
        shape, dt = dbg_specs[name]
        dbg_t[name] = nc.dram_tensor("dbg_" + name, shape, dt,
                                     kind="ExternalOutput").ap()

    rs_in = nc.dram_tensor("rs_in", [SEQ, D], BF16)
    rs_out = nc.dram_tensor("rs_out", [TPC, D], BF16)
    hag_in = nc.dram_tensor("hag_in", [TPC, D], BF16)
    hag_out = nc.dram_tensor("hag_out", [T, D], BF16)
    cnt_in = nc.dram_tensor("cnt_in", [1, E], F32)
    cnt_out = nc.dram_tensor("cnt_out", [NCORE, E], F32)
    hl_in = nc.dram_tensor("hl_in", [64, CP], F32)
    hl_out = nc.dram_tensor("hl_out", [NCORE * 64, CP], F32)
    yag_in = nc.dram_tensor("yag_in", [EPC * CCAP, D], BF16)
    lscr = nc.dram_tensor("lscr", [1, 8 * E], F32)
    yag_out = nc.dram_tensor("yag_out", [NCORE * EPC * CCAP, D], BF16)

    es = ExitStack()
    with es:
        def sb(name, shape, dt=F32, stk=es):
            return stk.enter_context(nc.sbuf_tensor(name, shape, dt))

        ident_bf = sb("ident_bf", [128, 128], BF16)
        ident_f = sb("ident_f", [128, 128])
        lstrict = sb("lstrict", [128, 128], BF16)
        masks = sb("masks", [128, 4, 512], BF16)
        iota_bc = sb("iota_bc", [128, 384])
        iplusb = sb("iplusb", [128, 384])
        iotaP1 = sb("iotaP1", [128, 384])
        grpidx_bc = sb("grpidx_bc", [128, E])
        ebias_bc = sb("ebias_bc", [128, E])
        hiloown = sb("hiloown", [128, 16], BF16)
        cmask8 = sb("cmask8", [8, 1])
        selhl = sb("selhl", [128, 8])
        sbase_bc = sb("sbase_bc", [128, E])
        ones_cb = sb("ones_cb", [128, 1], BF16)
        ones_cf = sb("ones_cf", [128, 1])
        ones_rf = sb("ones_rf", [1, 128])
        routerT_sb = sb("routerT_sb", [128, 16, E])
        scores_sb = sb("scores_sb", [128, 8, E])
        A_loc = sb("A_loc", [128, 8, E], BF16)
        idxk_sb = sb("idxk_sb", [128, 8, TOPK])
        wtop_sb = sb("wtop_sb", [128, 8, TOPK])
        weff_sb = sb("weff_sb", [128, 8, TOPK])
        sidx_tok = sb("sidx_tok", [128, 8, TOPK], I32)
        pos_mine = sb("pos_mine", [128, 8, E])
        sidx_exp = sb("sidx_exp", [128, 12], I32)
        arena1 = sb("arena1", [128, 32768], BF16)
        arena2 = sb("arena2", [128, 16384], BF16)
        a1 = arena1.ap()
        xT_sb = a1.rearrange("p (a b) -> p a b", b=2048)
        expS = [a1[:, i * 8192:(i + 1) * 8192].rearrange(
            "p (a b) -> p a b", b=512) for i in range(2)]
        ctxT_sb = a1[:, 16384:32768].rearrange("p (a b) -> p a b", b=2048)
        wdense_sb = a1[:, 0:16384].rearrange("p (a b) -> p a b", b=2048)
        hT_bf = a1[:, 0:16384].rearrange("p (a b) -> p a b", b=1024)
        XpT = a1[:, 16384:22528].rearrange("p (a b) -> p a b", b=CP)
        sdT_sb = a1.rearrange("p (a b) -> p a b", b=2048)
        a2 = arena2.ap()
        qT_sb = a2.rearrange("p (a b) -> p a b", b=2048)
        HshT = a2.rearrange("p (a b) -> p a b", b=1024)

        dl = Sem(es, nc, "dl")
        dsc = Sem(es, nc, "dsc")
        dg = Sem(es, nc, "dg")
        dgi = Sem(es, nc, "dgi")
        stm = Sem(es, nc, "stm")
        tS = Sem(es, nc, "tS")
        vS = Sem(es, nc, "vS")
        sS = Sem(es, nc, "sS")
        dB = Sem(es, nc, "dB")
        dsl = [Sem(es, nc, f"dsl{i}") for i in range(7)]
        ssl = [Sem(es, nc, f"ssl{i}") for i in range(2)]
        tsl = [Sem(es, nc, f"tsl{i}") for i in range(2)]
        gsl = [Sem(es, nc, f"gsl{i}") for i in range(2)]
        gS = Sem(es, nc, "gS")
        cSx = Sem(es, nc, "cSx")
        ccR = Sem(es, nc, "ccR")
        ccH = Sem(es, nc, "ccH")
        ccA = Sem(es, nc, "ccA")
        ccL = Sem(es, nc, "ccL")
        hss = [Sem(es, nc, f"hss{i}") for i in range(2)]
        ccY = Sem(es, nc, "ccY")

        block = es.enter_context(nc.Block())

        @block.sync
        def _(sync):
            dl.inc(sync.dma_start(out=ident_bf[:, :], in_=ident_bf_i), 16)
            dl.inc(sync.dma_start(out=ident_f[:, :], in_=ident_f_i), 16)
            dl.inc(sync.dma_start(out=lstrict[:, :], in_=lstrict_i), 16)
            dl.inc(sync.dma_start(out=masks[:, :, :],
                                  in_=masks_i.rearrange("m k q -> k m q")), 16)
            dl.inc(sync.dma_start(out=iota_bc[:, :],
                                  in_=iota384_i.partition_broadcast(128)), 16)
            dl.inc(sync.dma_start(out=grpidx_bc[:, :],
                                  in_=grpidx_i.partition_broadcast(128)), 16)
            dl.inc(sync.dma_start(out=ebias_bc[:, :],
                                  in_=ebias_i.partition_broadcast(128)), 16)
            dl.inc(sync.dma_start(out=hiloown[:, :], in_=hiloown_i), 16)
            dl.inc(sync.dma_start(out=cmask8[:, :], in_=cmask8_i), 16)
            dl.inc(sync.dma_start(out=selhl[:, :], in_=selhl_i), 16)
            dl.inc(sync.dma_start(out=sbase_bc[:, :],
                                  in_=sbase_i.partition_broadcast(128)), 16)
        p0_loads = dl.n

        @block.scalar
        def _(scalar):
            dsc.inc(scalar.dma_start(
                out=routerT_sb[:, :, :],
                in_=routerT.rearrange("(kc p) e -> p kc e", p=128)), 16)
        routerT_ld = dsc.n

        @block.vector
        def _(vector):
            vector.wait_ge(vS.h, vS.n)
            vector.memset(ones_cb[:, :], 1.0)
            vector.memset(ones_cf[:, :], 1.0)
            vector.memset(ones_rf[:, :], 1.0)
            vector.wait_ge(dl.h, p0_loads)
            vector.tensor_scalar(iplusb[:, :], iota_bc[:, :], BIGI,
                                 None, ALU.add)
            vS.inc(vector.tensor_scalar(iotaP1[:, :], iota_bc[:, :], 1.0,
                                        None, ALU.add))
        consts_done = vS.n

        esAq = ExitStack()
        kT_sb = sb("kT_sb", [128, NGC, SEQ], BF16, esAq)
        v_sb = sb("v_sb", [128, 16, NGC * KVC], BF16, esAq)
        sq_scr = sb("sq_scr", [128, SEQ], BF16, esAq)
        nrm_scr = sb("nrm_scr", [1, SEQ], F32, esAq)
        bnd_scr = sb("bnd_scr", [1, 20], F32, esAq)
        negb_col = sb("negb_col", [128, NH], F32, esAq)

        esA1 = ExitStack()
        wk_sb = sb("wk_sb", [128, 16, NGC * KVC], BF16, esA1)
        wv_sb = sb("wv_sb", [128, 16, NGC * KVC], BF16, esA1)
        wq_buf = sb("wq_buf", [128, 2, 16, KVC], BF16, esA1)

        @block.gpsimd
        def _(gpsimd):
            xTv = xT.rearrange("(kc p) t -> p kc t", p=128)
            for i in range(4):
                dB.inc(gpsimd.dma_start(out=xT_sb[:, 4 * i:4 * i + 4, :],
                                        in_=xTv[:, 4 * i:4 * i + 4, :]), 16)
            dB.inc(gpsimd.dma_start(
                out=wk_sb[:, :, :],
                in_=wk.rearrange("(kc p) e -> p kc e", p=128)), 16)
            dB.inc(gpsimd.dma_start(
                out=wv_sb[:, :, :],
                in_=wv.rearrange("(kc p) e -> p kc e", p=128)), 16)
        xT_loaded = dB.n

        esA1p = ExitStack()
        ps_q = [esA1p.enter_context(nc.psum_tensor(f"ps_q{i}", [128, 4, 512], F32))
                for i in range(2)]
        qmm_snap = [0] * (NH + NGC)
        qev_snap = [0] * (NH + NGC)
        wqld_snap = [0] * NH

        for h in range(NH):
            @block.gpsimd
            def _(gpsimd, h=h):
                if h >= 2:
                    gpsimd.wait_ge(tS.h, qmm_snap[h - 2])
                dsl[h % 2].inc(gpsimd.dma_start(
                    out=wq_buf[:, h % 2, :, :],
                    in_=wqh[h].rearrange("(kc p) c -> p kc c", p=128)), 16)
            wqld_snap[h] = dsl[h % 2].n

            @block.tensor
            def _(tensor, h=h):
                tensor.wait_ge(dB.h, xT_loaded)
                tensor.wait_ge(dsl[h % 2].h, wqld_snap[h])
                if h >= 2:
                    tensor.wait_ge(sS.h, qev_snap[h - 2])
                for kc in range(16):
                    for qc in range(4):
                        tS.inc(tensor.matmul(
                            ps_q[h % 2][:, qc, :], wq_buf[:, h % 2, kc, :],
                            xT_sb[:, kc, qc * 512:(qc + 1) * 512],
                            start=(kc == 0), stop=(kc == 15)))
            qmm_snap[h] = tS.n

            @block.scalar
            def _(scalar, h=h):
                scalar.wait_ge(tS.h, qmm_snap[h])
                sS.inc(scalar.copy(qT_sb[:, h, :],
                                   ps_q[h % 2].rearrange("p a b -> p (a b)")))
            qev_snap[h] = sS.n

        if "qt0" in DBG:
            @block.sync
            def _(sync):
                sync.wait_ge(sS.h, qev_snap[0])
                stm.inc(sync.dma_start(out=dbg_t["qt0"], in_=qT_sb[:, 0, :]), 16)

        for g in range(NGC):
            u = NH + g

            @block.tensor
            def _(tensor, g=g, u=u):
                tensor.wait_ge(sS.h, qev_snap[u - 2])
                for kc in range(16):
                    for qc in range(4):
                        tS.inc(tensor.matmul(
                            ps_q[u % 2][:, qc, :],
                            wk_sb[:, kc, g * KVC:(g + 1) * KVC],
                            xT_sb[:, kc, qc * 512:(qc + 1) * 512],
                            start=(kc == 0), stop=(kc == 15)))
            qmm_snap[u] = tS.n

            @block.scalar
            def _(scalar, g=g, u=u):
                scalar.wait_ge(tS.h, qmm_snap[u])
                sS.inc(scalar.copy(kT_sb[:, g, :],
                                   ps_q[u % 2].rearrange("p a b -> p (a b)")))
            qev_snap[u] = sS.n

        @block.tensor
        def _(tensor):
            tensor.wait_ge(sS.h, qev_snap[NH])
            tensor.wait_ge(sS.h, qev_snap[NH + 1])
        esA1p.close()

        esA1v = ExitStack()
        ps_v = [esA1v.enter_context(nc.psum_tensor(f"ps_v{i}", [128, NGC * KVC], F32))
                for i in range(2)]
        vmm_snap = [0] * 16
        vev_snap = [0] * 16
        for tt in range(16):
            @block.tensor
            def _(tensor, tt=tt):
                if tt >= 2:
                    tensor.wait_ge(sS.h, vev_snap[tt - 2])
                for kc in range(16):
                    tS.inc(tensor.matmul(
                        ps_v[tt % 2][:, :], xT_sb[:, kc, tt * 128:(tt + 1) * 128],
                        wv_sb[:, kc, :], start=(kc == 0), stop=(kc == 15)))
            vmm_snap[tt] = tS.n

            @block.scalar
            def _(scalar, tt=tt):
                scalar.wait_ge(tS.h, vmm_snap[tt])
                sS.inc(scalar.copy(v_sb[:, tt, :], ps_v[tt % 2][:, :]))
            vev_snap[tt] = sS.n
        a1_done = sS.n

        @block.tensor
        def _(tensor):
            tensor.wait_ge(sS.h, a1_done)
        esA1v.close()
        esA1.close()

        esB = ExitStack()
        ps_nb = esB.enter_context(nc.psum_tensor("ps_nb", [1, 4, 512], F32))
        ps_nc = esB.enter_context(nc.psum_tensor("ps_nc", [128, NH], F32))
        bnd_units = [("q", h) for h in range(NH)] + [("k", g) for g in range(NGC)]
        bsq_snap = [0] * 10
        bmm_snap = [0] * 10
        bred_snap = [0] * 10
        for ui, (kind, i) in enumerate(bnd_units):
            @block.scalar
            def _(scalar, ui=ui, kind=kind, i=i):
                scalar.wait_ge(sS.h, a1_done)
                if ui > 0:
                    scalar.wait_ge(tS.h, bmm_snap[ui - 1])
                src = qT_sb[:, i, :] if kind == "q" else kT_sb[:, i, :]
                sS.inc(scalar.activation(sq_scr[:, :], src, AF.Square))
            bsq_snap[ui] = sS.n

            @block.tensor
            def _(tensor, ui=ui):
                tensor.wait_ge(sS.h, bsq_snap[ui])
                if ui == 0:
                    tensor.wait_ge(vS.h, consts_done)
                if ui > 0:
                    tensor.wait_ge(vS.h, bred_snap[ui - 1])
                for qc in range(4):
                    tS.inc(tensor.matmul(
                        ps_nb[0:1, qc, :], ones_cb[:, :],
                        sq_scr[:, qc * 512:(qc + 1) * 512],
                        start=True, stop=True))
            bmm_snap[ui] = tS.n

            @block.vector
            def _(vector, ui=ui):
                vector.wait_ge(vS.h, vS.n)
                vector.wait_ge(tS.h, bmm_snap[ui])
                vector.tensor_copy(nrm_scr[:, :],
                                   ps_nb.rearrange("p a b -> p (a b)"))
                vector.drain()
                vS.inc(vector.reduce_max(bnd_scr[:, ui:ui + 1], nrm_scr[:, :],
                                         axis=AX.X))
            bred_snap[ui] = vS.n

        @block.scalar
        def _(scalar):
            scalar.wait_ge(vS.h, bred_snap[9])
            sS.inc(scalar.activation(bnd_scr[:, 0:10], bnd_scr[:, 0:10],
                                     AF.Sqrt))
        bnd_sqrt = sS.n

        @block.vector
        def _(vector):
            vector.wait_ge(vS.h, vS.n)
            vector.wait_ge(sS.h, bnd_sqrt)
            for h in range(NH):
                vector.tensor_tensor(bnd_scr[:, 11 + h:12 + h],
                                     bnd_scr[:, h:h + 1],
                                     bnd_scr[:, 8 + h // 4:9 + h // 4], ALU.mult)
            vector.drain()
            vS.inc(vector.tensor_scalar(bnd_scr[:, 11:11 + NH],
                                        bnd_scr[:, 11:11 + NH],
                                        -SCALE_ATT * 1.001, -1e-3,
                                        ALU.mult, ALU.add))
        negb_row = vS.n

        @block.tensor
        def _(tensor):
            tensor.wait_ge(vS.h, negb_row)
            for h in range(NH):
                tS.inc(tensor.matmul(ps_nc[:, h:h + 1], ones_rf[:, :],
                                     bnd_scr[0:1, 11 + h:12 + h],
                                     start=True, stop=True))
        negb_mm = tS.n

        @block.vector
        def _(vector):
            vector.wait_ge(vS.h, vS.n)
            vector.wait_ge(tS.h, negb_mm)
            vS.inc(vector.tensor_copy(negb_col[:, :], ps_nc[:, :]))
        negb_done = vS.n

        @block.tensor
        def _(tensor):
            tensor.wait_ge(vS.h, negb_done)
        esB.close()

        if "qt1" in DBG:
            @block.sync
            def _(sync):
                sync.wait_ge(sS.h, a1_done)
                stm.inc(sync.dma_start(out=dbg_t["qt1"], in_=qT_sb[:, 0, :]), 16)
        if "qt" in DBG:
            @block.sync
            def _(sync):
                sync.wait_ge(sS.h, a1_done)
                stm.inc(sync.dma_start(
                    out=dbg_t["qt"], in_=qT_sb.rearrange("p a b -> p (a b)")), 16)
        if "kt" in DBG:
            @block.sync
            def _(sync):
                sync.wait_ge(sS.h, a1_done)
                stm.inc(sync.dma_start(
                    out=dbg_t["kt"], in_=kT_sb.rearrange("p a b -> p (a b)")), 16)
        if "vt" in DBG:
            @block.sync
            def _(sync):
                sync.wait_ge(sS.h, a1_done)
                stm.inc(sync.dma_start(
                    out=dbg_t["vt"], in_=v_sb.rearrange("p a b -> p (a b)")), 16)
        if "negb" in DBG:
            @block.sync
            def _(sync):
                sync.wait_ge(vS.h, negb_done)
                stm.inc(sync.dma_start(out=dbg_t["negb"], in_=negb_col[:, :]), 16)

        esA2 = ExitStack()
        sum_row = [sb(f"sum_row{i}", [1, 512], F32, esA2) for i in range(2)]
        rec_row = [sb(f"rec_row{i}", [1, 512], F32, esA2) for i in range(2)]
        rb_sb = [sb(f"rb_sb{i}", [128, 512], F32, esA2) for i in range(2)]
        ps_S = [esA2.enter_context(nc.psum_tensor(f"ps_S{i}", [128, 512], F32))
                for i in range(2)]
        ps_sum = [esA2.enter_context(nc.psum_tensor(f"ps_sum{i}", [1, 512], F32))
                  for i in range(2)]
        ps_rb = [esA2.enter_context(nc.psum_tensor(f"ps_rb{i}", [128, 512], F32))
                 for i in range(2)]
        ps_ctx = [esA2.enter_context(nc.psum_tensor(f"ps_ctx{i}", [128, 512], F32))
                  for i in range(2)]

        chunks = [(h, qc) for h in range(NH) for qc in range(4)]
        NCH = len(chunks)
        tiles_of = []
        g0 = 0
        for (h, qc) in chunks:
            tiles_of.append(list(range(g0, g0 + 4 * qc + 4)))
            g0 += 4 * qc + 4
        exp_snap = {}
        smm_snap = {}
        maskd = [0] * NCH
        sumsd = [0] * NCH
        recipd = [0] * NCH
        rbmmd = [0] * NCH
        rbevd = [0] * NCH
        ctxmmd = [0] * NCH
        ctxevd = [0] * NCH

        def a2_S(ci):
            h, qc = chunks[ci]
            sbase = sS.n
            t0c = tiles_of[ci][0]

            @block.tensor
            def _(tensor):
                if ci == 0:
                    tensor.wait_ge(vS.h, consts_done)
                g = h // 4
                for j, tg in enumerate(tiles_of[ci]):
                    if tg >= 2:
                        tg2 = tg - 2
                        snap = (exp_snap[tg2] if tg2 < t0c
                                else sbase + (tg2 - t0c + 1))
                        tensor.wait_ge(sS.h, snap)
                    tS.inc(tensor.matmul(
                        ps_S[tg % 2][:, :],
                        kT_sb[:, g, j * 128:(j + 1) * 128],
                        qT_sb[:, h, qc * 512:(qc + 1) * 512],
                        start=True, stop=True))
                    smm_snap[tg] = tS.n

        def a2_exp(ci):
            h, qc = chunks[ci]

            @block.scalar
            def _(scalar):
                if ci >= 2:
                    scalar.wait_ge(tS.h, ctxmmd[ci - 2])
                for j, tg in enumerate(tiles_of[ci]):
                    scalar.wait_ge(tS.h, smm_snap[tg])
                    sS.inc(scalar.activation(
                        expS[ci % 2][:, j, :], ps_S[tg % 2][:, :], AF.Exp,
                        bias=negb_col[:, h:h + 1], scale=SCALE_ATT))
                    exp_snap[tg] = sS.n

        def a2_mask(ci):
            h, qc = chunks[ci]

            @block.vector
            def _(vector):
                vector.wait_ge(vS.h, vS.n)
                vector.wait_ge(sS.h, exp_snap[tiles_of[ci][-1]])
                for m in range(4):
                    vS.inc(vector.tensor_mul(expS[ci % 2][:, 4 * qc + m, :],
                                             expS[ci % 2][:, 4 * qc + m, :],
                                             masks[:, m, :]))
            maskd[ci] = vS.n

        def a2_sums(ci):
            h, qc = chunks[ci]
            nkt = 4 * qc + 4

            @block.tensor
            def _(tensor):
                tensor.wait_ge(vS.h, maskd[ci])
                if ci >= 2:
                    tensor.wait_ge(vS.h, recipd[ci - 2])
                for j in range(nkt):
                    tS.inc(tensor.matmul(
                        ps_sum[ci % 2][0:1, :], ones_cb[:, :],
                        expS[ci % 2][:, j, :],
                        start=(j == 0), stop=(j == nkt - 1)))
            sumsd[ci] = tS.n

        def a2_recip(ci):
            @block.vector
            def _(vector):
                vector.wait_ge(vS.h, vS.n)
                vector.wait_ge(tS.h, sumsd[ci])
                vector.drain()
                vS.inc(vector.tensor_copy(sum_row[ci % 2][:, :],
                                          ps_sum[ci % 2][0:1, :]))
            recipd[ci] = vS.n

        def a2_rbctx(ci):
            h, qc = chunks[ci]
            nkt = 4 * qc + 4
            g = h // 4

            @block.tensor
            def _(tensor):
                tensor.wait_ge(vS.h, recipd[ci])
                if ci >= 2:
                    tensor.wait_ge(vS.h, rbevd[ci - 2])
                    tensor.wait_ge(vS.h, ctxevd[ci - 2])
                tS.inc(tensor.matmul(ps_rb[ci % 2][:, :], ones_rf[:, :],
                                     sum_row[ci % 2][0:1, :],
                                     start=True, stop=True))
                rbmmd[ci] = tS.n
                for j in range(nkt):
                    tS.inc(tensor.matmul(
                        ps_ctx[ci % 2][:, :],
                        v_sb[:, j, g * KVC:(g + 1) * KVC],
                        expS[ci % 2][:, j, :],
                        start=(j == 0), stop=(j == nkt - 1)))
                ctxmmd[ci] = tS.n

        def a2_evac(ci):
            h, qc = chunks[ci]

            @block.vector
            def _(vector):
                vector.wait_ge(vS.h, vS.n)
                vector.wait_ge(tS.h, rbmmd[ci])
                vector.drain()
                vS.inc(vector.reciprocal(rb_sb[ci % 2][:, :],
                                         ps_rb[ci % 2][:, :]))
                rbevd[ci] = vS.n
                vector.wait_ge(tS.h, ctxmmd[ci])
                vector.wait_ge(vS.h, vS.n)
                vector.drain()
                vS.inc(vector.tensor_mul(
                    ctxT_sb[:, h, qc * 512:(qc + 1) * 512],
                    ps_ctx[ci % 2][:, :], rb_sb[ci % 2][:, :]))
                ctxevd[ci] = vS.n

        for ci in range(NCH):
            a2_S(ci)
            a2_exp(ci)
            a2_mask(ci)
            if ci == 0 and "es0" in DBG:
                @block.sync
                def _(sync):
                    sync.wait_ge(vS.h, maskd[0])
                    stm.inc(sync.dma_start(
                        out=dbg_t["es0"],
                        in_=expS[0].rearrange("p a b -> p (a b)")), 16)
            if ci >= 1:
                a2_sums(ci - 1)
                a2_recip(ci - 1)
                a2_rbctx(ci - 1)
                a2_evac(ci - 1)
        a2_sums(NCH - 1)
        a2_recip(NCH - 1)
        a2_rbctx(NCH - 1)
        a2_evac(NCH - 1)
        a2_done = ctxevd[NCH - 1]
        a2_last_mm = ctxmmd[NCH - 1]

        if "ctxt" in DBG:
            @block.sync
            def _(sync):
                sync.wait_ge(vS.h, a2_done)
                stm.inc(sync.dma_start(
                    out=dbg_t["ctxt"],
                    in_=ctxT_sb.rearrange("p h s -> p (h s)")), 16)

        esA2.close()
        esD = ExitStack()
        dstage = [sb(f"dstage{i}", [128, D], BF16, esD) for i in range(2)]
        ps_d = [esD.enter_context(nc.psum_tensor(f"ps_d{i}", [128, 4, 512], F32))
                for i in range(2)]

        @block.gpsimd
        def _(gpsimd):
            gpsimd.wait_ge(tS.h, a2_last_mm)
            dB.inc(gpsimd.dma_start(
                out=wdense_sb[:, :, :],
                in_=wdense.rearrange("(h p) d -> p h d", p=128)), 16)
        wdense_ld = dB.n

        dmm_snap = [0] * 16
        dev_snap = [0] * 16
        dst_snap = [0] * 16
        for qt in range(16):
            @block.tensor
            def _(tensor, qt=qt):
                if qt == 0:
                    tensor.wait_ge(dB.h, wdense_ld)
                    tensor.wait_ge(vS.h, a2_done)
                if qt >= 2:
                    tensor.wait_ge(vS.h, dev_snap[qt - 2])
                for ec in range(NH):
                    for nn in range(4):
                        tS.inc(tensor.matmul(
                            ps_d[qt % 2][:, nn, :],
                            ctxT_sb[:, ec, qt * 128:(qt + 1) * 128],
                            wdense_sb[:, ec, nn * 512:(nn + 1) * 512],
                            start=(ec == 0), stop=(ec == NH - 1)))
            dmm_snap[qt] = tS.n

            @block.vector
            def _(vector, qt=qt):
                vector.wait_ge(vS.h, vS.n)
                vector.wait_ge(tS.h, dmm_snap[qt])
                if qt >= 2:
                    vector.wait_ge(tsl[qt % 2].h, dst_snap[qt - 2])
                vS.inc(vector.tensor_copy(
                    dstage[qt % 2][:, :],
                    ps_d[qt % 2].rearrange("p a b -> p (a b)")))
            dev_snap[qt] = vS.n

            @block.sync
            def _(sync, qt=qt):
                sync.wait_ge(vS.h, dev_snap[qt])
                tsl[qt % 2].inc(sync.dma_start(
                    out=rs_in[qt * 128:(qt + 1) * 128, :],
                    in_=dstage[qt % 2][:, :]), 16)
            dst_snap[qt] = tsl[qt % 2].n
        dense_last = dmm_snap[15]
        dense_stores = (tsl[0].n, tsl[1].n)

        @block.gpsimd
        def _(gpsimd):
            gpsimd.wait_ge(tsl[0].h, dense_stores[0])
            gpsimd.wait_ge(tsl[1].h, dense_stores[1])
            ccR.inc(gpsimd.collective_compute(
                "ReduceScatter", ALU.add,
                replica_groups=[[0, 1], [2, 3], [4, 5], [6, 7]],
                ins=[rs_in.ap().opt()], outs=[rs_out.ap().opt()]))
        rs_done = ccR.n

        @block.tensor
        def _(tensor):
            tensor.wait_ge(vS.h, dev_snap[15])
        esD.close()
        esAq.close()

        if "attn" in DBG:
            @block.sync
            def _(sync):
                sync.wait_ge(ccR.h, rs_done)
                stm.inc(sync.dma_start(out=dbg_t["attn"], in_=rs_out[:, :]), 16)

        esL = ExitStack()
        xo_sb = [sb(f"xo{i}", [128, D], F32, esL) for i in range(2)]
        at_sb = [sb(f"at{i}", [128, D], BF16, esL) for i in range(2)]
        h_st = [sb(f"h_st{i}", [128, D], F32, esL) for i in range(2)]
        ln_scr = sb("ln_scr", [128, D], F32, esL)
        g_scr = sb("g_scr", [128, D], F32, esL)
        ln_cols = sb("ln_cols", [128, 8], F32, esL)
        htmp = sb("htmp", [128, 16, 128], F32, esL)
        ps_t1 = [esL.enter_context(nc.psum_tensor(f"ps_t1{i}", [128, 128], F32))
                 for i in range(2)]
        ps_r = esL.enter_context(nc.psum_tensor("ps_r", [128, E], F32))

        ld_snap = [0] * 8
        sig_snap = [0] * 8
        hdone = [0] * 8
        hdma = [0] * 8
        trN = [0] * 8
        htc = [0] * 8
        rmm = [0] * 8
        for tt in range(8):
            @block.scalar
            def _(scalar, tt=tt):
                if tt == 0:
                    scalar.wait_ge(ccR.h, rs_done)
                if tt >= 2:
                    scalar.wait_ge(vS.h, hdone[tt - 2])
                ssl[tt % 2].inc(scalar.dma_start(
                    out=xo_sb[tt % 2][:, :],
                    in_=x_own[tt * 128:(tt + 1) * 128, :]), 16)
                ssl[tt % 2].inc(scalar.dma_start(
                    out=at_sb[tt % 2][:, :],
                    in_=rs_out[tt * 128:(tt + 1) * 128, :]), 16)
            ld_snap[tt] = ssl[tt % 2].n

            @block.vector
            def _(vector, tt=tt):
                vector.wait_ge(vS.h, vS.n)
                V = vector
                V.wait_ge(ssl[tt % 2].h, ld_snap[tt])
                if tt >= 2:
                    V.wait_ge(dsl[2 + tt % 2].h, hdma[tt - 2])
                    V.wait_ge(tS.h, smm_snap[("t1", tt - 2, 15)])
                hs = h_st[tt % 2]
                V.drain()
                V.tensor_copy(g_scr[:, :], at_sb[tt % 2][:, :])
                V.drain()
                vS.inc(V.tensor_add(hs[:, :], g_scr[:, :],
                                    xo_sb[tt % 2][:, :]))
            resid_done = vS.n

            @block.scalar
            def _(scalar, tt=tt, resid_done=resid_done):
                scalar.wait_ge(vS.h, resid_done)
                scalar.wait_ge(sS.h, sS.n)
                hs = h_st[tt % 2]
                sS.inc(scalar.activation(ln_scr[:, :], hs[:, :], AF.Copy,
                                         accum_out=ln_cols[:, 0:1]))
                scalar.wait_ge(sS.h, sS.n)
                sS.inc(scalar.activation(ln_scr[:, :], hs[:, :], AF.Square,
                                         accum_out=ln_cols[:, 2:3]))
            stats_done = sS.n

            @block.vector
            def _(vector, tt=tt, stats_done=stats_done):
                vector.wait_ge(vS.h, vS.n)
                V = vector
                V.wait_ge(sS.h, stats_done)
                V.drain()
                V.tensor_scalar(ln_cols[:, 1:2], ln_cols[:, 0:1], 1.0 / D,
                                None, ALU.mult)
                V.drain()
                V.tensor_tensor(ln_cols[:, 3:4], ln_cols[:, 1:2],
                                ln_cols[:, 1:2], ALU.mult)
                V.drain()
                V.tensor_scalar(ln_cols[:, 4:5], ln_cols[:, 2:3], 1.0 / D,
                                None, ALU.mult)
                V.drain()
                V.tensor_sub(ln_cols[:, 5:6], ln_cols[:, 4:5], ln_cols[:, 3:4])
                V.drain()
                vS.inc(V.tensor_scalar(ln_cols[:, 5:6], ln_cols[:, 5:6], 1.0,
                                       LN_EPS, ALU.mult, ALU.add))
            var_done = vS.n

            @block.scalar
            def _(scalar, var_done=var_done):
                scalar.wait_ge(vS.h, var_done)
                sS.inc(scalar.activation(ln_cols[:, 6:7], ln_cols[:, 5:6],
                                         AF.Sqrt))
            std_done = sS.n

            @block.vector
            def _(vector, tt=tt, std_done=std_done):
                vector.wait_ge(vS.h, vS.n)
                V = vector
                V.wait_ge(sS.h, std_done)
                V.reciprocal(ln_cols[:, 7:8], ln_cols[:, 6:7])
                V.drain()
                vS.inc(V.tensor_scalar(h_st[tt % 2][:, :], h_st[tt % 2][:, :],
                                       ln_cols[:, 1:2], ln_cols[:, 7:8],
                                       ALU.subtract, ALU.mult))
            hdone[tt] = vS.n

            @block.gpsimd
            def _(gpsimd, tt=tt):
                gpsimd.wait_ge(vS.h, hdone[tt])
                dsl[2 + tt % 2].inc(gpsimd.dma_start(
                    out=hag_in[tt * 128:(tt + 1) * 128, :],
                    in_=h_st[tt % 2][:, :]), 16)
            hdma[tt] = dsl[2 + tt % 2].n

            sbase2 = sS.n

            @block.tensor
            def _(tensor, tt=tt, sbase2=sbase2):
                tensor.wait_ge(vS.h, hdone[tt])
                if tt == 0:
                    tensor.wait_ge(tS.h, dense_last)
                if tt >= 1:
                    tensor.wait_ge(sS.h, trN[tt - 1])
                for dc in range(16):
                    if dc >= 2:
                        tensor.wait_ge(sS.h, sbase2 + dc - 1)
                    tS.inc(tensor.transpose(
                        ps_t1[dc % 2][:, :],
                        h_st[tt % 2][:, dc * 128:(dc + 1) * 128], ident_f[:, :]))
                    smm_snap[("t1", tt, dc)] = tS.n

            @block.scalar
            def _(scalar, tt=tt):
                for dc in range(16):
                    scalar.wait_ge(tS.h, smm_snap[("t1", tt, dc)])
                    sS.inc(scalar.copy(htmp[:, dc, :], ps_t1[dc % 2][:, :]))
            trN[tt] = sS.n
            assert trN[tt] == sbase2 + 16

            @block.tensor
            def _(tensor, tt=tt):
                tensor.wait_ge(sS.h, trN[tt])
                if tt == 0:
                    tensor.wait_ge(dsc.h, routerT_ld)
                if tt >= 1:
                    tensor.wait_ge(sS.h, sig_snap[tt - 1])
                for dc in range(16):
                    tS.inc(tensor.matmul(
                        ps_r[:, :], htmp[:, dc, :], routerT_sb[:, dc, :],
                        start=(dc == 0), stop=(dc == 15)))
            rmm[tt] = tS.n

            @block.vector
            def _(vector, tt=tt):
                vector.wait_ge(vS.h, vS.n)
                vector.wait_ge(sS.h, trN[tt])
                vS.inc(vector.tensor_copy(hT_bf[:, :, tt * 128:(tt + 1) * 128],
                                          htmp[:, :, :]))
            htc[tt] = vS.n

            @block.scalar
            def _(scalar, tt=tt):
                scalar.wait_ge(tS.h, rmm[tt])
                sS.inc(scalar.activation(scores_sb[:, tt, :], ps_r[:, :],
                                         AF.Sigmoid))
            sig_snap[tt] = sS.n
        scores_done = sS.n
        h_all_dma = (dsl[2].n, dsl[3].n)
        ht_all = htc[7]
        p4_ts_all = tS.n

        @block.gpsimd
        def _(gpsimd):
            gpsimd.wait_ge(dsl[2].h, h_all_dma[0])
            gpsimd.wait_ge(dsl[3].h, h_all_dma[1])
            ccH.inc(gpsimd.collective_compute(
                "AllGather", ALU.bypass, replica_groups=[list(range(NCORE))],
                ins=[hag_in.ap().opt()], outs=[hag_out.ap().opt()]))
        hag_done = ccH.n

        if "h1" in DBG:
            @block.sync
            def _(sync):
                sync.wait_ge(ccH.h, hag_done)
                stm.inc(sync.dma_start(out=dbg_t["h1"], in_=hag_in[:, :]), 16)

        @block.tensor
        def _(tensor):
            tensor.wait_ge(sS.h, scores_done)
        @block.vector
        def _(vector):
            vector.wait_ge(vS.h, vS.n)
            vector.wait_ge(sS.h, scores_done)
        esL.close()

        guld = {}
        esXw = ExitStack()
        wgbuf = [sb(f"wgbuf{i}", [128, 4, 1024], BF16, esXw) for i in range(3)]

        @block.gpsimd
        def _(gpsimd):
            for k in range(3):
                e, r0 = divmod(k, 8)
                pw, b4 = divmod(r0, 4)
                wsrc = [wg, wu][pw]
                dsl[4 + k % 3].inc(gpsimd.dma_start(
                    out=wgbuf[k % 3][:, :, :],
                    in_=wsrc[e].rearrange("(c p) i -> p c i", p=128)[
                        :, b4 * 4:b4 * 4 + 4, :]), 16)
                guld[(e, pw, b4)] = dsl[4 + k % 3].n

        esR = ExitStack()
        r_sc = sb("r_sc", [128, E], F32, esR)
        r_m1 = sb("r_m1", [128, NGRP], F32, esR)
        r_b32 = sb("r_b32", [128, E], F32, esR)
        r_eq = sb("r_eq", [128, E], F32, esR)
        r_cand = sb("r_cand", [128, E], F32, esR)
        r_m2 = sb("r_m2", [128, NGRP], F32, esR)
        r_gsc = sb("r_gsc", [128, NGRP], F32, esR)
        r_col = sb("r_col", [128, 8], F32, esR)
        r_eq8 = sb("r_eq8", [128, NGRP], F32, esR)
        r_smask = sb("r_smask", [128, E], F32, esR)
        r_msk = sb("r_msk", [128, E], F32, esR)
        r_acc = sb("r_acc", [128, E], F32, esR)
        r_tmp = sb("r_tmp", [128, E], F32, esR)

        @block.vector
        def _(vector):
            vector.wait_ge(vS.h, vS.n)
            vector.wait_ge(sS.h, scores_done)
            vector.wait_ge(dsl[2].h, h_all_dma[0])
            vector.wait_ge(dsl[3].h, h_all_dma[1])
            for tt in range(8):
                V = vector
                V.drain()
                V.tensor_add(r_sc[:, :], scores_sb[:, tt, :], ebias_bc[:, :])
                V.drain()
                V.reduce_max(r_m1[:, :],
                             r_sc.rearrange("p (g e) -> p g e", e=EPG),
                             axis=AX.X)
                V.drain()
                for j in range(EPG):
                    V.tensor_copy(
                        r_b32.rearrange("p (g e) -> p g e", e=EPG)[:, :, j],
                        r_m1[:, :])
                V.drain()
                V.tensor_tensor(r_eq[:, :], r_sc[:, :], r_b32[:, :],
                                ALU.is_equal)
                V.drain()
                V.scalar_tensor_tensor(r_cand[:, :], r_eq[:, :], -BIGI,
                                       iplusb[:, 0:E], ALU.mult, ALU.add)
                V.drain()
                V.tensor_reduce(r_m2[:, :],
                                r_cand.rearrange("p (g e) -> p g e", e=EPG),
                                axis=AX.X, op=ALU.min)
                V.drain()
                for j in range(EPG):
                    V.tensor_copy(
                        r_b32.rearrange("p (g e) -> p g e", e=EPG)[:, :, j],
                        r_m2[:, :])
                V.drain()
                V.tensor_tensor(r_eq[:, :], iota_bc[:, 0:E], r_b32[:, :],
                                ALU.is_equal)
                V.drain()
                V.scalar_tensor_tensor(r_cand[:, :], r_eq[:, :], -BIGM,
                                       r_sc[:, :], ALU.mult, ALU.add)
                V.drain()
                V.reduce_max(r_m2[:, :],
                             r_cand.rearrange("p (g e) -> p g e", e=EPG),
                             axis=AX.X)
                V.drain()
                V.tensor_add(r_gsc[:, :], r_m1[:, :], r_m2[:, :])
                V.drain()
                V.memset(r_smask[:, :], 0.0)
                for it in range(TKG):
                    V.reduce_max(r_col[:, 0:1], r_gsc[:, :], axis=AX.X)
                    V.drain()
                    V.tensor_scalar(r_eq8[:, :], r_gsc[:, :], r_col[:, 0:1],
                                    None, ALU.is_equal)
                    V.drain()
                    V.scalar_tensor_tensor(r_eq8[:, :], r_eq8[:, :], -BIGI,
                                           iplusb[:, 0:NGRP], ALU.mult,
                                           ALU.add)
                    V.drain()
                    V.tensor_reduce(r_col[:, 1:2], r_eq8[:, :], axis=AX.X,
                                    op=ALU.min)
                    V.drain()
                    V.tensor_scalar(r_eq8[:, :], iplusb[:, 0:NGRP],
                                    r_col[:, 1:2], BIGI, ALU.subtract,
                                    ALU.is_equal)
                    V.drain()
                    V.scalar_tensor_tensor(r_gsc[:, :], r_eq8[:, :], -BIGM,
                                           r_gsc[:, :], ALU.mult, ALU.add)
                    V.tensor_scalar(r_tmp[:, :], grpidx_bc[:, :],
                                    r_col[:, 1:2], None, ALU.is_equal)
                    V.drain()
                    V.tensor_add(r_smask[:, :], r_smask[:, :], r_tmp[:, :])
                    V.drain()
                V.tensor_mul(r_msk[:, :], r_sc[:, :], r_smask[:, :])
                V.drain()
                V.memset(r_acc[:, :], 0.0)
                for kk in range(TOPK):
                    V.reduce_max(r_col[:, 2:3], r_msk[:, :], axis=AX.X)
                    V.drain()
                    V.tensor_scalar(r_eq[:, :], r_msk[:, :], r_col[:, 2:3],
                                    None, ALU.is_equal)
                    V.drain()
                    V.scalar_tensor_tensor(r_cand[:, :], r_eq[:, :], -BIGI,
                                           iplusb[:, 0:E], ALU.mult, ALU.add)
                    V.drain()
                    V.tensor_reduce(idxk_sb[:, tt, kk:kk + 1], r_cand[:, :],
                                    axis=AX.X, op=ALU.min)
                    V.drain()
                    V.tensor_scalar(r_eq[:, :], iplusb[:, 0:E],
                                    idxk_sb[:, tt, kk:kk + 1], BIGI,
                                    ALU.subtract, ALU.is_equal)
                    V.drain()
                    V.tensor_mul(r_tmp[:, :], r_eq[:, :], scores_sb[:, tt, :])
                    V.drain()
                    V.reduce_sum(wtop_sb[:, tt, kk:kk + 1], r_tmp[:, :],
                                 axis=AX.X)
                    V.tensor_add(r_acc[:, :], r_acc[:, :], r_eq[:, :])
                    V.drain()
                    V.scalar_tensor_tensor(r_msk[:, :], r_eq[:, :], -BIGM,
                                           r_msk[:, :], ALU.mult, ALU.add)
                    V.drain()
                V.reduce_sum(r_col[:, 3:4], wtop_sb[:, tt, :], axis=AX.X)
                V.drain()
                V.tensor_scalar(r_col[:, 3:4], r_col[:, 3:4], 1.0, 1e-20,
                                ALU.mult, ALU.add)
                V.drain()
                V.reciprocal(r_col[:, 4:5], r_col[:, 3:4])
                V.drain()
                V.tensor_scalar(wtop_sb[:, tt, :], wtop_sb[:, tt, :],
                                r_col[:, 4:5], SCALE_W, ALU.mult, ALU.mult)
                vS.inc(V.tensor_copy(A_loc[:, tt, :], r_acc[:, :]))
        topk_done = vS.n

        if "scores" in DBG:
            @block.sync
            def _(sync):
                sync.wait_ge(vS.h, topk_done)
                stm.inc(sync.dma_start(
                    out=dbg_t["scores"],
                    in_=scores_sb.rearrange("p a b -> p (a b)")), 16)

        esS7 = ExitStack()
        sg_buf = [sb(f"sg_buf{i}", [128, 16, 128], BF16, esS7) for i in range(2)]
        su_buf = [sb(f"su_buf{i}", [128, 16, 128], BF16, esS7) for i in range(2)]
        gs_scr = sb("gs_scr", [128, TPC], F32, esS7)
        us_scr = sb("us_scr", [128, TPC], F32, esS7)
        ps_G = [esS7.enter_context(nc.psum_tensor(f"ps_G{i}", [128, 2, 512], F32))
                for i in range(2)]
        ps_U = [esS7.enter_context(nc.psum_tensor(f"ps_U{i}", [128, 2, 512], F32))
                for i in range(2)]
        sgld = [0] * 16
        shmm = [0] * 16
        shev = [0] * 16

        def emit_sgld(jc):
            @block.gpsimd
            def _(gpsimd, jc=jc):
                if jc >= 2:
                    gpsimd.wait_ge(tS.h, shmm[jc - 2])
                else:
                    gpsimd.wait_ge(tS.h, p4_ts_all)
                dsl[jc % 2].inc(gpsimd.dma_start(
                    out=sg_buf[jc % 2][:, :, :],
                    in_=sgT[jc].rearrange("(kc p) j -> p kc j", p=128)), 16)
                dsl[jc % 2].inc(gpsimd.dma_start(
                    out=su_buf[jc % 2][:, :, :],
                    in_=suT[jc].rearrange("(kc p) j -> p kc j", p=128)), 16)
            sgld[jc] = dsl[jc % 2].n

        emit_sgld(0)
        emit_sgld(1)


        for jc in range(16):
            if jc >= 2:
                emit_sgld(jc)

            @block.tensor
            def _(tensor, jc=jc):
                tensor.wait_ge(dsl[jc % 2].h, sgld[jc])
                if jc == 0:
                    tensor.wait_ge(vS.h, ht_all)
                    tensor.wait_ge(tS.h, a2_last_mm)
                if jc >= 2:
                    tensor.wait_ge(vS.h, shev[jc - 2])
                for dc in range(16):
                    for nn in range(2):
                        tS.inc(tensor.matmul(
                            ps_G[jc % 2][:, nn, :], sg_buf[jc % 2][:, dc, :],
                            hT_bf[:, dc, nn * 512:(nn + 1) * 512],
                            start=(dc == 0), stop=(dc == 15)))
                        tS.inc(tensor.matmul(
                            ps_U[jc % 2][:, nn, :], su_buf[jc % 2][:, dc, :],
                            hT_bf[:, dc, nn * 512:(nn + 1) * 512],
                            start=(dc == 0), stop=(dc == 15)))
            shmm[jc] = tS.n

            @block.scalar
            def _(scalar, jc=jc):
                scalar.wait_ge(tS.h, shmm[jc])
                if jc >= 1:
                    scalar.wait_ge(vS.h, shev[jc - 1])
                sS.inc(scalar.activation(
                    gs_scr[:, :], ps_G[jc % 2].rearrange("p a b -> p (a b)"),
                    AF.Silu))
            silu_d = sS.n

            @block.vector
            def _(vector, jc=jc, silu_d=silu_d):
                vector.wait_ge(vS.h, vS.n)
                vector.wait_ge(sS.h, silu_d)
                vector.drain()
                vector.tensor_copy(us_scr[:, :],
                                   ps_U[jc % 2].rearrange("p a b -> p (a b)"))
                vector.drain()
                vS.inc(vector.tensor_mul(HshT[:, jc, :], gs_scr[:, :],
                                         us_scr[:, :]))
            shev[jc] = vS.n
        shared_gu_done = shev[15]
        shared_gu_mm = shmm[15]

        @block.tensor
        def _(tensor):
            tensor.wait_ge(vS.h, shared_gu_done)
        esS7.close()

        if "hsh" in DBG:
            @block.sync
            def _(sync):
                sync.wait_ge(vS.h, shared_gu_done)
                stm.inc(sync.dma_start(
                    out=dbg_t["hsh"],
                    in_=HshT.rearrange("p a b -> p (a b)")), 16)

        esG = ExitStack()
        lb_bc = sb("lb_bc", [128, 8, E], F32, esG)
        lb_flat = sb("lb_flat", [1, 8 * E], F32, esG)
        bm_row = sb("bm_row", [1, E], F32, esG)
        lcs = sb("lcs", [E, 8], F32, esG)
        lcs2 = sb("lcs2", [E, 8], F32, esG)
        lexcl = sb("lexcl", [E, 8], F32, esG)
        lbT = sb("lbT", [8, E], F32, esG)
        cnt_row = sb("cnt_row", [1, E], F32, esG)
        cnt_all = sb("cnt_all", [8, E], F32, esG)
        g_t0 = sb("g_t0", [128, E], F32, esG)
        A_locf = sb("A_locf", [128, 8, E], F32, esG)
        g_sel = sb("g_sel", [128, 8, E], F32, esG)
        hl_tmp = [sb(f"hl_tmp{i}", [2, EPC, CP], F32, esG) for i in range(2)]
        hlch = sb("hlch", [128, 4, CP], F32, esG)

        esGp1 = ExitStack()
        ps_cs = esGp1.enter_context(nc.psum_tensor("ps_cs", [E, 64], F32))
        ps_tr = esGp1.enter_context(nc.psum_tensor("ps_tr", [64, E], F32))
        ps_bb = [esGp1.enter_context(nc.psum_tensor(f"ps_bb{i}", [128, E], F32))
                 for i in range(2)]
        ps_bm = esGp1.enter_context(nc.psum_tensor("ps_bm", [1, E], F32))
        ps_cr = esGp1.enter_context(nc.psum_tensor("ps_cr", [1, E], F32))

        @block.tensor
        def _(tensor):
            tensor.wait_ge(vS.h, topk_done)
            for m in range(8):
                tS.inc(tensor.matmul(ps_cs[:, m:m + 1], A_loc[:, m, :],
                                     ones_cb[:, :], start=True, stop=True))
        lcs_mm = tS.n

        @block.vector
        def _(vector):
            vector.wait_ge(vS.h, vS.n)
            V = vector
            V.wait_ge(tS.h, lcs_mm)
            V.drain()
            V.tensor_copy(lcs[:, :], ps_cs[:, 0:8])
            V.drain()
            bufs = [lcs, lcs2]
            cur = 0
            sh = 1
            while sh < 8:
                a, b = bufs[cur], bufs[1 - cur]
                V.tensor_copy(b[:, 0:sh], a[:, 0:sh])
                V.tensor_add(b[:, sh:8], a[:, sh:8], a[:, 0:8 - sh])
                V.drain()
                cur = 1 - cur
                sh *= 2
            incl = bufs[cur]
            V.memset(lexcl[:, 0:1], 0.0)
            V.tensor_copy(lexcl[:, 1:8], incl[:, 0:7])
            vS.inc(V.tensor_copy(lcs[:, 0:1], incl[:, 7:8]))
        lpre_done = vS.n

        @block.tensor
        def _(tensor):
            tensor.wait_ge(vS.h, lpre_done)
            tS.inc(tensor.transpose(ps_tr[0:8, :], lexcl[:, 0:8],
                                    ident_f[0:E, 0:E]))
            tS.inc(tensor.transpose(ps_cr[0:1, :], lcs[:, 0:1],
                                    ident_f[0:E, 0:E]))
        ltr = tS.n

        @block.vector
        def _(vector):
            vector.wait_ge(vS.h, vS.n)
            vector.wait_ge(tS.h, ltr)
            vector.tensor_copy(lbT[:, :], ps_tr[0:8, :])
            vS.inc(vector.tensor_copy(cnt_row[:, :], ps_cr[0:1, :]))
        lbT_d = vS.n

        @block.sync
        def _(sync):
            sync.wait_ge(vS.h, lbT_d)
            stm.inc(sync.dma_start(out=cnt_in[:, :], in_=cnt_row[:, :]), 16)
            w0 = dl.n
            dl.inc(sync.dma_start(
                out=lscr.rearrange("o (n e) -> (o n) e", e=E),
                in_=lbT[:, :]), 16)
            sync.wait_ge(dl.h, w0 + 16)
            dl.inc(sync.dma_start(out=lb_flat[0:1, :], in_=lscr[:, :]), 16)
        cnt_store = stm.n
        lb_flat_d = dl.n

        @block.gpsimd
        def _(gpsimd):
            gpsimd.wait_ge(stm.h, cnt_store)
            ccA.inc(gpsimd.collective_compute(
                "AllGather", ALU.bypass, replica_groups=[list(range(NCORE))],
                ins=[cnt_in.ap().opt()], outs=[cnt_out.ap().opt()]))
        cnt_ag = ccA.n

        @block.scalar
        def _(scalar):
            scalar.wait_ge(ccA.h, cnt_ag)
            dsc.inc(scalar.dma_start(out=cnt_all[:, :], in_=cnt_out[:, :]), 16)
        cnt_ld = dsc.n

        @block.tensor
        def _(tensor):
            tensor.wait_ge(dsc.h, cnt_ld)
            tS.inc(tensor.matmul(ps_bm[0:1, :], cmask8[:, :], cnt_all[:, :],
                                 start=True, stop=True))
        bm_mm = tS.n

        @block.vector
        def _(vector):
            vector.wait_ge(vS.h, vS.n)
            vector.wait_ge(tS.h, bm_mm)
            vS.inc(vector.tensor_copy(bm_row[:, :], ps_bm[0:1, :]))
        bm_done = vS.n

        lb_ev = [0] * 8
        for m in range(8):
            @block.tensor
            def _(tensor, m=m):
                if m == 0:
                    tensor.wait_ge(dl.h, lb_flat_d)
                else:
                    tensor.wait_ge(vS.h, lb_ev[max(0, m - 2)])
                tensor.wait_ge(vS.h, bm_done)
                tS.inc(tensor.matmul(ps_bb[m % 2][:, :], ones_rf[:, :],
                                     lb_flat[0:1, m * E:(m + 1) * E],
                                     start=True, stop=False))
                tS.inc(tensor.matmul(ps_bb[m % 2][:, :], ones_rf[:, :],
                                     bm_row[0:1, :], start=False, stop=True))
                smm_snap[("lb", m)] = tS.n

            @block.vector
            def _(vector, m=m):
                vector.wait_ge(vS.h, vS.n)
                vector.wait_ge(tS.h, smm_snap[("lb", m)])
                vS.inc(vector.tensor_copy(lb_bc[:, m, :], ps_bb[m % 2][:, :]))
            lb_ev[m] = vS.n

        @block.tensor
        def _(tensor):
            tensor.wait_ge(vS.h, lb_ev[7])
        esGp1.close()

        esGp2 = ExitStack()
        ps_p = [esGp2.enter_context(nc.psum_tensor(f"ps_p{i}", [128, E], F32))
                for i in range(2)]
        ps_sl = [esGp2.enter_context(nc.psum_tensor(f"ps_sl{i}", [2, CP], F32))
                 for i in range(EPC)]
        ps_my = esGp2.enter_context(nc.psum_tensor("ps_my", [128, 128], F32))
        Ecmp = sb("Ecmp", [128, 2, EPC, CP], BF16, esG)
        sl_sb = sb("sl_sb", [2, EPC * CP], F32, esG)
        trs_sb = sb("trs_sb", [128, 2], F32, esG)

        pm_ev = [0] * 8
        for m in range(8):
            @block.tensor
            def _(tensor, m=m):
                if m >= 2:
                    tensor.wait_ge(vS.h, pm_ev[m - 2])
                tS.inc(tensor.matmul(ps_p[m % 2][:, :], lstrict[:, :],
                                     A_loc[:, m, :], start=True, stop=True))
            smm_snap[("pm", m)] = tS.n

            @block.vector
            def _(vector, m=m):
                vector.wait_ge(vS.h, vS.n)
                vector.wait_ge(tS.h, smm_snap[("pm", m)])
                vector.tensor_copy(g_t0[:, :], ps_p[m % 2][:, :])
                vector.drain()
                vS.inc(vector.tensor_add(pos_mine[:, m, :], g_t0[:, :],
                                         lb_bc[:, m, :]))
            pm_ev[m] = vS.n
        pos_mine_d = pm_ev[7]

        if "posm" in DBG:
            @block.sync
            def _(sync):
                sync.wait_ge(vS.h, pos_mine_d)
                stm.inc(sync.dma_start(
                    out=dbg_t["posm"],
                    in_=pos_mine.rearrange("p a b -> p (a b)")), 16)

        @block.vector
        def _(vector):
            vector.wait_ge(vS.h, vS.n)
            vector.wait_ge(vS.h, pos_mine_d)
            for tt in range(8):
                V = vector
                V.drain()
                V.tensor_add(r_msk[:, :], pos_mine[:, tt, :], sbase_bc[:, :])
                V.drain()
                for kk in range(TOPK):
                    V.wait_ge(vS.h, vS.n)
                    V.tensor_scalar(r_eq[:, :], iplusb[:, 0:E],
                                    idxk_sb[:, tt, kk:kk + 1], BIGI,
                                    ALU.subtract, ALU.is_equal)
                    V.drain()
                    V.tensor_mul(r_tmp[:, :], r_eq[:, :], pos_mine[:, tt, :])
                    V.drain()
                    V.reduce_sum(r_col[:, 0:1], r_tmp[:, :], axis=AX.X)
                    V.drain()
                    V.tensor_mul(r_tmp[:, :], r_eq[:, :], r_msk[:, :])
                    V.drain()
                    V.reduce_sum(r_col[:, 3:4], r_tmp[:, :], axis=AX.X)
                    V.drain()
                    V.tensor_scalar(r_col[:, 1:2], r_col[:, 0:1], float(CCAP),
                                    None, ALU.is_lt)
                    V.drain()
                    V.tensor_tensor(weff_sb[:, tt, kk:kk + 1],
                                    wtop_sb[:, tt, kk:kk + 1], r_col[:, 1:2],
                                    ALU.mult)
                    V.tensor_mul(r_col[:, 3:4], r_col[:, 3:4], r_col[:, 1:2])
                    V.drain()
                    V.tensor_scalar(r_col[:, 3:4], r_col[:, 3:4],
                                    float(NCORE * EPC * CCAP - 1), 0.0,
                                    ALU.min, ALU.max)
                    V.drain()
                    vS.inc(V.tensor_copy(sidx_tok[:, tt, kk:kk + 1],
                                         r_col[:, 3:4]))
        tokp_done = vS.n

        if "weff" in DBG:
            @block.sync
            def _(sync):
                sync.wait_ge(vS.h, tokp_done)
                stm.inc(sync.dma_start(
                    out=dbg_t["weff"],
                    in_=weff_sb.rearrange("p a b -> p (a b)")), 16)
        if "stok" in DBG:
            @block.sync
            def _(sync):
                sync.wait_ge(vS.h, tokp_done)
                stm.inc(sync.dma_start(
                    out=dbg_t["stok"],
                    in_=sidx_tok.rearrange("p a b -> p (a b)")), 16)

        @block.vector
        def _(vector):
            vector.wait_ge(vS.h, vS.n)
            vector.wait_ge(vS.h, pos_mine_d)
            vector.tensor_copy(A_locf.rearrange("p a b -> p (a b)"),
                               A_loc.rearrange("p a b -> p (a b)"))
            vector.drain()
            for m in range(8):
                vS.inc(vector.scalar_tensor_tensor(
                    g_sel[:, m, :], pos_mine[:, m, :], 1.0, A_locf[:, m, :],
                    ALU.add, ALU.mult))
        gsel_done = vS.n

        ecs = {}
        hlmm = {}
        hlev = [0] * 8
        hlst = [0] * 8
        for g in range(8):
            for m in range(8):
                @block.vector
                def _(vector, g=g, m=m):
                    vector.wait_ge(vS.h, vS.n)
                    if g == 0 and m == 0:
                        vector.wait_ge(vS.h, gsel_done)
                    km = g * 8 + m
                    if km >= 2:
                        vector.wait_ge(tS.h, hlmm[km - 2])
                    for i in range(EPC):
                        vS.inc(vector.tensor_scalar(
                            Ecmp[:, m % 2, i, :], iotaP1[:, :],
                            g_sel[:, m, 4 * g + i:4 * g + i + 1], None,
                            ALU.is_equal))
                ecs[(g, m)] = vS.n

                @block.tensor
                def _(tensor, g=g, m=m):
                    tensor.wait_ge(vS.h, ecs[(g, m)])
                    if m == 0 and g > 0:
                        tensor.wait_ge(vS.h, hlev[g - 1])
                    for i in range(EPC):
                        tS.inc(tensor.matmul(
                            ps_sl[i][:, :], hiloown[:, 2 * m:2 * m + 2],
                            Ecmp[:, m % 2, i, :],
                            start=(m == 0), stop=(m == 7)))
                hlmm[g * 8 + m] = tS.n

            @block.vector
            def _(vector, g=g):
                vector.wait_ge(vS.h, vS.n)
                vector.wait_ge(tS.h, hlmm[g * 8 + 7])
                if g >= 2:
                    vector.wait_ge(hss[g % 2].h, hlst[g - 2])
                for i in range(EPC):
                    vS.inc(vector.tensor_copy(hl_tmp[g % 2][:, i, :],
                                              ps_sl[i][:, :]))
            hlev[g] = vS.n

            @block.sync
            def _(sync, g=g):
                sync.wait_ge(vS.h, hlev[g])
                hss[g % 2].inc(sync.dma_start(
                    out=hl_in[8 * g:8 * g + 8, :].rearrange(
                        "(i h) s -> h i s", h=2),
                    in_=hl_tmp[g % 2][:, :, :]), 16)
            hlst[g] = hss[g % 2].n
        hl_store = (hss[0].n, hss[1].n)

        @block.gpsimd
        def _(gpsimd):
            gpsimd.wait_ge(hss[0].h, hl_store[0])
            gpsimd.wait_ge(hss[1].h, hl_store[1])
            ccL.inc(gpsimd.collective_compute(
                "AllGather", ALU.bypass, replica_groups=[list(range(NCORE))],
                ins=[hl_in.ap().opt()], outs=[hl_out.ap().opt()]))
        hl_ag = ccL.n

        @block.scalar
        def _(scalar):
            scalar.wait_ge(ccL.h, hl_ag)
            dsc.inc(scalar.dma_start(
                out=hlch[:, :, :],
                in_=hl_out.rearrange("(k p) s -> p k s", p=128)), 16)
        hl_ld = dsc.n

        @block.tensor
        def _(tensor):
            tensor.wait_ge(dsc.h, hl_ld)
            for i in range(EPC):
                for k in range(4):
                    tS.inc(tensor.matmul(ps_sl[i][:, :], selhl[:, 2 * i:2 * i + 2],
                                         hlch[:, k, :],
                                         start=(k == 0), stop=(k == 3)))
        h8_mm = tS.n

        @block.vector
        def _(vector):
            vector.wait_ge(vS.h, vS.n)
            vector.wait_ge(tS.h, h8_mm)
            for i in range(EPC):
                vS.inc(vector.tensor_copy(sl_sb[:, i * CP:(i + 1) * CP],
                                          ps_sl[i][:, :]))
        sl_ev = vS.n

        trs_snap = {}
        prev_trs = 0
        for i in range(EPC):
            for ch in range(3):
                @block.tensor
                def _(tensor, i=i, ch=ch, prev_trs=prev_trs):
                    if i == 0 and ch == 0:
                        tensor.wait_ge(vS.h, sl_ev)
                    else:
                        tensor.wait_ge(vS.h, prev_trs)
                    tS.inc(tensor.transpose(
                        ps_my[:, 0:2],
                        sl_sb[:, i * CP + ch * 128:i * CP + (ch + 1) * 128],
                        ident_f[0:2, 0:2]))
                    smm_snap[("trs", i, ch)] = tS.n

                @block.vector
                def _(vector, i=i, ch=ch):
                    vector.wait_ge(vS.h, vS.n)
                    V = vector
                    V.wait_ge(tS.h, smm_snap[("trs", i, ch)])
                    V.drain()
                    V.tensor_copy(trs_sb[:, :], ps_my[:, 0:2])
                    V.drain()
                    V.scalar_tensor_tensor(
                        trs_sb[:, 0:1], trs_sb[:, 0:1], 64.0, trs_sb[:, 1:2],
                        ALU.mult, ALU.add)
                    V.drain()
                    V.tensor_scalar(trs_sb[:, 0:1], trs_sb[:, 0:1],
                                    float(T - 1), 0.0, ALU.min, ALU.max)
                    V.drain()
                    vS.inc(V.tensor_copy(
                        sidx_exp[:, i * 3 + ch:i * 3 + ch + 1],
                        trs_sb[:, 0:1]))
                trs_snap[(i, ch)] = vS.n
                prev_trs = vS.n
        sidx_done = prev_trs

        if "sidx" in DBG:
            @block.sync
            def _(sync):
                sync.wait_ge(vS.h, sidx_done)
                stm.inc(sync.dma_start(out=dbg_t["sidx"], in_=sidx_exp[:, :]),
                        16)

        @block.tensor
        def _(tensor):
            tensor.wait_ge(vS.h, sidx_done)
            tensor.wait_ge(vS.h, tokp_done)
        esGp2.close()
        esG.close()
        esR.close()

        esX = ExitStack()
        xp_tok = sb("xp_tok", [128, 2, D], BF16, esX)
        wd_sb = a1[:, 0:16384].rearrange("p (a b) -> p a b", b=2048)
        gsxa = sb("gsxa", [128, 3, 1024], F32, esX)
        Hx = sb("Hx", [128, 3, 1024], BF16, esX)
        HTx = sb("HTx", [128, 8, 128], BF16, esX)
        ystage = [sb(f"ystage{i}", [128, D], BF16, esX) for i in range(2)]

        xg_snap = {}
        xpt_ev = {}
        kbatch_mm = {}
        wdld = {}
        gu_done = {}
        hx_done = {}
        silu_all = {}
        ymm = {}
        yev = {}
        yst = {}
        for e in range(EPC):
            esXt = ExitStack()
            ps_sm = [esXt.enter_context(
                nc.psum_tensor(f"ps_sm{e}_{i}", [128, 128], BF16))
                for i in range(2)]
            for ch in range(3):
                @block.gpsimd
                def _(gpsimd, e=e, ch=ch):
                    if e == 0 and ch == 0:
                        gpsimd.wait_ge(vS.h, sidx_done)
                        gpsimd.wait_ge(ccH.h, hag_done)
                    idx = 3 * e + ch
                    if idx >= 2:
                        pe, pch = divmod(idx - 2, 3)
                        gpsimd.wait_ge(vS.h, xpt_ev[(pe, pch, 15)])
                    gsl[(3 * e + ch) % 2].inc(gpsimd.indirect_dma_start(
                        out=xp_tok[:, (3 * e + ch) % 2, :], out_offset=None,
                        in_=hag_out[0:T, :],
                        in_offset=IndirectOffsetOnAxis(
                            sidx_exp[:, e * 3 + ch:e * 3 + ch + 1], 0)), 16)
                xg_snap[(e, ch)] = gsl[(3 * e + ch) % 2].n

                vbase = vS.n

                @block.tensor
                def _(tensor, e=e, ch=ch, vbase=vbase):
                    tensor.wait_ge(gsl[(3 * e + ch) % 2].h, xg_snap[(e, ch)])
                    if e == 0 and ch == 0:
                        tensor.wait_ge(tS.h, dense_last)
                    if e > 0 and ch == 0:
                        tensor.wait_ge(vS.h, yev[(e - 1, 2)])
                    for dc in range(16):
                        if dc < 2 and (3 * e + ch) >= 1:
                            pe2, pch2 = divmod(3 * e + ch - 1, 3)
                            tensor.wait_ge(vS.h, xpt_ev[(pe2, pch2, 14 + dc)])
                        if dc >= 2:
                            tensor.wait_ge(vS.h, vbase + dc - 1)
                        tS.inc(tensor.transpose(
                            ps_sm[dc % 2][:, :],
                            xp_tok[:, (3 * e + ch) % 2, dc * 128:(dc + 1) * 128],
                            ident_bf[:, :]))
                        smm_snap[("xt", e, ch, dc)] = tS.n

                @block.vector
                def _(vector, e=e, ch=ch):
                    vector.wait_ge(vS.h, vS.n)
                    for dc in range(16):
                        vector.wait_ge(tS.h, smm_snap[("xt", e, ch, dc)])
                        vS.inc(vector.tensor_copy(
                            XpT[:, dc, ch * 128:(ch + 1) * 128],
                            ps_sm[dc % 2][:, :]))
                        xpt_ev[(e, ch, dc)] = vS.n
                assert xpt_ev[(e, ch, 15)] == vbase + 16
            xpt_done = xpt_ev[(e, 2, 15)]
            esXt.close()

            @block.gpsimd
            def _(gpsimd, e=e):
                if e == 0:
                    gpsimd.wait_ge(tS.h, shared_gu_mm)
                else:
                    gpsimd.wait_ge(tS.h, ymm[(e - 1, 2)])
                wdv = wd[e].rearrange("(c p) d -> p c d", p=128)
                dB.inc(gpsimd.dma_start(out=wd_sb[:, 0:4, :],
                                        in_=wdv[:, 0:4, :]), 16)
                dB.inc(gpsimd.dma_start(out=wd_sb[:, 4:8, :],
                                        in_=wdv[:, 4:8, :]), 16)
            wdld[e] = dB.n

            esXp = ExitStack()
            ps_gu = [esXp.enter_context(
                nc.psum_tensor(f"ps_gu{e}_{st}", [128, 2, 512], F32))
                for st in range(3)]
            for pw, wsrc in ((0, wg), (1, wu)):
                for b4 in range(4):
                    k = (e * 2 + pw) * 4 + b4
                    if k >= 3:
                        @block.gpsimd
                        def _(gpsimd, e=e, pw=pw, b4=b4, k=k, wsrc=wsrc):
                            gpsimd.wait_ge(tS.h, kbatch_mm[k - 3])
                            dsl[4 + k % 3].inc(gpsimd.dma_start(
                                out=wgbuf[k % 3][:, :, :],
                                in_=wsrc[e].rearrange(
                                    "(c p) i -> p c i", p=128)[
                                    :, b4 * 4:b4 * 4 + 4, :]), 16)
                        guld[(e, pw, b4)] = dsl[4 + k % 3].n

                    @block.tensor
                    def _(tensor, e=e, pw=pw, b4=b4, k=k, xpt_done=xpt_done):
                        tensor.wait_ge(dsl[4 + k % 3].h, guld[(e, pw, b4)])
                        if b4 == 0:
                            tensor.wait_ge(vS.h, xpt_done)
                            if pw == 0 and e > 0:
                                tensor.wait_ge(vS.h, yev[(e - 1, 2)])
                            if pw == 1:
                                tensor.wait_ge(sS.h, silu_all[e])
                        for c4 in range(4):
                            dc = b4 * 4 + c4
                            for st in range(3):
                                sl = slice(st * 128, (st + 1) * 128)
                                for nn in range(2):
                                    tS.inc(tensor.matmul(
                                        ps_gu[st][:, nn, :], XpT[:, dc, sl],
                                        wgbuf[k % 3][:, c4, nn * 512:(nn + 1) * 512],
                                        start=(dc == 0), stop=(dc == 15)))
                    kbatch_mm[k] = tS.n
                    gu_done[(e, pw, b4)] = tS.n

                if pw == 0:
                    for st in range(3):
                        @block.scalar
                        def _(scalar, e=e, st=st):
                            scalar.wait_ge(tS.h, gu_done[(e, 0, 3)])
                            if e > 0 and st == 0:
                                scalar.wait_ge(vS.h, hx_done[(e - 1, 2)])
                            sS.inc(scalar.activation(
                                gsxa[:, st, :],
                                ps_gu[st].rearrange("p a b -> p (a b)"),
                                AF.Silu))
                    silu_all[e] = sS.n
                else:
                    for st in range(3):
                        @block.vector
                        def _(vector, e=e, st=st):
                            vector.wait_ge(vS.h, vS.n)
                            vector.wait_ge(tS.h, gu_done[(e, 1, 3)])
                            vector.wait_ge(sS.h, silu_all[e])
                            vector.drain()
                            vS.inc(vector.tensor_mul(
                                Hx[:, st, :], gsxa[:, st, :],
                                ps_gu[st].rearrange("p a b -> p (a b)")))
                        hx_done[(e, st)] = vS.n
            esXp.close()

            esXy = ExitStack()
            ps_eY = esXy.enter_context(
                nc.psum_tensor(f"ps_eY{e}", [128, 4, 512], F32))
            ps_sm = [esXy.enter_context(
                nc.psum_tensor(f"ps_smy{e}_{i}", [128, 128], BF16))
                for i in range(2)]
            for st in range(3):
                vbase = vS.n

                @block.tensor
                def _(tensor, e=e, st=st, vbase=vbase):
                    tensor.wait_ge(vS.h, hx_done[(e, st)])
                    if st == 0:
                        tensor.wait_ge(vS.h, hx_done[(e, 2)])
                        k = (e - 1, 2)
                        if k in yev:
                            tensor.wait_ge(vS.h, yev[k])
                    for ic in range(8):
                        if ic >= 2:
                            tensor.wait_ge(vS.h, vbase + ic - 1)
                        tS.inc(tensor.transpose(
                            ps_sm[ic % 2][:, :], Hx[:, st, ic * 128:(ic + 1) * 128],
                            ident_bf[:, :]))
                        smm_snap[("ht", e, st, ic)] = tS.n

                @block.vector
                def _(vector, e=e, st=st):
                    vector.wait_ge(vS.h, vS.n)
                    for ic in range(8):
                        vector.wait_ge(tS.h, smm_snap[("ht", e, st, ic)])
                        vS.inc(vector.tensor_copy(HTx[:, ic, :],
                                                  ps_sm[ic % 2][:, :]))
                ht_d = vS.n
                assert ht_d == vbase + 8

                @block.tensor
                def _(tensor, e=e, st=st, ht_d=ht_d):
                    tensor.wait_ge(vS.h, ht_d)
                    tensor.wait_ge(dB.h, wdld[e])
                    for ic in range(8):
                        for nn in range(4):
                            tS.inc(tensor.matmul(
                                ps_eY[:, nn, :], HTx[:, ic, :],
                                wd_sb[:, ic, nn * 512:(nn + 1) * 512],
                                start=(ic == 0), stop=(ic == 7)))
                ymm[(e, st)] = tS.n

                @block.vector
                def _(vector, e=e, st=st):
                    vector.wait_ge(vS.h, vS.n)
                    vector.wait_ge(tS.h, ymm[(e, st)])
                    k = ("st", e, st - 2) if st >= 2 else ("st", e - 1, st + 1)
                    if k in yst:
                        vector.wait_ge(tsl[(3 * (k[1]) + k[2]) % 2].h, yst[k])
                    vS.inc(vector.tensor_copy(
                        ystage[(3 * e + st) % 2][:, :],
                        ps_eY.rearrange("p a b -> p (a b)")))
                yev[(e, st)] = vS.n

                @block.sync
                def _(sync, e=e, st=st):
                    sync.wait_ge(vS.h, yev[(e, st)])
                    nrows = 128 if st < 2 else 64
                    tsl[(3 * e + st) % 2].inc(sync.dma_start(
                        out=yag_in[e * CCAP + st * 128:
                                   e * CCAP + st * 128 + nrows, :],
                        in_=ystage[(3 * e + st) % 2][0:nrows, :]), 16)
                yst[("st", e, st)] = tsl[(3 * e + st) % 2].n

            @block.tensor
            def _(tensor, e=e):
                tensor.wait_ge(vS.h, yev[(e, 2)])
            esXy.close()
        y_stores = (tsl[0].n, tsl[1].n)
        exp_gu_last = gu_done[(EPC - 1, 1, 3)]

        for i in range(EPC):
            @block.gpsimd
            def _(gpsimd, i=i):
                for st in range(3):
                    par = (3 * i + st) % 2
                    gpsimd.wait_ge(tsl[par].h, yst[("st", i, st)])
                ccY.inc(gpsimd.collective_compute(
                    "AllGather", ALU.bypass,
                    replica_groups=[list(range(NCORE))],
                    ins=[yag_in[i * CCAP:(i + 1) * CCAP, :].opt()],
                    outs=[yag_out[i * NCORE * CCAP:(i + 1) * NCORE * CCAP, :].opt()]))
        yag_done = ccY.n

        @block.tensor
        def _(tensor):
            tensor.wait_ge(vS.h, yev[(EPC - 1, 2)])
        esX.close()
        esXw.close()

        if "yag" in DBG:
            @block.sync
            def _(sync):
                sync.wait_ge(ccY.h, yag_done)
                stm.inc(sync.dma_start(out=dbg_t["yag"], in_=yag_out[:, :]), 16)

        esC = ExitStack()
        ygb = [sb(f"ygb{i}", [128, TOPK, D], BF16, esC) for i in range(2)]
        h_rel = [sb(f"h_rel{i}", [128, D], BF16, esC) for i in range(2)]
        dw_sb = [sb(f"dw{i}", [128, TOPK, 128], BF16, esC) for i in range(2)]
        o_scr = sb("o_scr", [128, D], F32, esC)
        o_scr2 = sb("o_scr2", [128, D], F32, esC)
        o_cols = sb("o_cols", [128, 8], F32, esC)
        out_st = [sb(f"out_st{i}", [128, D], F32, esC) for i in range(2)]
        ps_m = [esC.enter_context(nc.psum_tensor(f"ps_m{i}", [128, 4, 512], F32))
                for i in range(2)]

        @block.gpsimd
        def _(gpsimd):
            gpsimd.wait_ge(tS.h, exp_gu_last)
            gpsimd.wait_ge(tS.h, ymm[(EPC - 1, 2)])
            sdv = sdT.rearrange("(kc p) d -> p kc d", p=128)
            for i in range(4):
                dB.inc(gpsimd.dma_start(out=sdT_sb[:, 4 * i:4 * i + 4, :],
                                        in_=sdv[:, 4 * i:4 * i + 4, :]), 16)
        sdT_ld = dB.n

        ygg = [0] * 8
        dwb = [0] * 8
        hrl = [0] * 8
        cmm = [0] * 8
        oev = [0] * 8
        ost = [0] * 8
        for tt in range(8):
            @block.gpsimd
            def _(gpsimd, tt=tt):
                if tt == 0:
                    gpsimd.wait_ge(ccY.h, yag_done)
                    gpsimd.wait_ge(vS.h, tokp_done)
                if tt >= 2:
                    gpsimd.wait_ge(tS.h, cmm[tt - 2])
                for kk in range(TOPK):
                    gsl[tt % 2].inc(gpsimd.indirect_dma_start(
                        out=ygb[tt % 2][:, kk, :], out_offset=None,
                        in_=yag_out[0:NCORE * EPC * CCAP, :],
                        in_offset=IndirectOffsetOnAxis(
                            sidx_tok[:, tt, kk:kk + 1], 0)), 16)
            ygg[tt] = gsl[tt % 2].n

            @block.vector
            def _(vector, tt=tt):
                vector.wait_ge(vS.h, vS.n)
                if tt == 0:
                    vector.wait_ge(vS.h, tokp_done)
                    vector.wait_ge(ccY.h, yag_done)
                if tt >= 2:
                    vector.wait_ge(tS.h, cmm[tt - 2])
                for kk in range(TOPK):
                    vS.inc(vector.tensor_scalar(
                        dw_sb[tt % 2][:, kk, :], ident_bf[:, :],
                        weff_sb[:, tt, kk:kk + 1], None, ALU.mult))
            dwb[tt] = vS.n

            @block.scalar
            def _(scalar, tt=tt):
                if tt == 0:
                    scalar.wait_ge(ccY.h, yag_done)
                if tt >= 2:
                    scalar.wait_ge(vS.h, oev[tt - 2])
                ssl[tt % 2].inc(scalar.dma_start(
                    out=h_rel[tt % 2][:, :],
                    in_=hag_in[tt * 128:(tt + 1) * 128, :]), 16)
            hrl[tt] = ssl[tt % 2].n

            @block.tensor
            def _(tensor, tt=tt):
                if tt == 0:
                    tensor.wait_ge(dB.h, sdT_ld)
                    tensor.wait_ge(vS.h, shared_gu_done)
                    tensor.wait_ge(vS.h, yev[(EPC - 1, 2)])
                if tt >= 2:
                    tensor.wait_ge(vS.h, oev[tt - 2])
                for jc in range(16):
                    for nn in range(4):
                        tS.inc(tensor.matmul(
                            ps_m[tt % 2][:, nn, :],
                            HshT[:, jc, tt * 128:(tt + 1) * 128],
                            sdT_sb[:, jc, nn * 512:(nn + 1) * 512],
                            start=(jc == 0), stop=False))
                tensor.wait_ge(gsl[tt % 2].h, ygg[tt])
                tensor.wait_ge(vS.h, dwb[tt])
                for kk in range(TOPK):
                    for nn in range(4):
                        tS.inc(tensor.matmul(
                            ps_m[tt % 2][:, nn, :], dw_sb[tt % 2][:, kk, :],
                            ygb[tt % 2][:, kk, nn * 512:(nn + 1) * 512],
                            start=False, stop=(kk == TOPK - 1)))
            cmm[tt] = tS.n

            @block.vector
            def _(vector, tt=tt):
                vector.wait_ge(vS.h, vS.n)
                V = vector
                V.wait_ge(ssl[tt % 2].h, hrl[tt])
                if tt >= 2:
                    V.wait_ge(vS.h, oev[tt - 2])
                V.drain()
                V.tensor_copy(o_scr[:, :], h_rel[tt % 2][:, :])
                V.wait_ge(tS.h, cmm[tt])
                V.drain()
                vS.inc(V.tensor_add(o_scr2[:, :],
                                    ps_m[tt % 2].rearrange("p a b -> p (a b)"),
                                    o_scr[:, :]))
            resid2_done = vS.n

            @block.scalar
            def _(scalar, tt=tt, resid2_done=resid2_done):
                scalar.wait_ge(vS.h, resid2_done)
                scalar.wait_ge(sS.h, sS.n)
                sS.inc(scalar.activation(o_scr[:, :], o_scr2[:, :], AF.Copy,
                                         accum_out=o_cols[:, 0:1]))
                scalar.wait_ge(sS.h, sS.n)
                sS.inc(scalar.activation(o_scr[:, :], o_scr2[:, :], AF.Square,
                                         accum_out=o_cols[:, 2:3]))
            stats2 = sS.n

            @block.vector
            def _(vector, tt=tt, stats2=stats2):
                vector.wait_ge(vS.h, vS.n)
                V = vector
                V.wait_ge(sS.h, stats2)
                V.drain()
                V.tensor_scalar(o_cols[:, 1:2], o_cols[:, 0:1], 1.0 / D, None,
                                ALU.mult)
                V.drain()
                V.tensor_tensor(o_cols[:, 3:4], o_cols[:, 1:2], o_cols[:, 1:2],
                                ALU.mult)
                V.drain()
                V.tensor_scalar(o_cols[:, 4:5], o_cols[:, 2:3], 1.0 / D, None,
                                ALU.mult)
                V.drain()
                V.tensor_sub(o_cols[:, 5:6], o_cols[:, 4:5], o_cols[:, 3:4])
                V.drain()
                vS.inc(V.tensor_scalar(o_cols[:, 5:6], o_cols[:, 5:6], 1.0,
                                       LN_EPS, ALU.mult, ALU.add))
            var2 = vS.n

            @block.scalar
            def _(scalar, var2=var2):
                scalar.wait_ge(vS.h, var2)
                sS.inc(scalar.activation(o_cols[:, 6:7], o_cols[:, 5:6],
                                         AF.Sqrt))
            std2 = sS.n

            @block.vector
            def _(vector, tt=tt, std2=std2):
                vector.wait_ge(vS.h, vS.n)
                V = vector
                V.wait_ge(sS.h, std2)
                V.reciprocal(o_cols[:, 7:8], o_cols[:, 6:7])
                V.drain()
                if tt >= 2:
                    V.wait_ge(tsl[tt % 2].h, ost[tt - 2])
                vS.inc(V.tensor_scalar(out_st[tt % 2][:, :], o_scr2[:, :],
                                       o_cols[:, 1:2], o_cols[:, 7:8],
                                       ALU.subtract, ALU.mult))
            oev[tt] = vS.n

            if "moe" in DBG:
                @block.sync
                def _(sync, tt=tt):
                    sync.wait_ge(vS.h, oev[tt])
                    stm.inc(sync.dma_start(
                        out=dbg_t["moe"][tt * 128:(tt + 1) * 128, :],
                        in_=o_scr2[:, :]), 16)

            @block.sync
            def _(sync, tt=tt):
                sync.wait_ge(vS.h, oev[tt])
                tsl[tt % 2].inc(sync.dma_start(
                    out=out[tt * 128:(tt + 1) * 128, :],
                    in_=out_st[tt % 2][:, :]), 16)
            ost[tt] = tsl[tt % 2].n
        final_st = (tsl[0].n, tsl[1].n, stm.n)

        @block.sync
        def _(sync):
            sync.wait_ge(tsl[0].h, final_st[0])
            sync.wait_ge(tsl[1].h, final_st[1])
            sync.wait_ge(stm.h, final_st[2])

        @block.gpsimd
        def _(gpsimd):
            gpsimd.wait_ge(tsl[0].h, final_st[0])
            gpsimd.wait_ge(tsl[1].h, final_st[1])
            gpsimd.wait_ge(stm.h, final_st[2])
        esC.close()

    return nc


def _host_inputs(inputs):
    x = np.ascontiguousarray(np.asarray(inputs["L0_attn_x"], np.float32))
    w_q = np.asarray(inputs["L0_attn_w_q"], np.float32)
    w_kv = np.asarray(inputs["L0_attn_w_kv"], np.float32)
    w_dense = np.asarray(inputs["L0_attn_w_dense"], np.float32)
    router_w = np.asarray(inputs["L0_ffn_router_w"], np.float32)
    e_bias = np.asarray(inputs["L0_ffn_e_bias"], np.float32)
    W_gate = np.asarray(inputs["L0_ffn_W_gate"], np.float32)
    W_up = np.asarray(inputs["L0_ffn_W_up"], np.float32)
    W_down = np.asarray(inputs["L0_ffn_W_down"], np.float32)
    sg = np.asarray(inputs["L0_ffn_shared_gate"], np.float32)
    su = np.asarray(inputs["L0_ffn_shared_up"], np.float32)
    sd = np.asarray(inputs["L0_ffn_shared_down"], np.float32)

    ident = np.eye(128, dtype=np.float32)
    lstrict = (np.arange(128)[:, None] < np.arange(128)[None, :]).astype(BF)
    masks = np.zeros((4, 128, 512), dtype=BF)
    for m in range(4):
        masks[m] = (np.arange(128)[:, None] <=
                    (np.arange(512)[None, :] - 128 * m)).astype(BF)
    iota384 = np.arange(384, dtype=np.float32)[None, :]
    grpidx = (np.arange(E) // EPG).astype(np.float32)[None, :]
    hilo = np.zeros((128, 128), dtype=BF)
    for j in range(64):
        t = 128 * j + np.arange(128)
        hilo[:, 2 * j] = (t // 64).astype(BF)
        hilo[:, 2 * j + 1] = (t % 64).astype(BF)
    sgT = np.ascontiguousarray(sg.T.reshape(D, 16, 128).transpose(1, 0, 2))
    suT = np.ascontiguousarray(su.T.reshape(D, 16, 128).transpose(1, 0, 2))
    sdT = np.ascontiguousarray(sd.T)

    in_maps = []
    for c in range(NCORE):
        b, half = c // 2, c % 2
        wq_c = w_q[:, half * 1024:(half + 1) * 1024]
        wqh = np.ascontiguousarray(wq_c.reshape(D, NH, KVC).transpose(1, 0, 2))
        cmask8 = (np.arange(NCORE) < c).astype(np.float32)[:, None]
        selhl = np.zeros((128, 8), np.float32)
        for p in range(128):
            j = p % 64 - 8 * c
            if 0 <= j < 8:
                selhl[p, j] = 1.0
        ee = np.arange(E)
        sbase = ((ee % EPC) * (NCORE * CCAP) + (ee // EPC) * CCAP).astype(
            np.float32)[None, :]
        m = {
            "xT": np.ascontiguousarray(x[b].T),
            "x_own": np.ascontiguousarray(x[b, half * 1024:(half + 1) * 1024]),
            "wqh": wqh,
            "wk": np.ascontiguousarray(w_kv[:, 256 * half:256 * half + 256]),
            "wv": np.ascontiguousarray(
                w_kv[:, 512 + 256 * half:512 + 256 * half + 256]),
            "wdense": np.ascontiguousarray(
                w_dense[half * 1024:(half + 1) * 1024, :]),
            "routerT": np.ascontiguousarray(router_w.T),
            "wg": np.ascontiguousarray(W_gate[EPC * c:EPC * c + EPC]),
            "wu": np.ascontiguousarray(W_up[EPC * c:EPC * c + EPC]),
            "wd": np.ascontiguousarray(W_down[EPC * c:EPC * c + EPC]),
            "sgT": sgT, "suT": suT, "sdT": sdT,
            "ident_bf_i": ident.astype(BF), "ident_f_i": ident,
            "lstrict_i": lstrict, "masks_i": masks,
            "iota384_i": iota384, "grpidx_i": grpidx,
            "ebias_i": e_bias[None, :].astype(np.float32),
            "hiloown_i": np.ascontiguousarray(hilo[:, 16 * c:16 * c + 16]),
            "cmask8_i": cmask8, "selhl_i": selhl, "sbase_i": sbase,
        }
        in_maps.append(m)
    return in_maps


def kernel(**inputs):
    in_maps = _host_inputs(inputs)
    nc = build()
    res = run_bass_kernel_spmd(nc, in_maps, core_ids=list(range(NCORE)))
    y = np.zeros((B, SEQ, D), np.float32)
    for c in range(NCORE):
        b, half = c // 2, c % 2
        y[b, half * 1024:(half + 1) * 1024] = res.results[c]["out"]
    if DBG:
        kernel.dbg = [{k: np.asarray(v) for k, v in r.items()
                       if k.startswith("dbg_")} for r in res.results]
    return y


# version 12 (final)
# speedup vs baseline: 1.0719x; 1.0076x over previous
import numpy as np
import ml_dtypes
from contextlib import ExitStack

import concourse.bass as bass
import concourse.mybir as mybir
from concourse.bass import IndirectOffsetOnAxis
from concourse.bass_utils import run_bass_kernel_spmd

F32 = mybir.dt.float32
BF16 = mybir.dt.bfloat16
I32 = mybir.dt.int32
AF = mybir.ActivationFunctionType
ALU = mybir.AluOpType
AX = mybir.AxisListType
BF = ml_dtypes.bfloat16

D = 2048
SEQ = 2048
B = 4
NCORE = 8
TPC = 1024
T = 8192
NH = 8
KVC = 128
NGC = 2
E = 32
EPC = 4
TOPK = 4
CCAP = 320
CP = 384
NGRP = 8
EPG = 4
TKG = 3
SCALE_W = 2.5
SCALE_ATT = float(KVC) ** -0.5
BIGI = 1.0e4
BIGM = 1.0e6
LN_EPS = 1e-5

DBG = ()


class Sem:
    def __init__(self, es, nc, name):
        self.h = es.enter_context(nc.semaphore(name))
        self.n = 0

    def inc(self, instr, k=1):
        self.n += k
        instr.then_inc(self.h, k)
        return instr


def build():
    nc = bass.Bass("TRN2", target_bir_lowering=False, debug=False)

    def din(name, shape, dt=F32):
        return nc.dram_tensor(name, shape, dt, kind="ExternalInput").ap()

    xT = din("xT", [D, SEQ])
    x_own = din("x_own", [TPC, D])
    wqh = din("wqh", [NH, D, KVC])
    wk = din("wk", [D, NGC * KVC])
    wv = din("wv", [D, NGC * KVC])
    wdense = din("wdense", [NH * KVC, D])
    routerT = din("routerT", [D, E])
    wg = din("wg", [EPC, D, 1024])
    wu = din("wu", [EPC, D, 1024])
    wd = din("wd", [EPC, 1024, D])
    sgT = din("sgT", [16, D, 128])
    suT = din("suT", [16, D, 128])
    sdT = din("sdT", [D, D])
    ident_bf_i = din("ident_bf_i", [128, 128], BF16)
    ident_f_i = din("ident_f_i", [128, 128])
    lstrict_i = din("lstrict_i", [128, 128], BF16)
    masks_i = din("masks_i", [4, 128, 512], BF16)
    iota384_i = din("iota384_i", [1, 384])
    grpidx_i = din("grpidx_i", [1, E])
    ebias_i = din("ebias_i", [1, E])
    hiloown_i = din("hiloown_i", [128, 16], BF16)
    cmask8_i = din("cmask8_i", [8, 1])
    selhl_i = din("selhl_i", [128, 8])
    sbase_i = din("sbase_i", [1, E])

    out = nc.dram_tensor("out", [TPC, D], F32, kind="ExternalOutput").ap()

    dbg_t = {}
    dbg_specs = {
        "attn": ([TPC, D], BF16), "h1": ([TPC, D], BF16),
        "scores": ([128, 8 * 32], F32), "aall": ([128, 64 * 32], F32),
        "posm": ([128, 8 * 32], F32), "sidx": ([128, 12], I32),
        "weff": ([128, 8 * 4], F32), "stok": ([128, 8 * 4], I32),
        "yag": ([NCORE * EPC * CCAP, D], BF16), "moe": ([TPC, D], F32),
        "hsh": ([128, 16 * 1024], BF16), "ctxt": ([128, NH * SEQ], BF16),
        "qt": ([128, NH * SEQ], BF16), "kt": ([128, NGC * SEQ], BF16),
        "qt0": ([128, SEQ], BF16), "qt1": ([128, SEQ], BF16),
        "vt": ([128, 16 * 256], BF16), "negb": ([128, NH], F32),
        "es0": ([128, 16 * 512], BF16),
    }
    for name in DBG:
        shape, dt = dbg_specs[name]
        dbg_t[name] = nc.dram_tensor("dbg_" + name, shape, dt,
                                     kind="ExternalOutput").ap()

    rs_in = nc.dram_tensor("rs_in", [SEQ, D], BF16)
    rs_out = nc.dram_tensor("rs_out", [TPC, D], BF16)
    hag_in = nc.dram_tensor("hag_in", [TPC, D], BF16)
    hag_out = nc.dram_tensor("hag_out", [T, D], BF16)
    cnt_in = nc.dram_tensor("cnt_in", [1, E], F32)
    cnt_out = nc.dram_tensor("cnt_out", [NCORE, E], F32)
    hl_in = nc.dram_tensor("hl_in", [64, CP], F32)
    hl_out = nc.dram_tensor("hl_out", [NCORE * 64, CP], F32)
    yag_in = nc.dram_tensor("yag_in", [EPC * CCAP, D], BF16)
    lscr = nc.dram_tensor("lscr", [1, 8 * E], F32)
    yag_out = nc.dram_tensor("yag_out", [NCORE * EPC * CCAP, D], BF16)

    es = ExitStack()
    with es:
        def sb(name, shape, dt=F32, stk=es):
            return stk.enter_context(nc.sbuf_tensor(name, shape, dt))

        ident_bf = sb("ident_bf", [128, 128], BF16)
        ident_f = sb("ident_f", [128, 128])
        lstrict = sb("lstrict", [128, 128], BF16)
        masks = sb("masks", [128, 4, 512], BF16)
        iota_bc = sb("iota_bc", [128, 384])
        iplusb = sb("iplusb", [128, 384])
        iotaP1 = sb("iotaP1", [128, 384])
        grpidx_bc = sb("grpidx_bc", [128, E])
        ebias_bc = sb("ebias_bc", [128, E])
        hiloown = sb("hiloown", [128, 16], BF16)
        cmask8 = sb("cmask8", [8, 1])
        selhl = sb("selhl", [128, 8])
        sbase_bc = sb("sbase_bc", [128, E])
        ones_cb = sb("ones_cb", [128, 1], BF16)
        ones_cf = sb("ones_cf", [128, 1])
        ones_rf = sb("ones_rf", [1, 128])
        routerT_sb = sb("routerT_sb", [128, 16, E])
        scores_sb = sb("scores_sb", [128, 8, E])
        A_loc = sb("A_loc", [128, 8, E], BF16)
        idxk_sb = sb("idxk_sb", [128, 8, TOPK])
        wtop_sb = sb("wtop_sb", [128, 8, TOPK])
        weff_sb = sb("weff_sb", [128, 8, TOPK])
        sidx_tok = sb("sidx_tok", [128, 8, TOPK], I32)
        pos_mine = sb("pos_mine", [128, 8, E])
        sidx_exp = sb("sidx_exp", [128, 12], I32)
        arena1 = sb("arena1", [128, 32768], BF16)
        arena2 = sb("arena2", [128, 16384], BF16)
        a1 = arena1.ap()
        xT_sb = a1.rearrange("p (a b) -> p a b", b=2048)
        expS = [a1[:, i * 8192:(i + 1) * 8192].rearrange(
            "p (a b) -> p a b", b=512) for i in range(2)]
        ctxT_sb = a1[:, 16384:32768].rearrange("p (a b) -> p a b", b=2048)
        wdense_sb = a1[:, 0:16384].rearrange("p (a b) -> p a b", b=2048)
        hT_bf = a1[:, 0:16384].rearrange("p (a b) -> p a b", b=1024)
        XpT = a1[:, 16384:22528].rearrange("p (a b) -> p a b", b=CP)
        sdT_sb = a1.rearrange("p (a b) -> p a b", b=2048)
        a2 = arena2.ap()
        qT_sb = a2.rearrange("p (a b) -> p a b", b=2048)
        HshT = a2.rearrange("p (a b) -> p a b", b=1024)

        dl = Sem(es, nc, "dl")
        dsc = Sem(es, nc, "dsc")
        dg = Sem(es, nc, "dg")
        dgi = Sem(es, nc, "dgi")
        stm = Sem(es, nc, "stm")
        tS = Sem(es, nc, "tS")
        vS = Sem(es, nc, "vS")
        sS = Sem(es, nc, "sS")
        dB = Sem(es, nc, "dB")
        dsl = [Sem(es, nc, f"dsl{i}") for i in range(7)]
        ssl = [Sem(es, nc, f"ssl{i}") for i in range(2)]
        tsl = [Sem(es, nc, f"tsl{i}") for i in range(2)]
        gsl = [Sem(es, nc, f"gsl{i}") for i in range(2)]
        gS = Sem(es, nc, "gS")
        cSx = Sem(es, nc, "cSx")
        ccR = Sem(es, nc, "ccR")
        ccH = Sem(es, nc, "ccH")
        ccA = Sem(es, nc, "ccA")
        ccL = Sem(es, nc, "ccL")
        hss = [Sem(es, nc, f"hss{i}") for i in range(2)]
        ccY = Sem(es, nc, "ccY")

        block = es.enter_context(nc.Block())

        @block.sync
        def _(sync):
            dl.inc(sync.dma_start(out=ident_bf[:, :], in_=ident_bf_i), 16)
            dl.inc(sync.dma_start(out=ident_f[:, :], in_=ident_f_i), 16)
            dl.inc(sync.dma_start(out=lstrict[:, :], in_=lstrict_i), 16)
            dl.inc(sync.dma_start(out=masks[:, :, :],
                                  in_=masks_i.rearrange("m k q -> k m q")), 16)
            dl.inc(sync.dma_start(out=iota_bc[:, :],
                                  in_=iota384_i.partition_broadcast(128)), 16)
            dl.inc(sync.dma_start(out=grpidx_bc[:, :],
                                  in_=grpidx_i.partition_broadcast(128)), 16)
            dl.inc(sync.dma_start(out=ebias_bc[:, :],
                                  in_=ebias_i.partition_broadcast(128)), 16)
            dl.inc(sync.dma_start(out=hiloown[:, :], in_=hiloown_i), 16)
            dl.inc(sync.dma_start(out=cmask8[:, :], in_=cmask8_i), 16)
            dl.inc(sync.dma_start(out=selhl[:, :], in_=selhl_i), 16)
            dl.inc(sync.dma_start(out=sbase_bc[:, :],
                                  in_=sbase_i.partition_broadcast(128)), 16)
        p0_loads = dl.n

        @block.scalar
        def _(scalar):
            dsc.inc(scalar.dma_start(
                out=routerT_sb[:, :, :],
                in_=routerT.rearrange("(kc p) e -> p kc e", p=128)), 16)
        routerT_ld = dsc.n

        @block.vector
        def _(vector):
            vector.wait_ge(vS.h, vS.n)
            vector.memset(ones_cb[:, :], 1.0)
            vector.memset(ones_cf[:, :], 1.0)
            vector.memset(ones_rf[:, :], 1.0)
            vector.wait_ge(dl.h, p0_loads)
            vector.tensor_scalar(iplusb[:, :], iota_bc[:, :], BIGI,
                                 None, ALU.add)
            vS.inc(vector.tensor_scalar(iotaP1[:, :], iota_bc[:, :], 1.0,
                                        None, ALU.add))
        consts_done = vS.n

        esAq = ExitStack()
        kT_sb = sb("kT_sb", [128, NGC, SEQ], BF16, esAq)
        v_sb = sb("v_sb", [128, 16, NGC * KVC], BF16, esAq)
        sq_scr = sb("sq_scr", [128, SEQ], BF16, esAq)
        nrm_scr = sb("nrm_scr", [1, SEQ], F32, esAq)
        bnd_scr = sb("bnd_scr", [1, 20], F32, esAq)
        negb_col = sb("negb_col", [128, NH], F32, esAq)

        esA1 = ExitStack()
        wk_sb = sb("wk_sb", [128, 16, NGC * KVC], BF16, esA1)
        wv_sb = sb("wv_sb", [128, 16, NGC * KVC], BF16, esA1)
        wq_buf = sb("wq_buf", [128, 2, 16, KVC], BF16, esA1)

        @block.gpsimd
        def _(gpsimd):
            xTv = xT.rearrange("(kc p) t -> p kc t", p=128)
            for i in range(4):
                dB.inc(gpsimd.dma_start(out=xT_sb[:, 4 * i:4 * i + 4, :],
                                        in_=xTv[:, 4 * i:4 * i + 4, :]), 16)
            dB.inc(gpsimd.dma_start(
                out=wk_sb[:, :, :],
                in_=wk.rearrange("(kc p) e -> p kc e", p=128)), 16)
            dB.inc(gpsimd.dma_start(
                out=wv_sb[:, :, :],
                in_=wv.rearrange("(kc p) e -> p kc e", p=128)), 16)
        xT_loaded = dB.n

        esA1p = ExitStack()
        ps_q = [esA1p.enter_context(nc.psum_tensor(f"ps_q{i}", [128, 4, 512], F32))
                for i in range(2)]
        qmm_snap = [0] * (NH + NGC)
        qev_snap = [0] * (NH + NGC)
        wqld_snap = [0] * NH

        for h in range(NH):
            @block.gpsimd
            def _(gpsimd, h=h):
                if h >= 2:
                    gpsimd.wait_ge(tS.h, qmm_snap[h - 2])
                dsl[h % 2].inc(gpsimd.dma_start(
                    out=wq_buf[:, h % 2, :, :],
                    in_=wqh[h].rearrange("(kc p) c -> p kc c", p=128)), 16)
            wqld_snap[h] = dsl[h % 2].n

            @block.tensor
            def _(tensor, h=h):
                tensor.wait_ge(dB.h, xT_loaded)
                tensor.wait_ge(dsl[h % 2].h, wqld_snap[h])
                if h >= 2:
                    tensor.wait_ge(sS.h, qev_snap[h - 2])
                for kc in range(16):
                    for qc in range(4):
                        tS.inc(tensor.matmul(
                            ps_q[h % 2][:, qc, :], wq_buf[:, h % 2, kc, :],
                            xT_sb[:, kc, qc * 512:(qc + 1) * 512],
                            start=(kc == 0), stop=(kc == 15)))
            qmm_snap[h] = tS.n

            @block.scalar
            def _(scalar, h=h):
                scalar.wait_ge(tS.h, qmm_snap[h])
                sS.inc(scalar.copy(qT_sb[:, h, :],
                                   ps_q[h % 2].rearrange("p a b -> p (a b)")))
            qev_snap[h] = sS.n

        if "qt0" in DBG:
            @block.sync
            def _(sync):
                sync.wait_ge(sS.h, qev_snap[0])
                stm.inc(sync.dma_start(out=dbg_t["qt0"], in_=qT_sb[:, 0, :]), 16)

        for g in range(NGC):
            u = NH + g

            @block.tensor
            def _(tensor, g=g, u=u):
                tensor.wait_ge(sS.h, qev_snap[u - 2])
                for kc in range(16):
                    for qc in range(4):
                        tS.inc(tensor.matmul(
                            ps_q[u % 2][:, qc, :],
                            wk_sb[:, kc, g * KVC:(g + 1) * KVC],
                            xT_sb[:, kc, qc * 512:(qc + 1) * 512],
                            start=(kc == 0), stop=(kc == 15)))
            qmm_snap[u] = tS.n

            @block.scalar
            def _(scalar, g=g, u=u):
                scalar.wait_ge(tS.h, qmm_snap[u])
                sS.inc(scalar.copy(kT_sb[:, g, :],
                                   ps_q[u % 2].rearrange("p a b -> p (a b)")))
            qev_snap[u] = sS.n

        @block.tensor
        def _(tensor):
            tensor.wait_ge(sS.h, qev_snap[NH])
            tensor.wait_ge(sS.h, qev_snap[NH + 1])
        esA1p.close()

        esA1v = ExitStack()
        ps_v = [esA1v.enter_context(nc.psum_tensor(f"ps_v{i}", [128, NGC * KVC], F32))
                for i in range(2)]
        vmm_snap = [0] * 16
        vev_snap = [0] * 16
        for tt in range(16):
            @block.tensor
            def _(tensor, tt=tt):
                if tt >= 2:
                    tensor.wait_ge(sS.h, vev_snap[tt - 2])
                for kc in range(16):
                    tS.inc(tensor.matmul(
                        ps_v[tt % 2][:, :], xT_sb[:, kc, tt * 128:(tt + 1) * 128],
                        wv_sb[:, kc, :], start=(kc == 0), stop=(kc == 15)))
            vmm_snap[tt] = tS.n

            @block.scalar
            def _(scalar, tt=tt):
                scalar.wait_ge(tS.h, vmm_snap[tt])
                sS.inc(scalar.copy(v_sb[:, tt, :], ps_v[tt % 2][:, :]))
            vev_snap[tt] = sS.n
        a1_done = sS.n

        @block.tensor
        def _(tensor):
            tensor.wait_ge(sS.h, a1_done)
        esA1v.close()
        esA1.close()

        esB = ExitStack()
        ps_nb = esB.enter_context(nc.psum_tensor("ps_nb", [1, 4, 512], F32))
        ps_nc = esB.enter_context(nc.psum_tensor("ps_nc", [128, NH], F32))
        bnd_units = [("q", h) for h in range(NH)] + [("k", g) for g in range(NGC)]
        bsq_snap = [0] * 10
        bmm_snap = [0] * 10
        bred_snap = [0] * 10
        for ui, (kind, i) in enumerate(bnd_units):
            @block.scalar
            def _(scalar, ui=ui, kind=kind, i=i):
                scalar.wait_ge(sS.h, a1_done)
                if ui > 0:
                    scalar.wait_ge(tS.h, bmm_snap[ui - 1])
                src = qT_sb[:, i, :] if kind == "q" else kT_sb[:, i, :]
                sS.inc(scalar.activation(sq_scr[:, :], src, AF.Square))
            bsq_snap[ui] = sS.n

            @block.tensor
            def _(tensor, ui=ui):
                tensor.wait_ge(sS.h, bsq_snap[ui])
                if ui == 0:
                    tensor.wait_ge(vS.h, consts_done)
                if ui > 0:
                    tensor.wait_ge(vS.h, bred_snap[ui - 1])
                for qc in range(4):
                    tS.inc(tensor.matmul(
                        ps_nb[0:1, qc, :], ones_cb[:, :],
                        sq_scr[:, qc * 512:(qc + 1) * 512],
                        start=True, stop=True))
            bmm_snap[ui] = tS.n

            @block.vector
            def _(vector, ui=ui):
                vector.wait_ge(vS.h, vS.n)
                vector.wait_ge(tS.h, bmm_snap[ui])
                vector.tensor_copy(nrm_scr[:, :],
                                   ps_nb.rearrange("p a b -> p (a b)"))
                vector.drain()
                vS.inc(vector.reduce_max(bnd_scr[:, ui:ui + 1], nrm_scr[:, :],
                                         axis=AX.X))
            bred_snap[ui] = vS.n

        @block.scalar
        def _(scalar):
            scalar.wait_ge(vS.h, bred_snap[9])
            sS.inc(scalar.activation(bnd_scr[:, 0:10], bnd_scr[:, 0:10],
                                     AF.Sqrt))
        bnd_sqrt = sS.n

        @block.vector
        def _(vector):
            vector.wait_ge(vS.h, vS.n)
            vector.wait_ge(sS.h, bnd_sqrt)
            for h in range(NH):
                vector.tensor_tensor(bnd_scr[:, 11 + h:12 + h],
                                     bnd_scr[:, h:h + 1],
                                     bnd_scr[:, 8 + h // 4:9 + h // 4], ALU.mult)
            vector.drain()
            vS.inc(vector.tensor_scalar(bnd_scr[:, 11:11 + NH],
                                        bnd_scr[:, 11:11 + NH],
                                        -SCALE_ATT * 1.001, -1e-3,
                                        ALU.mult, ALU.add))
        negb_row = vS.n

        @block.tensor
        def _(tensor):
            tensor.wait_ge(vS.h, negb_row)
            for h in range(NH):
                tS.inc(tensor.matmul(ps_nc[:, h:h + 1], ones_rf[:, :],
                                     bnd_scr[0:1, 11 + h:12 + h],
                                     start=True, stop=True))
        negb_mm = tS.n

        @block.vector
        def _(vector):
            vector.wait_ge(vS.h, vS.n)
            vector.wait_ge(tS.h, negb_mm)
            vS.inc(vector.tensor_copy(negb_col[:, :], ps_nc[:, :]))
        negb_done = vS.n

        @block.tensor
        def _(tensor):
            tensor.wait_ge(vS.h, negb_done)
        esB.close()

        if "qt1" in DBG:
            @block.sync
            def _(sync):
                sync.wait_ge(sS.h, a1_done)
                stm.inc(sync.dma_start(out=dbg_t["qt1"], in_=qT_sb[:, 0, :]), 16)
        if "qt" in DBG:
            @block.sync
            def _(sync):
                sync.wait_ge(sS.h, a1_done)
                stm.inc(sync.dma_start(
                    out=dbg_t["qt"], in_=qT_sb.rearrange("p a b -> p (a b)")), 16)
        if "kt" in DBG:
            @block.sync
            def _(sync):
                sync.wait_ge(sS.h, a1_done)
                stm.inc(sync.dma_start(
                    out=dbg_t["kt"], in_=kT_sb.rearrange("p a b -> p (a b)")), 16)
        if "vt" in DBG:
            @block.sync
            def _(sync):
                sync.wait_ge(sS.h, a1_done)
                stm.inc(sync.dma_start(
                    out=dbg_t["vt"], in_=v_sb.rearrange("p a b -> p (a b)")), 16)
        if "negb" in DBG:
            @block.sync
            def _(sync):
                sync.wait_ge(vS.h, negb_done)
                stm.inc(sync.dma_start(out=dbg_t["negb"], in_=negb_col[:, :]), 16)

        esA2 = ExitStack()
        sum_row = [sb(f"sum_row{i}", [1, 512], F32, esA2) for i in range(2)]
        rec_row = [sb(f"rec_row{i}", [1, 512], F32, esA2) for i in range(2)]
        rb_sb = [sb(f"rb_sb{i}", [128, 512], F32, esA2) for i in range(2)]
        ps_S = [esA2.enter_context(nc.psum_tensor(f"ps_S{i}", [128, 512], F32))
                for i in range(2)]
        ps_sum = [esA2.enter_context(nc.psum_tensor(f"ps_sum{i}", [1, 512], F32))
                  for i in range(2)]
        ps_rb = [esA2.enter_context(nc.psum_tensor(f"ps_rb{i}", [128, 512], F32))
                 for i in range(2)]
        ps_ctx = [esA2.enter_context(nc.psum_tensor(f"ps_ctx{i}", [128, 512], F32))
                  for i in range(2)]

        chunks = [(h, qc) for h in range(NH) for qc in range(4)]
        NCH = len(chunks)
        tiles_of = []
        g0 = 0
        for (h, qc) in chunks:
            tiles_of.append(list(range(g0, g0 + 4 * qc + 4)))
            g0 += 4 * qc + 4
        exp_snap = {}
        smm_snap = {}
        maskd = [0] * NCH
        sumsd = [0] * NCH
        recipd = [0] * NCH
        rbmmd = [0] * NCH
        rbevd = [0] * NCH
        ctxmmd = [0] * NCH
        ctxevd = [0] * NCH

        def a2_S(ci):
            h, qc = chunks[ci]
            sbase = sS.n
            t0c = tiles_of[ci][0]

            @block.tensor
            def _(tensor):
                if ci == 0:
                    tensor.wait_ge(vS.h, consts_done)
                g = h // 4
                for j, tg in enumerate(tiles_of[ci]):
                    if tg >= 2:
                        tg2 = tg - 2
                        snap = (exp_snap[tg2] if tg2 < t0c
                                else sbase + (tg2 - t0c + 1))
                        tensor.wait_ge(sS.h, snap)
                    tS.inc(tensor.matmul(
                        ps_S[tg % 2][:, :],
                        kT_sb[:, g, j * 128:(j + 1) * 128],
                        qT_sb[:, h, qc * 512:(qc + 1) * 512],
                        start=True, stop=True))
                    smm_snap[tg] = tS.n

        def a2_exp(ci):
            h, qc = chunks[ci]

            @block.scalar
            def _(scalar):
                if ci >= 2:
                    scalar.wait_ge(tS.h, ctxmmd[ci - 2])
                for j, tg in enumerate(tiles_of[ci]):
                    scalar.wait_ge(tS.h, smm_snap[tg])
                    sS.inc(scalar.activation(
                        expS[ci % 2][:, j, :], ps_S[tg % 2][:, :], AF.Exp,
                        bias=negb_col[:, h:h + 1], scale=SCALE_ATT))
                    exp_snap[tg] = sS.n

        def a2_mask(ci):
            h, qc = chunks[ci]

            @block.vector
            def _(vector):
                vector.wait_ge(vS.h, vS.n)
                vector.wait_ge(sS.h, exp_snap[tiles_of[ci][-1]])
                for m in range(4):
                    vS.inc(vector.tensor_mul(expS[ci % 2][:, 4 * qc + m, :],
                                             expS[ci % 2][:, 4 * qc + m, :],
                                             masks[:, m, :]))
            maskd[ci] = vS.n

        def a2_sums(ci):
            h, qc = chunks[ci]
            nkt = 4 * qc + 4

            @block.tensor
            def _(tensor):
                tensor.wait_ge(vS.h, maskd[ci])
                if ci >= 2:
                    tensor.wait_ge(vS.h, recipd[ci - 2])
                for j in range(nkt):
                    tS.inc(tensor.matmul(
                        ps_sum[ci % 2][0:1, :], ones_cb[:, :],
                        expS[ci % 2][:, j, :],
                        start=(j == 0), stop=(j == nkt - 1)))
            sumsd[ci] = tS.n

        def a2_recip(ci):
            @block.vector
            def _(vector):
                vector.wait_ge(vS.h, vS.n)
                vector.wait_ge(tS.h, sumsd[ci])
                vector.drain()
                vS.inc(vector.tensor_copy(sum_row[ci % 2][:, :],
                                          ps_sum[ci % 2][0:1, :]))
            recipd[ci] = vS.n

        def a2_rbctx(ci):
            h, qc = chunks[ci]
            nkt = 4 * qc + 4
            g = h // 4

            @block.tensor
            def _(tensor):
                tensor.wait_ge(vS.h, recipd[ci])
                if ci >= 2:
                    tensor.wait_ge(vS.h, rbevd[ci - 2])
                    tensor.wait_ge(vS.h, ctxevd[ci - 2])
                tS.inc(tensor.matmul(ps_rb[ci % 2][:, :], ones_rf[:, :],
                                     sum_row[ci % 2][0:1, :],
                                     start=True, stop=True))
                rbmmd[ci] = tS.n
                for j in range(nkt):
                    tS.inc(tensor.matmul(
                        ps_ctx[ci % 2][:, :],
                        v_sb[:, j, g * KVC:(g + 1) * KVC],
                        expS[ci % 2][:, j, :],
                        start=(j == 0), stop=(j == nkt - 1)))
                ctxmmd[ci] = tS.n

        def a2_evac(ci):
            h, qc = chunks[ci]

            @block.vector
            def _(vector):
                vector.wait_ge(vS.h, vS.n)
                vector.wait_ge(tS.h, rbmmd[ci])
                vector.drain()
                vS.inc(vector.reciprocal(rb_sb[ci % 2][:, :],
                                         ps_rb[ci % 2][:, :]))
                rbevd[ci] = vS.n
                vector.wait_ge(tS.h, ctxmmd[ci])
                vector.wait_ge(vS.h, vS.n)
                vector.drain()
                vS.inc(vector.tensor_mul(
                    ctxT_sb[:, h, qc * 512:(qc + 1) * 512],
                    ps_ctx[ci % 2][:, :], rb_sb[ci % 2][:, :]))
                ctxevd[ci] = vS.n

        for ci in range(NCH):
            a2_S(ci)
            a2_exp(ci)
            a2_mask(ci)
            if ci == 0 and "es0" in DBG:
                @block.sync
                def _(sync):
                    sync.wait_ge(vS.h, maskd[0])
                    stm.inc(sync.dma_start(
                        out=dbg_t["es0"],
                        in_=expS[0].rearrange("p a b -> p (a b)")), 16)
            if ci >= 1:
                a2_sums(ci - 1)
                a2_recip(ci - 1)
                a2_rbctx(ci - 1)
                a2_evac(ci - 1)
        a2_sums(NCH - 1)
        a2_recip(NCH - 1)
        a2_rbctx(NCH - 1)
        a2_evac(NCH - 1)
        a2_done = ctxevd[NCH - 1]
        a2_last_mm = ctxmmd[NCH - 1]

        if "ctxt" in DBG:
            @block.sync
            def _(sync):
                sync.wait_ge(vS.h, a2_done)
                stm.inc(sync.dma_start(
                    out=dbg_t["ctxt"],
                    in_=ctxT_sb.rearrange("p h s -> p (h s)")), 16)

        esA2.close()
        esD = ExitStack()
        dstage = [sb(f"dstage{i}", [128, D], BF16, esD) for i in range(2)]
        ps_d = [esD.enter_context(nc.psum_tensor(f"ps_d{i}", [128, 4, 512], F32))
                for i in range(2)]

        @block.gpsimd
        def _(gpsimd):
            gpsimd.wait_ge(tS.h, a2_last_mm)
            dB.inc(gpsimd.dma_start(
                out=wdense_sb[:, :, :],
                in_=wdense.rearrange("(h p) d -> p h d", p=128)), 16)
        wdense_ld = dB.n

        dmm_snap = [0] * 16
        dev_snap = [0] * 16
        dst_snap = [0] * 16
        for qt in range(16):
            @block.tensor
            def _(tensor, qt=qt):
                if qt == 0:
                    tensor.wait_ge(dB.h, wdense_ld)
                    tensor.wait_ge(vS.h, a2_done)
                if qt >= 2:
                    tensor.wait_ge(vS.h, dev_snap[qt - 2])
                for ec in range(NH):
                    for nn in range(4):
                        tS.inc(tensor.matmul(
                            ps_d[qt % 2][:, nn, :],
                            ctxT_sb[:, ec, qt * 128:(qt + 1) * 128],
                            wdense_sb[:, ec, nn * 512:(nn + 1) * 512],
                            start=(ec == 0), stop=(ec == NH - 1)))
            dmm_snap[qt] = tS.n

            @block.vector
            def _(vector, qt=qt):
                vector.wait_ge(vS.h, vS.n)
                vector.wait_ge(tS.h, dmm_snap[qt])
                if qt >= 2:
                    vector.wait_ge(tsl[qt % 2].h, dst_snap[qt - 2])
                vS.inc(vector.tensor_copy(
                    dstage[qt % 2][:, :],
                    ps_d[qt % 2].rearrange("p a b -> p (a b)")))
            dev_snap[qt] = vS.n

            @block.sync
            def _(sync, qt=qt):
                sync.wait_ge(vS.h, dev_snap[qt])
                tsl[qt % 2].inc(sync.dma_start(
                    out=rs_in[qt * 128:(qt + 1) * 128, :],
                    in_=dstage[qt % 2][:, :]), 16)
            dst_snap[qt] = tsl[qt % 2].n
        dense_last = dmm_snap[15]
        dense_stores = (tsl[0].n, tsl[1].n)

        @block.gpsimd
        def _(gpsimd):
            gpsimd.wait_ge(tsl[0].h, dense_stores[0])
            gpsimd.wait_ge(tsl[1].h, dense_stores[1])
            ccR.inc(gpsimd.collective_compute(
                "ReduceScatter", ALU.add,
                replica_groups=[[0, 1], [2, 3], [4, 5], [6, 7]],
                ins=[rs_in.ap().opt()], outs=[rs_out.ap().opt()]))
        rs_done = ccR.n

        @block.tensor
        def _(tensor):
            tensor.wait_ge(vS.h, dev_snap[15])
        esD.close()
        esAq.close()

        if "attn" in DBG:
            @block.sync
            def _(sync):
                sync.wait_ge(ccR.h, rs_done)
                stm.inc(sync.dma_start(out=dbg_t["attn"], in_=rs_out[:, :]), 16)

        esL = ExitStack()
        xo_sb = [sb(f"xo{i}", [128, D], F32, esL) for i in range(2)]
        at_sb = [sb(f"at{i}", [128, D], BF16, esL) for i in range(2)]
        h_st = [sb(f"h_st{i}", [128, D], F32, esL) for i in range(2)]
        ln_scr = sb("ln_scr", [128, D], F32, esL)
        g_scr = sb("g_scr", [128, D], F32, esL)
        ln_cols = sb("ln_cols", [128, 8], F32, esL)
        htmp = sb("htmp", [128, 16, 128], F32, esL)
        ps_t1 = [esL.enter_context(nc.psum_tensor(f"ps_t1{i}", [128, 128], F32))
                 for i in range(2)]
        ps_r = esL.enter_context(nc.psum_tensor("ps_r", [128, E], F32))

        ld_snap = [0] * 8
        sig_snap = [0] * 8
        hdone = [0] * 8
        hdma = [0] * 8
        trN = [0] * 8
        htc = [0] * 8
        rmm = [0] * 8
        for tt in range(8):
            @block.scalar
            def _(scalar, tt=tt):
                if tt == 0:
                    scalar.wait_ge(ccR.h, rs_done)
                if tt >= 2:
                    scalar.wait_ge(vS.h, hdone[tt - 2])
                ssl[tt % 2].inc(scalar.dma_start(
                    out=xo_sb[tt % 2][:, :],
                    in_=x_own[tt * 128:(tt + 1) * 128, :]), 16)
                ssl[tt % 2].inc(scalar.dma_start(
                    out=at_sb[tt % 2][:, :],
                    in_=rs_out[tt * 128:(tt + 1) * 128, :]), 16)
            ld_snap[tt] = ssl[tt % 2].n

            @block.vector
            def _(vector, tt=tt):
                vector.wait_ge(vS.h, vS.n)
                V = vector
                V.wait_ge(ssl[tt % 2].h, ld_snap[tt])
                if tt >= 2:
                    V.wait_ge(dsl[2 + tt % 2].h, hdma[tt - 2])
                    V.wait_ge(tS.h, smm_snap[("t1", tt - 2, 15)])
                hs = h_st[tt % 2]
                V.drain()
                V.tensor_copy(g_scr[:, :], at_sb[tt % 2][:, :])
                V.drain()
                vS.inc(V.tensor_add(hs[:, :], g_scr[:, :],
                                    xo_sb[tt % 2][:, :]))
            resid_done = vS.n

            @block.scalar
            def _(scalar, tt=tt, resid_done=resid_done):
                scalar.wait_ge(vS.h, resid_done)
                scalar.wait_ge(sS.h, sS.n)
                hs = h_st[tt % 2]
                sS.inc(scalar.activation(ln_scr[:, :], hs[:, :], AF.Copy,
                                         accum_out=ln_cols[:, 0:1]))
                scalar.wait_ge(sS.h, sS.n)
                sS.inc(scalar.activation(ln_scr[:, :], hs[:, :], AF.Square,
                                         accum_out=ln_cols[:, 2:3]))
            stats_done = sS.n

            @block.vector
            def _(vector, tt=tt, stats_done=stats_done):
                vector.wait_ge(vS.h, vS.n)
                V = vector
                V.wait_ge(sS.h, stats_done)
                V.drain()
                V.tensor_scalar(ln_cols[:, 1:2], ln_cols[:, 0:1], 1.0 / D,
                                None, ALU.mult)
                V.drain()
                V.tensor_tensor(ln_cols[:, 3:4], ln_cols[:, 1:2],
                                ln_cols[:, 1:2], ALU.mult)
                V.drain()
                V.tensor_scalar(ln_cols[:, 4:5], ln_cols[:, 2:3], 1.0 / D,
                                None, ALU.mult)
                V.drain()
                V.tensor_sub(ln_cols[:, 5:6], ln_cols[:, 4:5], ln_cols[:, 3:4])
                V.drain()
                vS.inc(V.tensor_scalar(ln_cols[:, 5:6], ln_cols[:, 5:6], 1.0,
                                       LN_EPS, ALU.mult, ALU.add))
            var_done = vS.n

            @block.scalar
            def _(scalar, var_done=var_done):
                scalar.wait_ge(vS.h, var_done)
                sS.inc(scalar.activation(ln_cols[:, 6:7], ln_cols[:, 5:6],
                                         AF.Sqrt))
            std_done = sS.n

            @block.vector
            def _(vector, tt=tt, std_done=std_done):
                vector.wait_ge(vS.h, vS.n)
                V = vector
                V.wait_ge(sS.h, std_done)
                V.reciprocal(ln_cols[:, 7:8], ln_cols[:, 6:7])
                V.drain()
                vS.inc(V.tensor_scalar(h_st[tt % 2][:, :], h_st[tt % 2][:, :],
                                       ln_cols[:, 1:2], ln_cols[:, 7:8],
                                       ALU.subtract, ALU.mult))
            hdone[tt] = vS.n

            @block.gpsimd
            def _(gpsimd, tt=tt):
                gpsimd.wait_ge(vS.h, hdone[tt])
                dsl[2 + tt % 2].inc(gpsimd.dma_start(
                    out=hag_in[tt * 128:(tt + 1) * 128, :],
                    in_=h_st[tt % 2][:, :]), 16)
            hdma[tt] = dsl[2 + tt % 2].n

            sbase2 = sS.n

            @block.tensor
            def _(tensor, tt=tt, sbase2=sbase2):
                tensor.wait_ge(vS.h, hdone[tt])
                if tt == 0:
                    tensor.wait_ge(tS.h, dense_last)
                if tt >= 1:
                    tensor.wait_ge(sS.h, trN[tt - 1])
                for dc in range(16):
                    if dc >= 2:
                        tensor.wait_ge(sS.h, sbase2 + dc - 1)
                    tS.inc(tensor.transpose(
                        ps_t1[dc % 2][:, :],
                        h_st[tt % 2][:, dc * 128:(dc + 1) * 128], ident_f[:, :]))
                    smm_snap[("t1", tt, dc)] = tS.n

            @block.scalar
            def _(scalar, tt=tt):
                for dc in range(16):
                    scalar.wait_ge(tS.h, smm_snap[("t1", tt, dc)])
                    sS.inc(scalar.copy(htmp[:, dc, :], ps_t1[dc % 2][:, :]))
            trN[tt] = sS.n
            assert trN[tt] == sbase2 + 16

            @block.tensor
            def _(tensor, tt=tt):
                tensor.wait_ge(sS.h, trN[tt])
                if tt == 0:
                    tensor.wait_ge(dsc.h, routerT_ld)
                if tt >= 1:
                    tensor.wait_ge(sS.h, sig_snap[tt - 1])
                for dc in range(16):
                    tS.inc(tensor.matmul(
                        ps_r[:, :], htmp[:, dc, :], routerT_sb[:, dc, :],
                        start=(dc == 0), stop=(dc == 15)))
            rmm[tt] = tS.n

            @block.vector
            def _(vector, tt=tt):
                vector.wait_ge(vS.h, vS.n)
                vector.wait_ge(sS.h, trN[tt])
                vS.inc(vector.tensor_copy(hT_bf[:, :, tt * 128:(tt + 1) * 128],
                                          htmp[:, :, :]))
            htc[tt] = vS.n

            @block.scalar
            def _(scalar, tt=tt):
                scalar.wait_ge(tS.h, rmm[tt])
                sS.inc(scalar.activation(scores_sb[:, tt, :], ps_r[:, :],
                                         AF.Sigmoid))
            sig_snap[tt] = sS.n
        scores_done = sS.n
        h_all_dma = (dsl[2].n, dsl[3].n)
        ht_all = htc[7]
        p4_ts_all = tS.n

        if "h1" in DBG:
            @block.sync
            def _(sync):
                sync.wait_ge(ccH.h, 1)
                stm.inc(sync.dma_start(out=dbg_t["h1"], in_=hag_in[:, :]), 16)

        @block.tensor
        def _(tensor):
            tensor.wait_ge(sS.h, scores_done)
        @block.vector
        def _(vector):
            vector.wait_ge(vS.h, vS.n)
            vector.wait_ge(sS.h, scores_done)
        esL.close()

        guld = {}
        esXw = ExitStack()
        wgbuf = [sb(f"wgbuf{i}", [128, 4, 1024], BF16, esXw) for i in range(3)]

        @block.gpsimd
        def _(gpsimd):
            for k in range(3):
                e, r0 = divmod(k, 8)
                pw, b4 = divmod(r0, 4)
                wsrc = [wg, wu][pw]
                dsl[4 + k % 3].inc(gpsimd.dma_start(
                    out=wgbuf[k % 3][:, :, :],
                    in_=wsrc[e].rearrange("(c p) i -> p c i", p=128)[
                        :, b4 * 4:b4 * 4 + 4, :]), 16)
                guld[(e, pw, b4)] = dsl[4 + k % 3].n

        esR = ExitStack()
        r_sc = sb("r_sc", [128, E], F32, esR)
        r_m1 = sb("r_m1", [128, NGRP], F32, esR)
        r_b32 = sb("r_b32", [128, E], F32, esR)
        r_eq = sb("r_eq", [128, E], F32, esR)
        r_cand = sb("r_cand", [128, E], F32, esR)
        r_m2 = sb("r_m2", [128, NGRP], F32, esR)
        r_gsc = sb("r_gsc", [128, NGRP], F32, esR)
        r_col = sb("r_col", [128, 8], F32, esR)
        r_eq8 = sb("r_eq8", [128, NGRP], F32, esR)
        r_smask = sb("r_smask", [128, E], F32, esR)
        r_msk = sb("r_msk", [128, E], F32, esR)
        r_acc = sb("r_acc", [128, E], F32, esR)
        r_tmp = sb("r_tmp", [128, E], F32, esR)

        @block.vector
        def _(vector):
            vector.wait_ge(vS.h, vS.n)
            vector.wait_ge(sS.h, scores_done)
            vector.wait_ge(dsl[2].h, h_all_dma[0])
            vector.wait_ge(dsl[3].h, h_all_dma[1])
            for tt in range(8):
                V = vector
                V.drain()
                V.tensor_add(r_sc[:, :], scores_sb[:, tt, :], ebias_bc[:, :])
                V.drain()
                V.reduce_max(r_m1[:, :],
                             r_sc.rearrange("p (g e) -> p g e", e=EPG),
                             axis=AX.X)
                V.drain()
                for j in range(EPG):
                    V.tensor_copy(
                        r_b32.rearrange("p (g e) -> p g e", e=EPG)[:, :, j],
                        r_m1[:, :])
                V.drain()
                V.tensor_tensor(r_eq[:, :], r_sc[:, :], r_b32[:, :],
                                ALU.is_equal)
                V.drain()
                V.scalar_tensor_tensor(r_cand[:, :], r_eq[:, :], -BIGI,
                                       iplusb[:, 0:E], ALU.mult, ALU.add)
                V.drain()
                V.tensor_reduce(r_m2[:, :],
                                r_cand.rearrange("p (g e) -> p g e", e=EPG),
                                axis=AX.X, op=ALU.min)
                V.drain()
                for j in range(EPG):
                    V.tensor_copy(
                        r_b32.rearrange("p (g e) -> p g e", e=EPG)[:, :, j],
                        r_m2[:, :])
                V.drain()
                V.tensor_tensor(r_eq[:, :], iota_bc[:, 0:E], r_b32[:, :],
                                ALU.is_equal)
                V.drain()
                V.scalar_tensor_tensor(r_cand[:, :], r_eq[:, :], -BIGM,
                                       r_sc[:, :], ALU.mult, ALU.add)
                V.drain()
                V.reduce_max(r_m2[:, :],
                             r_cand.rearrange("p (g e) -> p g e", e=EPG),
                             axis=AX.X)
                V.drain()
                V.tensor_add(r_gsc[:, :], r_m1[:, :], r_m2[:, :])
                V.drain()
                V.memset(r_smask[:, :], 0.0)
                for it in range(TKG):
                    V.reduce_max(r_col[:, 0:1], r_gsc[:, :], axis=AX.X)
                    V.drain()
                    V.tensor_scalar(r_eq8[:, :], r_gsc[:, :], r_col[:, 0:1],
                                    None, ALU.is_equal)
                    V.drain()
                    V.scalar_tensor_tensor(r_eq8[:, :], r_eq8[:, :], -BIGI,
                                           iplusb[:, 0:NGRP], ALU.mult,
                                           ALU.add)
                    V.drain()
                    V.tensor_reduce(r_col[:, 1:2], r_eq8[:, :], axis=AX.X,
                                    op=ALU.min)
                    V.drain()
                    V.tensor_scalar(r_eq8[:, :], iplusb[:, 0:NGRP],
                                    r_col[:, 1:2], BIGI, ALU.subtract,
                                    ALU.is_equal)
                    V.drain()
                    V.scalar_tensor_tensor(r_gsc[:, :], r_eq8[:, :], -BIGM,
                                           r_gsc[:, :], ALU.mult, ALU.add)
                    V.tensor_scalar(r_tmp[:, :], grpidx_bc[:, :],
                                    r_col[:, 1:2], None, ALU.is_equal)
                    V.drain()
                    V.tensor_add(r_smask[:, :], r_smask[:, :], r_tmp[:, :])
                    V.drain()
                V.tensor_mul(r_msk[:, :], r_sc[:, :], r_smask[:, :])
                V.drain()
                V.memset(r_acc[:, :], 0.0)
                for kk in range(TOPK):
                    V.reduce_max(r_col[:, 2:3], r_msk[:, :], axis=AX.X)
                    V.drain()
                    V.tensor_scalar(r_eq[:, :], r_msk[:, :], r_col[:, 2:3],
                                    None, ALU.is_equal)
                    V.drain()
                    V.scalar_tensor_tensor(r_cand[:, :], r_eq[:, :], -BIGI,
                                           iplusb[:, 0:E], ALU.mult, ALU.add)
                    V.drain()
                    V.tensor_reduce(idxk_sb[:, tt, kk:kk + 1], r_cand[:, :],
                                    axis=AX.X, op=ALU.min)
                    V.drain()
                    V.tensor_scalar(r_eq[:, :], iplusb[:, 0:E],
                                    idxk_sb[:, tt, kk:kk + 1], BIGI,
                                    ALU.subtract, ALU.is_equal)
                    V.drain()
                    V.tensor_mul(r_tmp[:, :], r_eq[:, :], scores_sb[:, tt, :])
                    V.drain()
                    V.reduce_sum(wtop_sb[:, tt, kk:kk + 1], r_tmp[:, :],
                                 axis=AX.X)
                    V.tensor_add(r_acc[:, :], r_acc[:, :], r_eq[:, :])
                    V.drain()
                    V.scalar_tensor_tensor(r_msk[:, :], r_eq[:, :], -BIGM,
                                           r_msk[:, :], ALU.mult, ALU.add)
                    V.drain()
                V.reduce_sum(r_col[:, 3:4], wtop_sb[:, tt, :], axis=AX.X)
                V.drain()
                V.tensor_scalar(r_col[:, 3:4], r_col[:, 3:4], 1.0, 1e-20,
                                ALU.mult, ALU.add)
                V.drain()
                V.reciprocal(r_col[:, 4:5], r_col[:, 3:4])
                V.drain()
                V.tensor_scalar(wtop_sb[:, tt, :], wtop_sb[:, tt, :],
                                r_col[:, 4:5], SCALE_W, ALU.mult, ALU.mult)
                vS.inc(V.tensor_copy(A_loc[:, tt, :], r_acc[:, :]))
        topk_done = vS.n

        if "scores" in DBG:
            @block.sync
            def _(sync):
                sync.wait_ge(vS.h, topk_done)
                stm.inc(sync.dma_start(
                    out=dbg_t["scores"],
                    in_=scores_sb.rearrange("p a b -> p (a b)")), 16)

        esS7 = ExitStack()
        sg_buf = [sb(f"sg_buf{i}", [128, 16, 128], BF16, esS7) for i in range(2)]
        su_buf = [sb(f"su_buf{i}", [128, 16, 128], BF16, esS7) for i in range(2)]
        gs_scr = sb("gs_scr", [128, TPC], F32, esS7)
        us_scr = sb("us_scr", [128, TPC], F32, esS7)
        ps_G = [esS7.enter_context(nc.psum_tensor(f"ps_G{i}", [128, 2, 512], F32))
                for i in range(2)]
        ps_U = [esS7.enter_context(nc.psum_tensor(f"ps_U{i}", [128, 2, 512], F32))
                for i in range(2)]
        sgld = [0] * 16
        shmm = [0] * 16
        shev = [0] * 16

        def emit_sgld(jc):
            @block.gpsimd
            def _(gpsimd, jc=jc):
                if jc >= 2:
                    gpsimd.wait_ge(tS.h, shmm[jc - 2])
                else:
                    gpsimd.wait_ge(tS.h, p4_ts_all)
                    gpsimd.wait_ge(dsl[2].h, h_all_dma[0])
                    gpsimd.wait_ge(dsl[3].h, h_all_dma[1])
                dsl[jc % 2].inc(gpsimd.dma_start(
                    out=sg_buf[jc % 2][:, :, :],
                    in_=sgT[jc].rearrange("(kc p) j -> p kc j", p=128)), 16)
                dsl[jc % 2].inc(gpsimd.dma_start(
                    out=su_buf[jc % 2][:, :, :],
                    in_=suT[jc].rearrange("(kc p) j -> p kc j", p=128)), 16)
            sgld[jc] = dsl[jc % 2].n

        emit_sgld(0)
        emit_sgld(1)

        @block.gpsimd
        def _(gpsimd):
            gpsimd.wait_ge(dsl[2].h, h_all_dma[0])
            gpsimd.wait_ge(dsl[3].h, h_all_dma[1])
            gpsimd.wait_ge(dsl[0].h, sgld[0])
            gpsimd.wait_ge(dsl[1].h, sgld[1])
            ccH.inc(gpsimd.collective_compute(
                "AllGather", ALU.bypass, replica_groups=[list(range(NCORE))],
                ins=[hag_in.ap().opt()], outs=[hag_out.ap().opt()]))
        hag_done = ccH.n


        for jc in range(16):
            if jc >= 2:
                emit_sgld(jc)

            @block.tensor
            def _(tensor, jc=jc):
                tensor.wait_ge(dsl[jc % 2].h, sgld[jc])
                if jc == 0:
                    tensor.wait_ge(vS.h, ht_all)
                    tensor.wait_ge(tS.h, a2_last_mm)
                if jc >= 2:
                    tensor.wait_ge(vS.h, shev[jc - 2])
                for dc in range(16):
                    for nn in range(2):
                        tS.inc(tensor.matmul(
                            ps_G[jc % 2][:, nn, :], sg_buf[jc % 2][:, dc, :],
                            hT_bf[:, dc, nn * 512:(nn + 1) * 512],
                            start=(dc == 0), stop=(dc == 15)))
                        tS.inc(tensor.matmul(
                            ps_U[jc % 2][:, nn, :], su_buf[jc % 2][:, dc, :],
                            hT_bf[:, dc, nn * 512:(nn + 1) * 512],
                            start=(dc == 0), stop=(dc == 15)))
            shmm[jc] = tS.n

            @block.scalar
            def _(scalar, jc=jc):
                scalar.wait_ge(tS.h, shmm[jc])
                if jc >= 1:
                    scalar.wait_ge(vS.h, shev[jc - 1])
                sS.inc(scalar.activation(
                    gs_scr[:, :], ps_G[jc % 2].rearrange("p a b -> p (a b)"),
                    AF.Silu))
            silu_d = sS.n

            @block.vector
            def _(vector, jc=jc, silu_d=silu_d):
                vector.wait_ge(vS.h, vS.n)
                vector.wait_ge(sS.h, silu_d)
                vector.drain()
                vector.tensor_copy(us_scr[:, :],
                                   ps_U[jc % 2].rearrange("p a b -> p (a b)"))
                vector.drain()
                vS.inc(vector.tensor_mul(HshT[:, jc, :], gs_scr[:, :],
                                         us_scr[:, :]))
            shev[jc] = vS.n
        shared_gu_done = shev[15]
        shared_gu_mm = shmm[15]

        @block.tensor
        def _(tensor):
            tensor.wait_ge(vS.h, shared_gu_done)
        esS7.close()

        if "hsh" in DBG:
            @block.sync
            def _(sync):
                sync.wait_ge(vS.h, shared_gu_done)
                stm.inc(sync.dma_start(
                    out=dbg_t["hsh"],
                    in_=HshT.rearrange("p a b -> p (a b)")), 16)

        esG = ExitStack()
        lb_bc = sb("lb_bc", [128, 8, E], F32, esG)
        lb_flat = sb("lb_flat", [1, 8 * E], F32, esG)
        bm_row = sb("bm_row", [1, E], F32, esG)
        lcs = sb("lcs", [E, 8], F32, esG)
        lcs2 = sb("lcs2", [E, 8], F32, esG)
        lexcl = sb("lexcl", [E, 8], F32, esG)
        lbT = sb("lbT", [8, E], F32, esG)
        cnt_row = sb("cnt_row", [1, E], F32, esG)
        cnt_all = sb("cnt_all", [8, E], F32, esG)
        g_t0 = sb("g_t0", [128, E], F32, esG)
        A_locf = sb("A_locf", [128, 8, E], F32, esG)
        g_sel = sb("g_sel", [128, 8, E], F32, esG)
        hl_tmp = [sb(f"hl_tmp{i}", [2, EPC, CP], F32, esG) for i in range(2)]
        hlch = sb("hlch", [128, 4, CP], F32, esG)

        esGp1 = ExitStack()
        ps_cs = esGp1.enter_context(nc.psum_tensor("ps_cs", [E, 64], F32))
        ps_tr = esGp1.enter_context(nc.psum_tensor("ps_tr", [64, E], F32))
        ps_bb = [esGp1.enter_context(nc.psum_tensor(f"ps_bb{i}", [128, E], F32))
                 for i in range(2)]
        ps_bm = esGp1.enter_context(nc.psum_tensor("ps_bm", [1, E], F32))
        ps_cr = esGp1.enter_context(nc.psum_tensor("ps_cr", [1, E], F32))

        @block.tensor
        def _(tensor):
            tensor.wait_ge(vS.h, topk_done)
            for m in range(8):
                tS.inc(tensor.matmul(ps_cs[:, m:m + 1], A_loc[:, m, :],
                                     ones_cb[:, :], start=True, stop=True))
        lcs_mm = tS.n

        @block.vector
        def _(vector):
            vector.wait_ge(vS.h, vS.n)
            V = vector
            V.wait_ge(tS.h, lcs_mm)
            V.drain()
            V.tensor_copy(lcs[:, :], ps_cs[:, 0:8])
            V.drain()
            bufs = [lcs, lcs2]
            cur = 0
            sh = 1
            while sh < 8:
                a, b = bufs[cur], bufs[1 - cur]
                V.tensor_copy(b[:, 0:sh], a[:, 0:sh])
                V.tensor_add(b[:, sh:8], a[:, sh:8], a[:, 0:8 - sh])
                V.drain()
                cur = 1 - cur
                sh *= 2
            incl = bufs[cur]
            V.memset(lexcl[:, 0:1], 0.0)
            V.tensor_copy(lexcl[:, 1:8], incl[:, 0:7])
            vS.inc(V.tensor_copy(lcs[:, 0:1], incl[:, 7:8]))
        lpre_done = vS.n

        @block.tensor
        def _(tensor):
            tensor.wait_ge(vS.h, lpre_done)
            tS.inc(tensor.transpose(ps_tr[0:8, :], lexcl[:, 0:8],
                                    ident_f[0:E, 0:E]))
            tS.inc(tensor.transpose(ps_cr[0:1, :], lcs[:, 0:1],
                                    ident_f[0:E, 0:E]))
        ltr = tS.n

        @block.vector
        def _(vector):
            vector.wait_ge(vS.h, vS.n)
            vector.wait_ge(tS.h, ltr)
            vector.tensor_copy(lbT[:, :], ps_tr[0:8, :])
            vS.inc(vector.tensor_copy(cnt_row[:, :], ps_cr[0:1, :]))
        lbT_d = vS.n

        @block.sync
        def _(sync):
            sync.wait_ge(vS.h, lbT_d)
            stm.inc(sync.dma_start(out=cnt_in[:, :], in_=cnt_row[:, :]), 16)
            w0 = dl.n
            dl.inc(sync.dma_start(
                out=lscr.rearrange("o (n e) -> (o n) e", e=E),
                in_=lbT[:, :]), 16)
            sync.wait_ge(dl.h, w0 + 16)
            dl.inc(sync.dma_start(out=lb_flat[0:1, :], in_=lscr[:, :]), 16)
        cnt_store = stm.n
        lb_flat_d = dl.n

        @block.gpsimd
        def _(gpsimd):
            gpsimd.wait_ge(stm.h, cnt_store)
            ccA.inc(gpsimd.collective_compute(
                "AllGather", ALU.bypass, replica_groups=[list(range(NCORE))],
                ins=[cnt_in.ap().opt()], outs=[cnt_out.ap().opt()]))
        cnt_ag = ccA.n

        @block.scalar
        def _(scalar):
            scalar.wait_ge(ccA.h, cnt_ag)
            dsc.inc(scalar.dma_start(out=cnt_all[:, :], in_=cnt_out[:, :]), 16)
        cnt_ld = dsc.n

        @block.tensor
        def _(tensor):
            tensor.wait_ge(dsc.h, cnt_ld)
            tS.inc(tensor.matmul(ps_bm[0:1, :], cmask8[:, :], cnt_all[:, :],
                                 start=True, stop=True))
        bm_mm = tS.n

        @block.vector
        def _(vector):
            vector.wait_ge(vS.h, vS.n)
            vector.wait_ge(tS.h, bm_mm)
            vS.inc(vector.tensor_copy(bm_row[:, :], ps_bm[0:1, :]))
        bm_done = vS.n

        lb_ev = [0] * 8
        for m in range(8):
            @block.tensor
            def _(tensor, m=m):
                if m == 0:
                    tensor.wait_ge(dl.h, lb_flat_d)
                else:
                    tensor.wait_ge(vS.h, lb_ev[max(0, m - 2)])
                tensor.wait_ge(vS.h, bm_done)
                tS.inc(tensor.matmul(ps_bb[m % 2][:, :], ones_rf[:, :],
                                     lb_flat[0:1, m * E:(m + 1) * E],
                                     start=True, stop=False))
                tS.inc(tensor.matmul(ps_bb[m % 2][:, :], ones_rf[:, :],
                                     bm_row[0:1, :], start=False, stop=True))
                smm_snap[("lb", m)] = tS.n

            @block.vector
            def _(vector, m=m):
                vector.wait_ge(vS.h, vS.n)
                vector.wait_ge(tS.h, smm_snap[("lb", m)])
                vS.inc(vector.tensor_copy(lb_bc[:, m, :], ps_bb[m % 2][:, :]))
            lb_ev[m] = vS.n

        @block.tensor
        def _(tensor):
            tensor.wait_ge(vS.h, lb_ev[7])
        esGp1.close()

        esGp2 = ExitStack()
        ps_p = [esGp2.enter_context(nc.psum_tensor(f"ps_p{i}", [128, E], F32))
                for i in range(2)]
        ps_sl = [esGp2.enter_context(nc.psum_tensor(f"ps_sl{i}", [2, CP], F32))
                 for i in range(EPC)]
        ps_my = esGp2.enter_context(nc.psum_tensor("ps_my", [128, 128], F32))
        Ecmp = sb("Ecmp", [128, 2, EPC, CP], BF16, esG)
        sl_sb = sb("sl_sb", [2, EPC * CP], F32, esG)
        trs_sb = sb("trs_sb", [128, 2], F32, esG)

        pm_ev = [0] * 8
        for m in range(8):
            @block.tensor
            def _(tensor, m=m):
                if m >= 2:
                    tensor.wait_ge(vS.h, pm_ev[m - 2])
                tS.inc(tensor.matmul(ps_p[m % 2][:, :], lstrict[:, :],
                                     A_loc[:, m, :], start=True, stop=True))
            smm_snap[("pm", m)] = tS.n

            @block.vector
            def _(vector, m=m):
                vector.wait_ge(vS.h, vS.n)
                vector.wait_ge(tS.h, smm_snap[("pm", m)])
                vector.tensor_copy(g_t0[:, :], ps_p[m % 2][:, :])
                vector.drain()
                vS.inc(vector.tensor_add(pos_mine[:, m, :], g_t0[:, :],
                                         lb_bc[:, m, :]))
            pm_ev[m] = vS.n
        pos_mine_d = pm_ev[7]

        if "posm" in DBG:
            @block.sync
            def _(sync):
                sync.wait_ge(vS.h, pos_mine_d)
                stm.inc(sync.dma_start(
                    out=dbg_t["posm"],
                    in_=pos_mine.rearrange("p a b -> p (a b)")), 16)

        @block.vector
        def _(vector):
            vector.wait_ge(vS.h, vS.n)
            vector.wait_ge(vS.h, pos_mine_d)
            for tt in range(8):
                V = vector
                V.drain()
                V.tensor_add(r_msk[:, :], pos_mine[:, tt, :], sbase_bc[:, :])
                V.drain()
                for kk in range(TOPK):
                    V.wait_ge(vS.h, vS.n)
                    V.tensor_scalar(r_eq[:, :], iplusb[:, 0:E],
                                    idxk_sb[:, tt, kk:kk + 1], BIGI,
                                    ALU.subtract, ALU.is_equal)
                    V.drain()
                    V.tensor_mul(r_tmp[:, :], r_eq[:, :], pos_mine[:, tt, :])
                    V.drain()
                    V.reduce_sum(r_col[:, 0:1], r_tmp[:, :], axis=AX.X)
                    V.drain()
                    V.tensor_mul(r_tmp[:, :], r_eq[:, :], r_msk[:, :])
                    V.drain()
                    V.reduce_sum(r_col[:, 3:4], r_tmp[:, :], axis=AX.X)
                    V.drain()
                    V.tensor_scalar(r_col[:, 1:2], r_col[:, 0:1], float(CCAP),
                                    None, ALU.is_lt)
                    V.drain()
                    V.tensor_tensor(weff_sb[:, tt, kk:kk + 1],
                                    wtop_sb[:, tt, kk:kk + 1], r_col[:, 1:2],
                                    ALU.mult)
                    V.tensor_mul(r_col[:, 3:4], r_col[:, 3:4], r_col[:, 1:2])
                    V.drain()
                    V.tensor_scalar(r_col[:, 3:4], r_col[:, 3:4],
                                    float(NCORE * EPC * CCAP - 1), 0.0,
                                    ALU.min, ALU.max)
                    V.drain()
                    vS.inc(V.tensor_copy(sidx_tok[:, tt, kk:kk + 1],
                                         r_col[:, 3:4]))
        tokp_done = vS.n

        if "weff" in DBG:
            @block.sync
            def _(sync):
                sync.wait_ge(vS.h, tokp_done)
                stm.inc(sync.dma_start(
                    out=dbg_t["weff"],
                    in_=weff_sb.rearrange("p a b -> p (a b)")), 16)
        if "stok" in DBG:
            @block.sync
            def _(sync):
                sync.wait_ge(vS.h, tokp_done)
                stm.inc(sync.dma_start(
                    out=dbg_t["stok"],
                    in_=sidx_tok.rearrange("p a b -> p (a b)")), 16)

        @block.vector
        def _(vector):
            vector.wait_ge(vS.h, vS.n)
            vector.wait_ge(vS.h, pos_mine_d)
            vector.tensor_copy(A_locf.rearrange("p a b -> p (a b)"),
                               A_loc.rearrange("p a b -> p (a b)"))
            vector.drain()
            for m in range(8):
                vS.inc(vector.scalar_tensor_tensor(
                    g_sel[:, m, :], pos_mine[:, m, :], 1.0, A_locf[:, m, :],
                    ALU.add, ALU.mult))
        gsel_done = vS.n

        ecs = {}
        hlmm = {}
        hlev = [0] * 8
        hlst = [0] * 8
        for g in range(8):
            for m in range(8):
                @block.vector
                def _(vector, g=g, m=m):
                    vector.wait_ge(vS.h, vS.n)
                    if g == 0 and m == 0:
                        vector.wait_ge(vS.h, gsel_done)
                    km = g * 8 + m
                    if km >= 2:
                        vector.wait_ge(tS.h, hlmm[km - 2])
                    for i in range(EPC):
                        vS.inc(vector.tensor_scalar(
                            Ecmp[:, m % 2, i, :], iotaP1[:, :],
                            g_sel[:, m, 4 * g + i:4 * g + i + 1], None,
                            ALU.is_equal))
                ecs[(g, m)] = vS.n

                @block.tensor
                def _(tensor, g=g, m=m):
                    tensor.wait_ge(vS.h, ecs[(g, m)])
                    if m == 0 and g > 0:
                        tensor.wait_ge(vS.h, hlev[g - 1])
                    for i in range(EPC):
                        tS.inc(tensor.matmul(
                            ps_sl[i][:, :], hiloown[:, 2 * m:2 * m + 2],
                            Ecmp[:, m % 2, i, :],
                            start=(m == 0), stop=(m == 7)))
                hlmm[g * 8 + m] = tS.n

            @block.vector
            def _(vector, g=g):
                vector.wait_ge(vS.h, vS.n)
                vector.wait_ge(tS.h, hlmm[g * 8 + 7])
                if g >= 2:
                    vector.wait_ge(hss[g % 2].h, hlst[g - 2])
                for i in range(EPC):
                    vS.inc(vector.tensor_copy(hl_tmp[g % 2][:, i, :],
                                              ps_sl[i][:, :]))
            hlev[g] = vS.n

            @block.sync
            def _(sync, g=g):
                sync.wait_ge(vS.h, hlev[g])
                hss[g % 2].inc(sync.dma_start(
                    out=hl_in[8 * g:8 * g + 8, :].rearrange(
                        "(i h) s -> h i s", h=2),
                    in_=hl_tmp[g % 2][:, :, :]), 16)
            hlst[g] = hss[g % 2].n
        hl_store = (hss[0].n, hss[1].n)

        @block.gpsimd
        def _(gpsimd):
            gpsimd.wait_ge(hss[0].h, hl_store[0])
            gpsimd.wait_ge(hss[1].h, hl_store[1])
            ccL.inc(gpsimd.collective_compute(
                "AllGather", ALU.bypass, replica_groups=[list(range(NCORE))],
                ins=[hl_in.ap().opt()], outs=[hl_out.ap().opt()]))
        hl_ag = ccL.n

        @block.scalar
        def _(scalar):
            scalar.wait_ge(ccL.h, hl_ag)
            dsc.inc(scalar.dma_start(
                out=hlch[:, :, :],
                in_=hl_out.rearrange("(k p) s -> p k s", p=128)), 16)
        hl_ld = dsc.n

        @block.tensor
        def _(tensor):
            tensor.wait_ge(dsc.h, hl_ld)
            for i in range(EPC):
                for k in range(4):
                    tS.inc(tensor.matmul(ps_sl[i][:, :], selhl[:, 2 * i:2 * i + 2],
                                         hlch[:, k, :],
                                         start=(k == 0), stop=(k == 3)))
        h8_mm = tS.n

        @block.vector
        def _(vector):
            vector.wait_ge(vS.h, vS.n)
            vector.wait_ge(tS.h, h8_mm)
            for i in range(EPC):
                vS.inc(vector.tensor_copy(sl_sb[:, i * CP:(i + 1) * CP],
                                          ps_sl[i][:, :]))
        sl_ev = vS.n

        trs_snap = {}
        prev_trs = 0
        for i in range(EPC):
            for ch in range(3):
                @block.tensor
                def _(tensor, i=i, ch=ch, prev_trs=prev_trs):
                    if i == 0 and ch == 0:
                        tensor.wait_ge(vS.h, sl_ev)
                    else:
                        tensor.wait_ge(vS.h, prev_trs)
                    tS.inc(tensor.transpose(
                        ps_my[:, 0:2],
                        sl_sb[:, i * CP + ch * 128:i * CP + (ch + 1) * 128],
                        ident_f[0:2, 0:2]))
                    smm_snap[("trs", i, ch)] = tS.n

                @block.vector
                def _(vector, i=i, ch=ch):
                    vector.wait_ge(vS.h, vS.n)
                    V = vector
                    V.wait_ge(tS.h, smm_snap[("trs", i, ch)])
                    V.drain()
                    V.tensor_copy(trs_sb[:, :], ps_my[:, 0:2])
                    V.drain()
                    V.scalar_tensor_tensor(
                        trs_sb[:, 0:1], trs_sb[:, 0:1], 64.0, trs_sb[:, 1:2],
                        ALU.mult, ALU.add)
                    V.drain()
                    V.tensor_scalar(trs_sb[:, 0:1], trs_sb[:, 0:1],
                                    float(T - 1), 0.0, ALU.min, ALU.max)
                    V.drain()
                    vS.inc(V.tensor_copy(
                        sidx_exp[:, i * 3 + ch:i * 3 + ch + 1],
                        trs_sb[:, 0:1]))
                trs_snap[(i, ch)] = vS.n
                prev_trs = vS.n
        sidx_done = prev_trs

        if "sidx" in DBG:
            @block.sync
            def _(sync):
                sync.wait_ge(vS.h, sidx_done)
                stm.inc(sync.dma_start(out=dbg_t["sidx"], in_=sidx_exp[:, :]),
                        16)

        @block.tensor
        def _(tensor):
            tensor.wait_ge(vS.h, sidx_done)
            tensor.wait_ge(vS.h, tokp_done)
        esGp2.close()
        esG.close()
        esR.close()

        esX = ExitStack()
        xp_tok = sb("xp_tok", [128, 2, D], BF16, esX)
        wd_sb = a1[:, 0:16384].rearrange("p (a b) -> p a b", b=2048)
        gsxa = sb("gsxa", [128, 3, 1024], F32, esX)
        Hx = sb("Hx", [128, 3, 1024], BF16, esX)
        HTx = sb("HTx", [128, 8, 128], BF16, esX)
        ystage = [sb(f"ystage{i}", [128, D], BF16, esX) for i in range(2)]

        xg_snap = {}
        xpt_ev = {}
        kbatch_mm = {}
        wdld = {}
        gu_done = {}
        hx_done = {}
        silu_all = {}
        ymm = {}
        yev = {}
        yst = {}
        for e in range(EPC):
            esXt = ExitStack()
            ps_sm = [esXt.enter_context(
                nc.psum_tensor(f"ps_sm{e}_{i}", [128, 128], BF16))
                for i in range(2)]
            for ch in range(3):
                @block.gpsimd
                def _(gpsimd, e=e, ch=ch):
                    if e == 0 and ch == 0:
                        gpsimd.wait_ge(vS.h, sidx_done)
                        gpsimd.wait_ge(ccH.h, hag_done)
                    idx = 3 * e + ch
                    if idx >= 2:
                        pe, pch = divmod(idx - 2, 3)
                        gpsimd.wait_ge(vS.h, xpt_ev[(pe, pch, 15)])
                    gsl[(3 * e + ch) % 2].inc(gpsimd.indirect_dma_start(
                        out=xp_tok[:, (3 * e + ch) % 2, :], out_offset=None,
                        in_=hag_out[0:T, :],
                        in_offset=IndirectOffsetOnAxis(
                            sidx_exp[:, e * 3 + ch:e * 3 + ch + 1], 0)), 16)
                xg_snap[(e, ch)] = gsl[(3 * e + ch) % 2].n

                vbase = vS.n

                @block.tensor
                def _(tensor, e=e, ch=ch, vbase=vbase):
                    tensor.wait_ge(gsl[(3 * e + ch) % 2].h, xg_snap[(e, ch)])
                    if e == 0 and ch == 0:
                        tensor.wait_ge(tS.h, dense_last)
                    if e > 0 and ch == 0:
                        tensor.wait_ge(vS.h, yev[(e - 1, 2)])
                    for dc in range(16):
                        if dc < 2 and (3 * e + ch) >= 1:
                            pe2, pch2 = divmod(3 * e + ch - 1, 3)
                            tensor.wait_ge(vS.h, xpt_ev[(pe2, pch2, 14 + dc)])
                        if dc >= 2:
                            tensor.wait_ge(vS.h, vbase + dc - 1)
                        tS.inc(tensor.transpose(
                            ps_sm[dc % 2][:, :],
                            xp_tok[:, (3 * e + ch) % 2, dc * 128:(dc + 1) * 128],
                            ident_bf[:, :]))
                        smm_snap[("xt", e, ch, dc)] = tS.n

                @block.vector
                def _(vector, e=e, ch=ch):
                    vector.wait_ge(vS.h, vS.n)
                    for dc in range(16):
                        vector.wait_ge(tS.h, smm_snap[("xt", e, ch, dc)])
                        vS.inc(vector.tensor_copy(
                            XpT[:, dc, ch * 128:(ch + 1) * 128],
                            ps_sm[dc % 2][:, :]))
                        xpt_ev[(e, ch, dc)] = vS.n
                assert xpt_ev[(e, ch, 15)] == vbase + 16
            xpt_done = xpt_ev[(e, 2, 15)]
            esXt.close()

            @block.gpsimd
            def _(gpsimd, e=e):
                if e == 0:
                    gpsimd.wait_ge(tS.h, shared_gu_mm)
                else:
                    gpsimd.wait_ge(tS.h, ymm[(e - 1, 2)])
                wdv = wd[e].rearrange("(c p) d -> p c d", p=128)
                dB.inc(gpsimd.dma_start(out=wd_sb[:, 0:4, :],
                                        in_=wdv[:, 0:4, :]), 16)
                dB.inc(gpsimd.dma_start(out=wd_sb[:, 4:8, :],
                                        in_=wdv[:, 4:8, :]), 16)
            wdld[e] = dB.n

            esXp = ExitStack()
            ps_gu = [esXp.enter_context(
                nc.psum_tensor(f"ps_gu{e}_{st}", [128, 2, 512], F32))
                for st in range(3)]
            for pw, wsrc in ((0, wg), (1, wu)):
                for b4 in range(4):
                    k = (e * 2 + pw) * 4 + b4
                    if k >= 3:
                        @block.gpsimd
                        def _(gpsimd, e=e, pw=pw, b4=b4, k=k, wsrc=wsrc):
                            gpsimd.wait_ge(tS.h, kbatch_mm[k - 3])
                            dsl[4 + k % 3].inc(gpsimd.dma_start(
                                out=wgbuf[k % 3][:, :, :],
                                in_=wsrc[e].rearrange(
                                    "(c p) i -> p c i", p=128)[
                                    :, b4 * 4:b4 * 4 + 4, :]), 16)
                        guld[(e, pw, b4)] = dsl[4 + k % 3].n

                    @block.tensor
                    def _(tensor, e=e, pw=pw, b4=b4, k=k, xpt_done=xpt_done):
                        tensor.wait_ge(dsl[4 + k % 3].h, guld[(e, pw, b4)])
                        if b4 == 0:
                            tensor.wait_ge(vS.h, xpt_done)
                            if pw == 0 and e > 0:
                                tensor.wait_ge(vS.h, yev[(e - 1, 2)])
                            if pw == 1:
                                tensor.wait_ge(sS.h, silu_all[e])
                        for c4 in range(4):
                            dc = b4 * 4 + c4
                            for st in range(3):
                                sl = slice(st * 128, (st + 1) * 128)
                                for nn in range(2):
                                    tS.inc(tensor.matmul(
                                        ps_gu[st][:, nn, :], XpT[:, dc, sl],
                                        wgbuf[k % 3][:, c4, nn * 512:(nn + 1) * 512],
                                        start=(dc == 0), stop=(dc == 15)))
                    kbatch_mm[k] = tS.n
                    gu_done[(e, pw, b4)] = tS.n

                if pw == 0:
                    for st in range(3):
                        @block.scalar
                        def _(scalar, e=e, st=st):
                            scalar.wait_ge(tS.h, gu_done[(e, 0, 3)])
                            if e > 0 and st == 0:
                                scalar.wait_ge(vS.h, hx_done[(e - 1, 2)])
                            sS.inc(scalar.activation(
                                gsxa[:, st, :],
                                ps_gu[st].rearrange("p a b -> p (a b)"),
                                AF.Silu))
                    silu_all[e] = sS.n
                else:
                    for st in range(3):
                        @block.vector
                        def _(vector, e=e, st=st):
                            vector.wait_ge(vS.h, vS.n)
                            vector.wait_ge(tS.h, gu_done[(e, 1, 3)])
                            vector.wait_ge(sS.h, silu_all[e])
                            vector.drain()
                            vS.inc(vector.tensor_mul(
                                Hx[:, st, :], gsxa[:, st, :],
                                ps_gu[st].rearrange("p a b -> p (a b)")))
                        hx_done[(e, st)] = vS.n
            esXp.close()

            esXy = ExitStack()
            ps_eY = esXy.enter_context(
                nc.psum_tensor(f"ps_eY{e}", [128, 4, 512], F32))
            ps_sm = [esXy.enter_context(
                nc.psum_tensor(f"ps_smy{e}_{i}", [128, 128], BF16))
                for i in range(2)]
            for st in range(3):
                vbase = vS.n

                @block.tensor
                def _(tensor, e=e, st=st, vbase=vbase):
                    tensor.wait_ge(vS.h, hx_done[(e, st)])
                    if st == 0:
                        tensor.wait_ge(vS.h, hx_done[(e, 2)])
                        k = (e - 1, 2)
                        if k in yev:
                            tensor.wait_ge(vS.h, yev[k])
                    for ic in range(8):
                        if ic >= 2:
                            tensor.wait_ge(vS.h, vbase + ic - 1)
                        tS.inc(tensor.transpose(
                            ps_sm[ic % 2][:, :], Hx[:, st, ic * 128:(ic + 1) * 128],
                            ident_bf[:, :]))
                        smm_snap[("ht", e, st, ic)] = tS.n

                @block.vector
                def _(vector, e=e, st=st):
                    vector.wait_ge(vS.h, vS.n)
                    for ic in range(8):
                        vector.wait_ge(tS.h, smm_snap[("ht", e, st, ic)])
                        vS.inc(vector.tensor_copy(HTx[:, ic, :],
                                                  ps_sm[ic % 2][:, :]))
                ht_d = vS.n
                assert ht_d == vbase + 8

                @block.tensor
                def _(tensor, e=e, st=st, ht_d=ht_d):
                    tensor.wait_ge(vS.h, ht_d)
                    tensor.wait_ge(dB.h, wdld[e])
                    for ic in range(8):
                        for nn in range(4):
                            tS.inc(tensor.matmul(
                                ps_eY[:, nn, :], HTx[:, ic, :],
                                wd_sb[:, ic, nn * 512:(nn + 1) * 512],
                                start=(ic == 0), stop=(ic == 7)))
                ymm[(e, st)] = tS.n

                @block.vector
                def _(vector, e=e, st=st):
                    vector.wait_ge(vS.h, vS.n)
                    vector.wait_ge(tS.h, ymm[(e, st)])
                    k = ("st", e, st - 2) if st >= 2 else ("st", e - 1, st + 1)
                    if k in yst:
                        vector.wait_ge(tsl[(3 * (k[1]) + k[2]) % 2].h, yst[k])
                    vS.inc(vector.tensor_copy(
                        ystage[(3 * e + st) % 2][:, :],
                        ps_eY.rearrange("p a b -> p (a b)")))
                yev[(e, st)] = vS.n

                @block.sync
                def _(sync, e=e, st=st):
                    sync.wait_ge(vS.h, yev[(e, st)])
                    nrows = 128 if st < 2 else 64
                    tsl[(3 * e + st) % 2].inc(sync.dma_start(
                        out=yag_in[e * CCAP + st * 128:
                                   e * CCAP + st * 128 + nrows, :],
                        in_=ystage[(3 * e + st) % 2][0:nrows, :]), 16)
                yst[("st", e, st)] = tsl[(3 * e + st) % 2].n

            @block.tensor
            def _(tensor, e=e):
                tensor.wait_ge(vS.h, yev[(e, 2)])
            esXy.close()
        y_stores = (tsl[0].n, tsl[1].n)
        exp_gu_last = gu_done[(EPC - 1, 1, 3)]

        for i in range(EPC):
            @block.gpsimd
            def _(gpsimd, i=i):
                for st in range(3):
                    par = (3 * i + st) % 2
                    gpsimd.wait_ge(tsl[par].h, yst[("st", i, st)])
                ccY.inc(gpsimd.collective_compute(
                    "AllGather", ALU.bypass,
                    replica_groups=[list(range(NCORE))],
                    ins=[yag_in[i * CCAP:(i + 1) * CCAP, :].opt()],
                    outs=[yag_out[i * NCORE * CCAP:(i + 1) * NCORE * CCAP, :].opt()]))
        yag_done = ccY.n

        @block.tensor
        def _(tensor):
            tensor.wait_ge(vS.h, yev[(EPC - 1, 2)])
        esX.close()
        esXw.close()

        if "yag" in DBG:
            @block.sync
            def _(sync):
                sync.wait_ge(ccY.h, yag_done)
                stm.inc(sync.dma_start(out=dbg_t["yag"], in_=yag_out[:, :]), 16)

        esC = ExitStack()
        ygb = [sb(f"ygb{i}", [128, TOPK, D], BF16, esC) for i in range(2)]
        h_rel = [sb(f"h_rel{i}", [128, D], BF16, esC) for i in range(2)]
        dw_sb = [sb(f"dw{i}", [128, TOPK, 128], BF16, esC) for i in range(2)]
        o_scr = sb("o_scr", [128, D], F32, esC)
        o_scr2 = sb("o_scr2", [128, D], F32, esC)
        o_cols = sb("o_cols", [128, 8], F32, esC)
        out_st = [sb(f"out_st{i}", [128, D], F32, esC) for i in range(2)]
        ps_m = [esC.enter_context(nc.psum_tensor(f"ps_m{i}", [128, 4, 512], F32))
                for i in range(2)]

        @block.gpsimd
        def _(gpsimd):
            gpsimd.wait_ge(tS.h, exp_gu_last)
            gpsimd.wait_ge(tS.h, ymm[(EPC - 1, 2)])
            sdv = sdT.rearrange("(kc p) d -> p kc d", p=128)
            for i in range(4):
                dB.inc(gpsimd.dma_start(out=sdT_sb[:, 4 * i:4 * i + 4, :],
                                        in_=sdv[:, 4 * i:4 * i + 4, :]), 16)
        sdT_ld = dB.n

        ygg = [0] * 8
        dwb = [0] * 8
        hrl = [0] * 8
        cmm = [0] * 8
        oev = [0] * 8
        ost = [0] * 8
        for tt in range(8):
            @block.gpsimd
            def _(gpsimd, tt=tt):
                if tt == 0:
                    gpsimd.wait_ge(ccY.h, yag_done)
                    gpsimd.wait_ge(vS.h, tokp_done)
                if tt >= 2:
                    gpsimd.wait_ge(tS.h, cmm[tt - 2])
                for kk in range(TOPK):
                    gsl[tt % 2].inc(gpsimd.indirect_dma_start(
                        out=ygb[tt % 2][:, kk, :], out_offset=None,
                        in_=yag_out[0:NCORE * EPC * CCAP, :],
                        in_offset=IndirectOffsetOnAxis(
                            sidx_tok[:, tt, kk:kk + 1], 0)), 16)
            ygg[tt] = gsl[tt % 2].n

            @block.vector
            def _(vector, tt=tt):
                vector.wait_ge(vS.h, vS.n)
                if tt == 0:
                    vector.wait_ge(vS.h, tokp_done)
                    vector.wait_ge(ccY.h, yag_done)
                if tt >= 2:
                    vector.wait_ge(tS.h, cmm[tt - 2])
                for kk in range(TOPK):
                    vS.inc(vector.tensor_scalar(
                        dw_sb[tt % 2][:, kk, :], ident_bf[:, :],
                        weff_sb[:, tt, kk:kk + 1], None, ALU.mult))
            dwb[tt] = vS.n

            @block.scalar
            def _(scalar, tt=tt):
                if tt == 0:
                    scalar.wait_ge(ccY.h, yag_done)
                if tt >= 2:
                    scalar.wait_ge(vS.h, oev[tt - 2])
                ssl[tt % 2].inc(scalar.dma_start(
                    out=h_rel[tt % 2][:, :],
                    in_=hag_in[tt * 128:(tt + 1) * 128, :]), 16)
            hrl[tt] = ssl[tt % 2].n

            @block.tensor
            def _(tensor, tt=tt):
                if tt == 0:
                    tensor.wait_ge(dB.h, sdT_ld)
                    tensor.wait_ge(vS.h, shared_gu_done)
                    tensor.wait_ge(vS.h, yev[(EPC - 1, 2)])
                if tt >= 2:
                    tensor.wait_ge(vS.h, oev[tt - 2])
                for jc in range(16):
                    for nn in range(4):
                        tS.inc(tensor.matmul(
                            ps_m[tt % 2][:, nn, :],
                            HshT[:, jc, tt * 128:(tt + 1) * 128],
                            sdT_sb[:, jc, nn * 512:(nn + 1) * 512],
                            start=(jc == 0), stop=False))
                tensor.wait_ge(gsl[tt % 2].h, ygg[tt])
                tensor.wait_ge(vS.h, dwb[tt])
                for kk in range(TOPK):
                    for nn in range(4):
                        tS.inc(tensor.matmul(
                            ps_m[tt % 2][:, nn, :], dw_sb[tt % 2][:, kk, :],
                            ygb[tt % 2][:, kk, nn * 512:(nn + 1) * 512],
                            start=False, stop=(kk == TOPK - 1)))
            cmm[tt] = tS.n

            @block.vector
            def _(vector, tt=tt):
                vector.wait_ge(vS.h, vS.n)
                V = vector
                V.wait_ge(ssl[tt % 2].h, hrl[tt])
                if tt >= 2:
                    V.wait_ge(vS.h, oev[tt - 2])
                V.drain()
                V.tensor_copy(o_scr[:, :], h_rel[tt % 2][:, :])
                V.wait_ge(tS.h, cmm[tt])
                V.drain()
                vS.inc(V.tensor_add(o_scr2[:, :],
                                    ps_m[tt % 2].rearrange("p a b -> p (a b)"),
                                    o_scr[:, :]))
            resid2_done = vS.n

            @block.scalar
            def _(scalar, tt=tt, resid2_done=resid2_done):
                scalar.wait_ge(vS.h, resid2_done)
                scalar.wait_ge(sS.h, sS.n)
                sS.inc(scalar.activation(o_scr[:, :], o_scr2[:, :], AF.Copy,
                                         accum_out=o_cols[:, 0:1]))
                scalar.wait_ge(sS.h, sS.n)
                sS.inc(scalar.activation(o_scr[:, :], o_scr2[:, :], AF.Square,
                                         accum_out=o_cols[:, 2:3]))
            stats2 = sS.n

            @block.vector
            def _(vector, tt=tt, stats2=stats2):
                vector.wait_ge(vS.h, vS.n)
                V = vector
                V.wait_ge(sS.h, stats2)
                V.drain()
                V.tensor_scalar(o_cols[:, 1:2], o_cols[:, 0:1], 1.0 / D, None,
                                ALU.mult)
                V.drain()
                V.tensor_tensor(o_cols[:, 3:4], o_cols[:, 1:2], o_cols[:, 1:2],
                                ALU.mult)
                V.drain()
                V.tensor_scalar(o_cols[:, 4:5], o_cols[:, 2:3], 1.0 / D, None,
                                ALU.mult)
                V.drain()
                V.tensor_sub(o_cols[:, 5:6], o_cols[:, 4:5], o_cols[:, 3:4])
                V.drain()
                vS.inc(V.tensor_scalar(o_cols[:, 5:6], o_cols[:, 5:6], 1.0,
                                       LN_EPS, ALU.mult, ALU.add))
            var2 = vS.n

            @block.scalar
            def _(scalar, var2=var2):
                scalar.wait_ge(vS.h, var2)
                sS.inc(scalar.activation(o_cols[:, 6:7], o_cols[:, 5:6],
                                         AF.Sqrt))
            std2 = sS.n

            @block.vector
            def _(vector, tt=tt, std2=std2):
                vector.wait_ge(vS.h, vS.n)
                V = vector
                V.wait_ge(sS.h, std2)
                V.reciprocal(o_cols[:, 7:8], o_cols[:, 6:7])
                V.drain()
                if tt >= 2:
                    V.wait_ge(tsl[tt % 2].h, ost[tt - 2])
                vS.inc(V.tensor_scalar(out_st[tt % 2][:, :], o_scr2[:, :],
                                       o_cols[:, 1:2], o_cols[:, 7:8],
                                       ALU.subtract, ALU.mult))
            oev[tt] = vS.n

            if "moe" in DBG:
                @block.sync
                def _(sync, tt=tt):
                    sync.wait_ge(vS.h, oev[tt])
                    stm.inc(sync.dma_start(
                        out=dbg_t["moe"][tt * 128:(tt + 1) * 128, :],
                        in_=o_scr2[:, :]), 16)

            @block.sync
            def _(sync, tt=tt):
                sync.wait_ge(vS.h, oev[tt])
                tsl[tt % 2].inc(sync.dma_start(
                    out=out[tt * 128:(tt + 1) * 128, :],
                    in_=out_st[tt % 2][:, :]), 16)
            ost[tt] = tsl[tt % 2].n
        final_st = (tsl[0].n, tsl[1].n, stm.n)

        @block.sync
        def _(sync):
            sync.wait_ge(tsl[0].h, final_st[0])
            sync.wait_ge(tsl[1].h, final_st[1])
            sync.wait_ge(stm.h, final_st[2])

        @block.gpsimd
        def _(gpsimd):
            gpsimd.wait_ge(tsl[0].h, final_st[0])
            gpsimd.wait_ge(tsl[1].h, final_st[1])
            gpsimd.wait_ge(stm.h, final_st[2])
        esC.close()

    return nc


def _host_inputs(inputs):
    x = np.ascontiguousarray(np.asarray(inputs["L0_attn_x"], np.float32))
    w_q = np.asarray(inputs["L0_attn_w_q"], np.float32)
    w_kv = np.asarray(inputs["L0_attn_w_kv"], np.float32)
    w_dense = np.asarray(inputs["L0_attn_w_dense"], np.float32)
    router_w = np.asarray(inputs["L0_ffn_router_w"], np.float32)
    e_bias = np.asarray(inputs["L0_ffn_e_bias"], np.float32)
    W_gate = np.asarray(inputs["L0_ffn_W_gate"], np.float32)
    W_up = np.asarray(inputs["L0_ffn_W_up"], np.float32)
    W_down = np.asarray(inputs["L0_ffn_W_down"], np.float32)
    sg = np.asarray(inputs["L0_ffn_shared_gate"], np.float32)
    su = np.asarray(inputs["L0_ffn_shared_up"], np.float32)
    sd = np.asarray(inputs["L0_ffn_shared_down"], np.float32)

    ident = np.eye(128, dtype=np.float32)
    lstrict = (np.arange(128)[:, None] < np.arange(128)[None, :]).astype(BF)
    masks = np.zeros((4, 128, 512), dtype=BF)
    for m in range(4):
        masks[m] = (np.arange(128)[:, None] <=
                    (np.arange(512)[None, :] - 128 * m)).astype(BF)
    iota384 = np.arange(384, dtype=np.float32)[None, :]
    grpidx = (np.arange(E) // EPG).astype(np.float32)[None, :]
    hilo = np.zeros((128, 128), dtype=BF)
    for j in range(64):
        t = 128 * j + np.arange(128)
        hilo[:, 2 * j] = (t // 64).astype(BF)
        hilo[:, 2 * j + 1] = (t % 64).astype(BF)
    sgT = np.ascontiguousarray(sg.T.reshape(D, 16, 128).transpose(1, 0, 2))
    suT = np.ascontiguousarray(su.T.reshape(D, 16, 128).transpose(1, 0, 2))
    sdT = np.ascontiguousarray(sd.T)

    in_maps = []
    for c in range(NCORE):
        b, half = c // 2, c % 2
        wq_c = w_q[:, half * 1024:(half + 1) * 1024]
        wqh = np.ascontiguousarray(wq_c.reshape(D, NH, KVC).transpose(1, 0, 2))
        cmask8 = (np.arange(NCORE) < c).astype(np.float32)[:, None]
        selhl = np.zeros((128, 8), np.float32)
        for p in range(128):
            j = p % 64 - 8 * c
            if 0 <= j < 8:
                selhl[p, j] = 1.0
        ee = np.arange(E)
        sbase = ((ee % EPC) * (NCORE * CCAP) + (ee // EPC) * CCAP).astype(
            np.float32)[None, :]
        m = {
            "xT": np.ascontiguousarray(x[b].T),
            "x_own": np.ascontiguousarray(x[b, half * 1024:(half + 1) * 1024]),
            "wqh": wqh,
            "wk": np.ascontiguousarray(w_kv[:, 256 * half:256 * half + 256]),
            "wv": np.ascontiguousarray(
                w_kv[:, 512 + 256 * half:512 + 256 * half + 256]),
            "wdense": np.ascontiguousarray(
                w_dense[half * 1024:(half + 1) * 1024, :]),
            "routerT": np.ascontiguousarray(router_w.T),
            "wg": np.ascontiguousarray(W_gate[EPC * c:EPC * c + EPC]),
            "wu": np.ascontiguousarray(W_up[EPC * c:EPC * c + EPC]),
            "wd": np.ascontiguousarray(W_down[EPC * c:EPC * c + EPC]),
            "sgT": sgT, "suT": suT, "sdT": sdT,
            "ident_bf_i": ident.astype(BF), "ident_f_i": ident,
            "lstrict_i": lstrict, "masks_i": masks,
            "iota384_i": iota384, "grpidx_i": grpidx,
            "ebias_i": e_bias[None, :].astype(np.float32),
            "hiloown_i": np.ascontiguousarray(hilo[:, 16 * c:16 * c + 16]),
            "cmask8_i": cmask8, "selhl_i": selhl, "sbase_i": sbase,
        }
        in_maps.append(m)
    return in_maps


def kernel(**inputs):
    in_maps = _host_inputs(inputs)
    nc = build()
    res = run_bass_kernel_spmd(nc, in_maps, core_ids=list(range(NCORE)))
    y = np.zeros((B, SEQ, D), np.float32)
    for c in range(NCORE):
        b, half = c // 2, c % 2
        y[b, half * 1024:(half + 1) * 1024] = res.results[c]["out"]
    if DBG:
        kernel.dbg = [{k: np.asarray(v) for k, v in r.items()
                       if k.startswith("dbg_")} for r in res.results]
    return y
